# version 16
# baseline (speedup 1.0000x reference)
import math
import numpy as np
from contextlib import ExitStack

import concourse.bass as bass
import concourse.mybir as mybir
from concourse.bass_utils import run_bass_kernel_spmd

F32 = mybir.dt.float32
BF16 = mybir.dt.bfloat16
I32 = mybir.dt.int32
AF = mybir.ActivationFunctionType
ALU = mybir.AluOpType
AX = mybir.AxisListType

D = 1024
L = 2048
B = 16
NCORES = 8
NSEQ = B // NCORES
H, KV, HD = 16, 4, 64
GROUPS = H // KV
NPAIR = H // 2
C, NN = 16, 256
CF = C * NN
DCH = D // 128
ROPE_BASE = 10000.0
EXP_MARGIN = 30.0
LN_EPS = 1e-5

TILE = 512
NTILE = L // TILE
NSUB = L // 128
QT = 1024
NQT = L // QT
NKC = L // 128

_DEBUG_TAPS = []
_PHASES = ("A", "AT", "F1", "F2")


def _alibi_slopes(n_heads):
    def pow2(n):
        start = 2.0 ** (-2.0 ** (-(math.log2(n) - 3)))
        return [start * start ** i for i in range(n)]
    if math.log2(n_heads).is_integer():
        return pow2(n_heads)
    c = 2 ** math.floor(math.log2(n_heads))
    return pow2(c) + _alibi_slopes(2 * c)[0::2][: n_heads - c]


COMPUTE_ENGINES = ("tensor", "vector", "scalar", "gpsimd")
ALL_ENGINES = ("tensor", "vector", "scalar", "gpsimd", "sync")
SEM_CAP = 50000


def _key(x):
    t = getattr(x, "tensor", None)
    if t is not None:
        return t.name
    return x.name


class Prog:
    def __init__(self, nc, ctx):
        self.nc = nc
        self.ctx = ctx
        self.ops = {e: [] for e in ALL_ENGINES}
        self.sem_objs = {}
        self.sem_count = {}
        self.sem_exact = {}
        for e in COMPUTE_ENGINES:
            sid = "e:" + e
            self.sem_objs[sid] = ctx.enter_context(nc.semaphore("sem_" + e))
            self.sem_count[sid] = 0
            self.sem_exact[sid] = True
        self.dma_gen = {}
        self.track = {}
        self.waited = {e: {} for e in ALL_ENGINES}
        self.handed_out = {}
        self.n_ops = 0

    def _resolve(self, sem_id, val):
        if val is None or not self.sem_exact[sem_id]:
            return self.sem_count[sem_id]
        return val

    def _edges_for(self, reads, writes):
        need = {}
        for x in reads:
            rec = self.track.get(_key(x))
            if rec and rec["w"]:
                s, v = rec["w"]
                need[s] = max(need.get(s, 0), self._resolve(s, v))
        for x in writes:
            rec = self.track.get(_key(x))
            if rec:
                if rec["w"]:
                    s, v = rec["w"]
                    need[s] = max(need.get(s, 0), self._resolve(s, v))
                for s, v in rec["r"].items():
                    need[s] = max(need.get(s, 0), self._resolve(s, v))
        return need

    def _mk_waits(self, engine, need, skip_own_pe):
        waits = []
        for s, v in sorted(need.items()):
            if skip_own_pe and s == "e:tensor":
                continue
            if v <= 0:
                continue
            self.handed_out[s] = max(self.handed_out.get(s, 0), v)
            if self.waited[engine].get(s, 0) >= v:
                continue
            self.waited[engine][s] = v
            waits.append((self.sem_objs[s], v))
        return waits

    def _record(self, reads, writes, sem_id, val):
        for x in reads:
            rec = self.track.setdefault(_key(x), {"w": None, "r": {}})
            old = rec["r"].get(sem_id, 0)
            if val is None or old is None:
                rec["r"][sem_id] = None
            else:
                rec["r"][sem_id] = max(old, val)
        for x in writes:
            self.track[_key(x)] = {"w": (sem_id, val), "r": {}}

    def op(self, engine, fn, reads=(), writes=(), tag=None):
        assert engine in COMPUTE_ENGINES, engine
        need = self._edges_for(reads, writes)
        waits = self._mk_waits(engine, need, skip_own_pe=(engine == "tensor"))
        sid = "e:" + engine
        self.sem_count[sid] += 1
        val = self.sem_count[sid]
        self._record(reads, writes, sid, val)
        self.ops[engine].append((waits, fn, self.sem_objs[sid], 1, tag))
        self.n_ops += 1

    def _dma_sid(self, queue, stream):
        gkey = (queue, stream)
        gen = self.dma_gen.get(gkey, 0)
        sid = f"d:{queue}:{stream}:{gen}"
        if sid not in self.sem_objs:
            self.sem_objs[sid] = self.ctx.enter_context(
                self.nc.semaphore(f"dma_{queue}_{stream}_{gen}"))
            self.sem_count[sid] = 0
            self.sem_exact[sid] = False
        if self.sem_count[sid] >= SEM_CAP:
            self.dma_gen[gkey] = gen + 1
            return self._dma_sid(queue, stream)
        return sid

    def dma(self, queue, out, in_, reads=(), writes=(), stream="w", tag=None, **kw):
        assert queue in ("sync", "gpsimd", "scalar"), queue
        need = self._edges_for(reads, writes)
        waits = self._mk_waits(queue, need, skip_own_pe=False)
        sid = self._dma_sid(queue, stream)
        ho = self.handed_out.get(sid, 0)
        if ho > self.waited[queue].get(sid, 0):
            self.waited[queue][sid] = ho
            waits.append((self.sem_objs[sid], ho))
        self.sem_count[sid] += 16
        self._record(list(reads), list(writes), sid, None)
        fn = lambda eng, out=out, in_=in_, kw=kw: eng.dma_start(out=out, in_=in_, **kw)
        self.ops[queue].append((waits, fn, self.sem_objs[sid], 16, tag))
        self.n_ops += 1

    def wait_all(self, engine):
        need = {s: c for s, c in self.sem_count.items() if c > 0}
        waits = self._mk_waits(engine, need, skip_own_pe=False)
        self.ops[engine].append((waits, None, None, 0, "barrier"))

    def barrier(self):
        for e in ALL_ENGINES:
            self.wait_all(e)

    def emit(self):
        nc = self.nc
        prog = self
        with nc.Block() as block:
            for engine in ALL_ENGINES:
                if not prog.ops[engine]:
                    continue

                def section(eng, engine=engine):
                    for waits, fn, sem, inc, tag in prog.ops[engine]:
                        for h, v in waits:
                            eng.wait_ge(h, v)
                        if fn is not None:
                            inst = fn(eng)
                            if sem is not None:
                                inst.then_inc(sem, inc)

                getattr(block, engine)(section)


def build_program(phases=_PHASES, dump_scratch=False):
    nc = bass.Bass("TRN2", target_bir_lowering=False)
    ctx = ExitStack()
    P = Prog(nc, ctx)

    def sb(name, shape, dt=F32):
        return ctx.enter_context(nc.sbuf_tensor(name, shape, dt))

    def psum(name, shape, dt=F32):
        return ctx.enter_context(nc.psum_tensor(name, shape, dt))

    def V(e, fn, reads=(), writes=(), tag=None):
        P.op(e, fn, reads=reads, writes=writes, tag=tag)

    def dramio(name, shape, dt, kind):
        return nc.dram_tensor(name, shape, dt, kind=kind).ap()

    x_in = dramio("x", [NSEQ * L, D], F32, "ExternalInput")
    wq_in = dramio("Wq", [D, H * HD], F32, "ExternalInput")
    wk_in = dramio("Wk", [D, KV * HD], F32, "ExternalInput")
    wv_in = dramio("Wv", [D, KV * HD], F32, "ExternalInput")
    wo_in = dramio("Wo", [H * HD, D], F32, "ExternalInput")
    ce_in = dramio("col_emb", [C, D], F32, "ExternalInput")
    gw1_in = dramio("gate_w1", [D, D // 2], F32, "ExternalInput")
    gb1_in = dramio("gate_b1", [D // 2], F32, "ExternalInput")
    gw2_in = dramio("gate_w2", [D // 2, C], F32, "ExternalInput")
    gb2_in = dramio("gate_b2", [C], F32, "ExternalInput")
    wcols_in = dramio("W_cols", [C, D, NN], F32, "ExternalInput")
    bcols_in = dramio("b_cols", [C, NN], F32, "ExternalInput")
    pw_in = dramio("proj_w", [CF, D], F32, "ExternalInput")
    pb_in = dramio("proj_b", [D], F32, "ExternalInput")
    lng_in = dramio("ln_g", [D], F32, "ExternalInput")
    lnb_in = dramio("ln_b", [D], F32, "ExternalInput")
    y_out = dramio("y", [NSEQ * L, D], F32, "ExternalOutput")

    def scratch(name, shape, dt, produced_by, consumed_by):
        prod = produced_by in phases
        cons = any(p in phases for p in consumed_by)
        if prod and dump_scratch:
            return dramio(name, shape, dt, "ExternalOutput")
        if not prod and cons:
            return dramio(name, shape, dt, "ExternalInput")
        if prod or cons:
            return nc.dram_tensor(name, shape, dt).ap()
        return None

    xnb_sc = scratch("xnb_sc", [D, NSEQ * L], BF16, "A", ("AT", "F1", "F2"))
    wqp_sc = nc.dram_tensor("wqp_sc", [NPAIR * 128, 1024], BF16)
    wop_sc = nc.dram_tensor("wop_sc", [DCH * 128, 1024], BF16)
    pw_sc = nc.dram_tensor("pw_sc", [CF, D], BF16)
    rows_sc = scratch("rows_sc", [4, NSEQ * L], F32, "A", ("F1", "F2"))
    attn_sc = scratch("attn_sc", [NSEQ * L, D], F32, "AT", ("F2",))
    cols_sc = scratch("cols_sc", [CF, NSEQ * L], BF16, "F1", ("F2",))

    taps = {}

    def tap(name, src_ap, shape, dt=F32, reads=()):
        if name not in _DEBUG_TAPS:
            return
        out = dramio("dbg_" + name, shape, dt, "ExternalOutput")
        P.dma("sync", out[:, :], src_ap, reads=reads, stream="dbg")

    pb = [psum(f"pb{i}", [128, 512]) for i in range(8)]

    ident = sb("ident", [128, 128])
    V("gpsimd", lambda e: e.memset(ident[:, :], 0.0), writes=[ident])
    V("gpsimd", lambda e: e.affine_select(out=ident[:, :], in_=ident[:, :],
                                          compare_op=ALU.not_equal, fill=1.0, base=0,
                                          pattern=[[-1, 128]], channel_multiplier=1),
      reads=[ident], writes=[ident])
    ones_row_bf = sb("ones_row_bf", [1, 128], BF16)
    V("gpsimd", lambda e: e.memset(ones_row_bf[:, :], 1.0), writes=[ones_row_bf])
    onescol_bf = sb("onescol_bf", [128, 1], BF16)
    V("gpsimd", lambda e: e.memset(onescol_bf[:, :], 1.0), writes=[onescol_bf])
    selc = sb("selc", [16, C * 128], BF16)
    V("gpsimd", lambda e: e.memset(selc[:, :], 1.0), writes=[selc])
    for c in range(C):
        V("gpsimd", lambda e, c=c: e.affine_select(
            out=selc[:, c * 128:(c + 1) * 128], in_=selc[:, c * 128:(c + 1) * 128],
            compare_op=ALU.is_equal, fill=0.0, base=-c, pattern=[[0, 128]],
            channel_multiplier=1), reads=[selc], writes=[selc])
    psel_a = sb("psel_a", [1, 128], BF16)
    psel_b = sb("psel_b", [1, 128], BF16)
    iotap = sb("iotap", [1, 128])
    V("gpsimd", lambda e: e.iota(iotap[:, :], [[1, 128]], channel_multiplier=0,
                                 allow_small_or_imprecise_dtypes=True), writes=[iotap])
    V("vector", lambda e: e.tensor_single_scalar(psel_a[:, :], iotap[:, :], 64.0, op=ALU.is_lt),
      reads=[iotap], writes=[psel_a])
    V("vector", lambda e: e.tensor_single_scalar(psel_b[:, :], iotap[:, :], 64.0, op=ALU.is_ge),
      reads=[iotap], writes=[psel_b])

    b1_fm = sb("b1_fm", [128, 4])
    pb_fm = sb("pb_fm", [128, 8])
    lng_fm = sb("lng_fm", [128, 8])
    lnb_fm = sb("lnb_fm", [128, 8])
    bcols_fm = sb("bcols_fm", [128, 32])
    gb2_t = sb("gb2_t", [C, 1])
    for dst, src, n in [(b1_fm, gb1_in, 4), (pb_fm, pb_in, 8),
                        (lng_fm, lng_in, 8), (lnb_fm, lnb_in, 8)]:
        P.dma("sync", dst[:, 0:n], src.rearrange("(n p) -> p n", p=128),
              writes=[dst], stream="smallw", allow_slow_non_contiguous=True)
    P.dma("sync", bcols_fm[:, :], bcols_in.rearrange("c (m p) -> p (c m)", p=128),
          writes=[bcols_fm], stream="smallw", allow_slow_non_contiguous=True)
    P.dma("sync", gb2_t[:, :], gb2_in.rearrange("(c u) -> c u", u=1),
          writes=[gb2_t], stream="smallw", allow_slow_non_contiguous=True)
    gb2h_t = sb("gb2h_t", [C, 1])
    V("vector", lambda e: e.tensor_scalar_mul(gb2h_t[:, :], gb2_t[:, :], 0.5),
      reads=[gb2_t], writes=[gb2h_t])

    bias_hk = sb("bias_hk", [128, H * NKC])
    negl2 = sb("negl2", [1, 1])
    V("gpsimd", lambda e: e.memset(negl2[:, :], -float(np.log(2.0) / 2.0)), writes=[negl2])
    eps_t = sb("eps_t", [1, 1])
    V("gpsimd", lambda e: e.memset(eps_t[:, :], float(LN_EPS)), writes=[eps_t])
    bias_tmp_ctx = ExitStack()
    slope_rep = bias_tmp_ctx.enter_context(nc.sbuf_tensor("slope_rep", [1, H * NKC], F32))
    c_row = bias_tmp_ctx.enter_context(nc.sbuf_tensor("c_row", [1, H * NKC], F32))
    kc_row = bias_tmp_ctx.enter_context(nc.sbuf_tensor("kc_row", [1, H * NKC], F32))
    iota_row = bias_tmp_ctx.enter_context(nc.sbuf_tensor("iota_row", [1, 128], F32))
    ones_row_f = bias_tmp_ctx.enter_context(nc.sbuf_tensor("ones_row_f", [1, 128], F32))
    V("gpsimd", lambda e: e.iota(iota_row[:, :], [[1, 128]], channel_multiplier=0,
                                 allow_small_or_imprecise_dtypes=True), writes=[iota_row])
    V("gpsimd", lambda e: e.memset(ones_row_f[:, :], 1.0), writes=[ones_row_f])
    V("gpsimd", lambda e: e.iota(slope_rep[:, :], [[1, H], [0, NKC]], channel_multiplier=0,
                                 allow_small_or_imprecise_dtypes=True), writes=[slope_rep],
      tag="sl-iota")
    V("scalar", lambda e: e.activation(slope_rep[:, :], slope_rep[:, :], AF.Exp,
                                       scale=-float(np.log(2.0) / 2.0), bias=negl2[:, :]),
      reads=[slope_rep, negl2], writes=[slope_rep])
    V("gpsimd", lambda e: e.iota(kc_row[:, :], [[0, H], [128, NKC]], channel_multiplier=0,
                                 allow_small_or_imprecise_dtypes=True), writes=[kc_row])
    V("vector", lambda e: e.tensor_scalar_add(c_row[:, :], kc_row[:, :], -2047.0),
      reads=[kc_row], writes=[c_row])
    V("vector", lambda e: e.tensor_tensor(c_row[:, :], c_row[:, :], slope_rep[:, :], ALU.mult),
      reads=[c_row, slope_rep], writes=[c_row])
    V("vector", lambda e: e.tensor_scalar_add(c_row[:, :], c_row[:, :], -float(EXP_MARGIN)),
      reads=[c_row], writes=[c_row])
    V("tensor", lambda e: e.matmul(pb[0][:, 0:H * NKC], iota_row[:, :], slope_rep[:, :],
                                   start=True, stop=False),
      reads=[iota_row, slope_rep], writes=[pb[0]])
    V("tensor", lambda e: e.matmul(pb[0][:, 0:H * NKC], ones_row_f[:, :], c_row[:, :],
                                   start=False, stop=True),
      reads=[ones_row_f, c_row], writes=[pb[0]])
    V("vector", lambda e: e.tensor_copy(bias_hk[:, :], pb[0][:, 0:H * NKC]),
      reads=[pb[0]], writes=[bias_hk])
    P.barrier()
    bias_tmp_ctx.close()

    ce_fm = sb("ce_fm", [128, DCH * C], BF16)
    stage_rows = sb("stage_rows", [1, 4 * TILE])
    rhi = sb("rhi", [1, 512], BF16)
    rlo = sb("rlo", [1, 512], BF16)
    rtmp = sb("rtmp", [1, 512])
    invf = sb("invf", [64, 1])
    ropes_ctx = ExitStack()
    cos_d = ropes_ctx.enter_context(nc.sbuf_tensor("cos_d", [64, L], F32))
    sin_s = ropes_ctx.enter_context(nc.sbuf_tensor("sin_s", [64, L], F32))
    tmp_ctx = ExitStack()
    angbuf = tmp_ctx.enter_context(nc.sbuf_tensor("angbuf", [64, L], F32))
    rrbuf = tmp_ctx.enter_context(nc.sbuf_tensor("rrbuf", [64, L], F32))
    kfbuf = tmp_ctx.enter_context(nc.sbuf_tensor("kfbuf", [64, L], F32))
    kibuf = tmp_ctx.enter_context(nc.sbuf_tensor("kibuf", [64, L], I32))
    for p0 in (0, 32):
        V("gpsimd", lambda e, p0=p0: e.iota(invf[p0:p0 + 32, :], [[0, 1]],
                                            channel_multiplier=1,
                                            allow_small_or_imprecise_dtypes=True),
          writes=[invf])
    V("scalar", lambda e: e.activation(invf[:, :], invf[:, :], AF.Exp,
                                       scale=-float(np.log(ROPE_BASE) / 32.0)),
      reads=[invf], writes=[invf])
    V("gpsimd", lambda e: e.iota(angbuf[:, :], [[1, L]], channel_multiplier=0,
                                 allow_small_or_imprecise_dtypes=True), writes=[angbuf])
    V("vector", lambda e: e.tensor_scalar_mul(angbuf[:, :], angbuf[:, :], invf[:, :]),
      reads=[angbuf, invf], writes=[angbuf])
    TWO_PI = 2 * np.pi
    HIPI = float(np.float32(6.28125))
    LOPI = float(TWO_PI - 6.28125)

    def range_reduce(dst, shift):
        V("vector", lambda e: e.tensor_scalar(rrbuf[:, :], angbuf[:, :], float(shift),
                                              float(1.0 / TWO_PI), op0=ALU.add, op1=ALU.mult),
          reads=[angbuf], writes=[rrbuf])
        V("vector", lambda e: e.tensor_copy(kibuf[:, :], rrbuf[:, :]), reads=[rrbuf], writes=[kibuf])
        V("vector", lambda e: e.tensor_copy(kfbuf[:, :], kibuf[:, :]), reads=[kibuf], writes=[kfbuf])
        V("vector", lambda e: e.tensor_tensor(rrbuf[:, :], rrbuf[:, :], kfbuf[:, :], ALU.subtract),
          reads=[rrbuf, kfbuf], writes=[rrbuf])
        V("vector", lambda e: e.tensor_single_scalar(rrbuf[:, :], rrbuf[:, :], 0.5, op=ALU.is_ge),
          reads=[rrbuf], writes=[rrbuf])
        V("vector", lambda e: e.tensor_tensor(kfbuf[:, :], kfbuf[:, :], rrbuf[:, :], ALU.add),
          reads=[kfbuf, rrbuf], writes=[kfbuf])
        V("vector", lambda e: e.tensor_scalar(rrbuf[:, :], angbuf[:, :], float(shift), 1.0,
                                              op0=ALU.add, op1=ALU.bypass),
          reads=[angbuf], writes=[rrbuf])
        V("vector", lambda e: e.scalar_tensor_tensor(rrbuf[:, :], kfbuf[:, :], -HIPI, rrbuf[:, :],
                                                     op0=ALU.mult, op1=ALU.add),
          reads=[kfbuf, rrbuf], writes=[rrbuf])
        V("vector", lambda e: e.scalar_tensor_tensor(dst[:, :], kfbuf[:, :], -LOPI, rrbuf[:, :],
                                                     op0=ALU.mult, op1=ALU.add),
          reads=[kfbuf, rrbuf], writes=[dst])
        V("scalar", lambda e: e.activation(dst[:, :], dst[:, :], AF.Sin),
          reads=[dst], writes=[dst])

    range_reduce(sin_s, 0.0)
    range_reduce(cos_d, np.pi / 2)
    V("vector", lambda e: e.tensor_scalar_mul(sin_s[0:32, :], sin_s[0:32, :], -1.0),
      reads=[sin_s], writes=[sin_s])

    ce_tm = tmp_ctx.enter_context(nc.sbuf_tensor("ce_tm", [C, D], F32))
    ce_sq = tmp_ctx.enter_context(nc.sbuf_tensor("ce_sq", [C, D], F32))
    ce_nrm = tmp_ctx.enter_context(nc.sbuf_tensor("ce_nrm", [C, 1], F32))
    ce_nf = tmp_ctx.enter_context(nc.sbuf_tensor("ce_nf", [C, D], F32))
    P.dma("sync", ce_tm[:, :], ce_in[:, :], writes=[ce_tm], stream="smallw")
    V("scalar", lambda e: e.activation(ce_sq[:, :], ce_tm[:, :], AF.Square),
      reads=[ce_tm], writes=[ce_sq])
    V("vector", lambda e: e.reduce_sum(ce_nrm[:, :], ce_sq[:, :], axis=AX.X),
      reads=[ce_sq], writes=[ce_nrm])
    V("scalar", lambda e: e.activation(ce_nrm[:, :], ce_nrm[:, :], AF.Sqrt),
      reads=[ce_nrm], writes=[ce_nrm])
    V("vector", lambda e: e.tensor_single_scalar(ce_nrm[:, :], ce_nrm[:, :], 1e-12, op=ALU.max),
      reads=[ce_nrm], writes=[ce_nrm])
    V("vector", lambda e: e.reciprocal(ce_nrm[:, :], ce_nrm[:, :]),
      reads=[ce_nrm], writes=[ce_nrm])
    V("vector", lambda e: e.tensor_scalar_mul(ce_nf[:, :], ce_tm[:, :], ce_nrm[:, :]),
      reads=[ce_tm, ce_nrm], writes=[ce_nf])
    for j in range(DCH):
        pbt = pb[1 + j // 4]
        V("tensor", lambda e, j=j, pbt=pbt: e.transpose(
            pbt[:, (j % 4) * 128: (j % 4) * 128 + C],
            ce_nf[:, j * 128:(j + 1) * 128], ident[0:C, 0:C]),
          reads=[ce_nf, ident], writes=[pbt])
    for j in range(DCH):
        pbt = pb[1 + j // 4]
        V("vector", lambda e, j=j, pbt=pbt: e.tensor_copy(
            ce_fm[:, j * C:(j + 1) * C], pbt[:, (j % 4) * 128: (j % 4) * 128 + C]),
          reads=[pbt], writes=[ce_fm])

    P.barrier()
    tmp_ctx.close()

    def row_hilo(src_ap, n, src_keys):
        V("vector", lambda e: e.tensor_copy(rhi[:, 0:n], src_ap), reads=src_keys, writes=[rhi])
        V("vector", lambda e: e.tensor_copy(rtmp[:, 0:n], rhi[:, 0:n]), reads=[rhi], writes=[rtmp])
        V("vector", lambda e: e.tensor_tensor(rlo[:, 0:n], src_ap, rtmp[:, 0:n], ALU.subtract),
          reads=list(src_keys) + [rtmp], writes=[rlo])
        return rhi[:, 0:n], rlo[:, 0:n]

    PAIRS = [(0, 4), (1, 5), (2, 6), (3, 7), (8, 12), (9, 13), (10, 14), (11, 15)]
    VROW = KV * 72
    QW = 512

    pw_sc_ap = pw_sc.ap()
    if ("AT" in phases) or ("F2" in phases):
        for half in range(4):
            P.dma("gpsimd", pw_sc_ap[half * 1024:(half + 1) * 1024, :],
                  pw_in[half * 1024:(half + 1) * 1024, :],
                  writes=[pw_sc_ap], stream="wgt")
    if "AT" in phases:
        wqp_ap = wqp_sc.ap()
        wop_ap = wop_sc.ap()
        for pi in range(NPAIR):
            for u, head in enumerate(PAIRS[pi]):
                P.dma("gpsimd",
                      bass.AP(wqp_sc, pi * 128 * 1024 + u * 64,
                              [[1024, 128], [128, 8], [1, 64]]),
                      bass.AP(wq_in.tensor, head * 64,
                              [[1024, 128], [131072, 8], [1, 64]]),
                      writes=[wqp_ap], stream="wgt")
        for j in range(NPAIR):
            for u, head in enumerate(PAIRS[j]):
                P.dma("gpsimd",
                      bass.AP(wop_sc, (j * 128 + u * 64) * 1024,
                              [[1024, 64], [1, 1024]]),
                      bass.AP(wo_in.tensor, head * 64 * 1024,
                              [[1024, 64], [1, 1024]]),
                      writes=[wop_ap], stream="wgt")

    if "A" in phases:
        merged = ("AT" in phases) and ("F1" in phases)
        aw_ctx = ExitStack()
        wk_sb = aw_ctx.enter_context(nc.sbuf_tensor("wk_sb", [128, DCH * 256], BF16))
        wv_sb = aw_ctx.enter_context(nc.sbuf_tensor("wv_sb", [128, DCH * 256], BF16))
        for dc in range(DCH):
            P.dma("gpsimd", wk_sb[:, dc * 256:(dc + 1) * 256],
                  wk_in[dc * 128:(dc + 1) * 128, :], writes=[wk_sb], stream="wgt")
            P.dma("gpsimd", wv_sb[:, dc * 256:(dc + 1) * 256],
                  wv_in[dc * 128:(dc + 1) * 128, :], writes=[wv_sb], stream="wgt")
        if merged:
            wo_sb = aw_ctx.enter_context(nc.sbuf_tensor("wo_sb", [128, NPAIR * 1024], BF16))
            for pi in range(NPAIR):
                P.dma("sync", wo_sb[:, pi * 1024:(pi + 1) * 1024],
                      wop_ap[pi * 128:(pi + 1) * 128, :],
                      reads=[wop_ap], writes=[wo_sb], stream="wq")
            wc_sb = [aw_ctx.enter_context(nc.sbuf_tensor(f"wc_sb{dc}", [128, CF], BF16))
                     for dc in range(DCH)]
            for dc in range(DCH):
                for c in range(C):
                    P.dma("gpsimd", wc_sb[dc][:, c * NN:(c + 1) * NN],
                          wcols_in[c, dc * 128:(dc + 1) * 128, :], writes=[wc_sb[dc]],
                          stream="wgt")

        def emit_seq(s):
            kv_ctx = ExitStack()
            ktile = kv_ctx.enter_context(nc.sbuf_tensor(f"ktile{s}", [128, 2 * L], BF16))
            v4 = kv_ctx.enter_context(nc.sbuf_tensor(f"v4_{s}", [128, NKC * VROW], BF16))
            V("gpsimd", lambda e: e.memset(v4[:, :], 1.0), writes=[v4])

            actx = ExitStack()
            x_tm = actx.enter_context(nc.sbuf_tensor(f"x_tm{s}", [128, 4 * D], F32))
            st6 = actx.enter_context(nc.sbuf_tensor(f"st6{s}", [128, 12], F32))
            st2 = actx.enter_context(nc.sbuf_tensor(f"st2{s}", [128, 2], F32))
            stq = actx.enter_context(nc.sbuf_tensor(f"stq{s}", [128, 4], F32))
            sttmp = actx.enter_context(nc.sbuf_tensor(f"sttmp{s}", [128, 1], F32))
            x_fm = actx.enter_context(nc.sbuf_tensor(f"x_fm{s}", [128, DCH * TILE], F32))
            xn32_t = actx.enter_context(nc.sbuf_tensor(f"xn32_t{s}", [128, DCH * TILE], F32))
            xnb_t = actx.enter_context(nc.sbuf_tensor(f"xnb_t{s}", [128, DCH * TILE], BF16))
            kc1 = actx.enter_context(nc.sbuf_tensor(f"kc1{s}", [128, TILE], F32))
            kc2 = actx.enter_context(nc.sbuf_tensor(f"kc2{s}", [128, TILE], F32))

            for it in range(NTILE):
                t0 = it * TILE
                for u in range(4):
                    P.dma("sync", x_tm[:, u * D:(u + 1) * D],
                          x_in[s * L + t0 + u * 128: s * L + t0 + (u + 1) * 128, :],
                          writes=[x_tm], stream="x")
                for u in range(4):
                    sub = it * 4 + u
                    xv = x_tm[:, u * D:(u + 1) * D]
                    V("vector", lambda e, xv=xv: e.bn_stats(st6[:, 0:6], xv[:, 0:512]),
                      reads=[x_tm], writes=[st6])
                    V("vector", lambda e, xv=xv: e.bn_stats(st6[:, 6:12], xv[:, 512:1024]),
                      reads=[x_tm], writes=[st6])
                    V("vector", lambda e: e.bn_aggr(st2[:, :], st6[:, :]),
                      reads=[st6], writes=[st2])
                    V("vector", lambda e: e.tensor_copy(stq[:, 0:1], st2[:, 0:1]),
                      reads=[st2], writes=[stq])
                    V("vector", lambda e: e.tensor_scalar_add(sttmp[:, :], st2[:, 1:2],
                                                              float(LN_EPS)),
                      reads=[st2], writes=[sttmp])
                    V("scalar", lambda e: e.activation(stq[:, 2:3], sttmp[:, :], AF.Sqrt),
                      reads=[sttmp], writes=[stq])
                    V("vector", lambda e: e.reciprocal(stq[:, 1:2], stq[:, 2:3]),
                      reads=[stq], writes=[stq])
                    V("scalar", lambda e: e.activation(sttmp[:, :], st2[:, 1:2], AF.Sqrt,
                                                       scale=float(D)),
                      reads=[st2], writes=[sttmp])
                    V("vector", lambda e: e.tensor_tensor(sttmp[:, :], sttmp[:, :],
                                                          stq[:, 1:2], ALU.mult),
                      reads=[sttmp, stq], writes=[sttmp])
                    V("vector", lambda e: e.tensor_single_scalar(sttmp[:, :], sttmp[:, :],
                                                                 1e-12, op=ALU.max),
                      reads=[sttmp], writes=[sttmp])
                    V("vector", lambda e: e.reciprocal(stq[:, 3:4], sttmp[:, :]),
                      reads=[sttmp], writes=[stq])
                    for q in range(4):
                        V("tensor", lambda e, q=q: e.transpose(
                            pb[2][0:1, q * 128:(q + 1) * 128],
                            stq[:, q:q + 1], ident[:, :]),
                          reads=[stq, ident], writes=[pb[2]], tag="stT")
                    for q in range(4):
                        V("vector", lambda e, q=q, u=u: e.tensor_copy(
                            stage_rows[:, q * TILE + u * 128:q * TILE + (u + 1) * 128],
                            pb[2][0:1, q * 128:(q + 1) * 128]),
                          reads=[pb[2]], writes=[stage_rows])
                    for j in range(DCH):
                        pbt = pb[0] if j < 4 else pb[1]
                        V("tensor", lambda e, j=j, xv=xv, pbt=pbt: e.transpose(
                            pbt[:, (j % 4) * 128:(j % 4 + 1) * 128],
                            xv[:, j * 128:(j + 1) * 128], ident[:, :]),
                          reads=[x_tm, ident], writes=[pbt], tag="xT")
                    for j in range(DCH):
                        pbt = pb[0] if j < 4 else pb[1]
                        V("scalar", lambda e, j=j, u=u, pbt=pbt: e.copy(
                            x_fm[:, j * TILE + u * 128:j * TILE + (u + 1) * 128],
                            pbt[:, (j % 4) * 128:(j % 4 + 1) * 128]),
                          reads=[pbt], writes=[x_fm])
                for z in (0, 1):
                    hi, lo = row_hilo(stage_rows[:, z * TILE:(z + 1) * TILE], TILE,
                                      [stage_rows])
                    V("tensor", lambda e, z=z, hi=hi: e.matmul(
                        pb[4 + z][:, :], ones_row_bf[:, :], hi,
                        start=True, stop=False), reads=[ones_row_bf, rhi], writes=[pb[4 + z]])
                    V("tensor", lambda e, z=z, lo=lo: e.matmul(
                        pb[4 + z][:, :], ones_row_bf[:, :], lo,
                        start=False, stop=True), reads=[ones_row_bf, rlo], writes=[pb[4 + z]])
                for j in range(DCH):
                    xfj = x_fm[:, j * TILE:(j + 1) * TILE]
                    V("vector", lambda e, xfj=xfj, j=j: e.scalar_tensor_tensor(
                        xn32_t[:, j * TILE:(j + 1) * TILE], xfj, 1.0, pb[4][:, :],
                        op0=ALU.mult, op1=ALU.subtract),
                      reads=[x_fm, pb[4]], writes=[xn32_t])
                    V("vector", lambda e, j=j: e.tensor_tensor(
                        xn32_t[:, j * TILE:(j + 1) * TILE],
                        xn32_t[:, j * TILE:(j + 1) * TILE], pb[5][:, :], ALU.mult),
                      reads=[xn32_t, pb[5]], writes=[xn32_t])
                    V("vector", lambda e, j=j: e.tensor_copy(
                        xnb_t[:, j * TILE:(j + 1) * TILE], xn32_t[:, j * TILE:(j + 1) * TILE]),
                      reads=[xn32_t], writes=[xnb_t])
                for j in range(0, DCH, 4):
                    P.dma("scalar", xnb_sc.rearrange("(a p) t -> p a t", p=128)[
                              :, j:j + 4, s * L + t0:s * L + t0 + TILE],
                          xnb_t[:, j * TILE:(j + 4) * TILE].rearrange(
                              "p (a t) -> p a t", a=4),
                          reads=[xnb_t], writes=[xnb_sc], stream="xnw")
                for ri in range(4):
                    P.dma("scalar", rows_sc[ri:ri + 1, s * L + t0:s * L + t0 + TILE],
                          stage_rows[:, ri * TILE:(ri + 1) * TILE], reads=[stage_rows],
                          writes=[rows_sc], stream="xnw")
                for kp in range(2):
                    kpb = pb[6 + kp]
                    for j in range(DCH):
                        V("tensor", lambda e, j=j, kp=kp, kpb=kpb: e.matmul(
                            kpb[:, :],
                            wk_sb[:, j * 256 + kp * 128: j * 256 + (kp + 1) * 128],
                            xnb_t[:, j * TILE:(j + 1) * TILE],
                            start=(j == 0), stop=(j == DCH - 1)),
                          reads=[wk_sb, xnb_t], writes=[kpb], tag="kmm")
                    for h0 in (0, 64):
                        V("vector", lambda e, kpb=kpb, t0=t0, h0=h0: e.tensor_tensor(
                            kc1[h0:h0 + 64, :], kpb[h0:h0 + 64, :],
                            cos_d[:, t0:t0 + TILE], ALU.mult),
                          reads=[kpb, cos_d], writes=[kc1])
                    for p0, rp0 in ((0, 32), (32, 0), (64, 96), (96, 64)):
                        V("vector", lambda e, p0=p0, rp0=rp0, t0=t0, kpb=kpb: e.tensor_tensor(
                            kc2[p0:p0 + 32, :], kpb[rp0:rp0 + 32, :],
                            sin_s[p0 % 64:p0 % 64 + 32, t0:t0 + TILE], ALU.mult),
                          reads=[kpb, sin_s], writes=[kc2])
                    V("vector", lambda e, kp=kp, t0=t0: e.tensor_tensor(
                        ktile[:, kp * L + t0:kp * L + t0 + TILE], kc1[:, :], kc2[:, :],
                        ALU.add),
                      reads=[kc1, kc2], writes=[ktile])
                for u in range(4):
                    sub = it * 4 + u
                    for j in range(DCH):
                        V("tensor", lambda e, j=j, u=u: e.matmul(
                            pb[3][:, 0:256],
                            xnb_t[:, j * TILE + u * 128: j * TILE + (u + 1) * 128],
                            wv_sb[:, j * 256:(j + 1) * 256],
                            start=(j == 0), stop=(j == DCH - 1)),
                          reads=[xnb_t, wv_sb], writes=[pb[3]], tag="vmm")
                    for kv in range(KV):
                        V("vector", lambda e, kv=kv, sub=sub: e.tensor_copy(
                            v4[:, sub * VROW + kv * 72: sub * VROW + kv * 72 + 64],
                            pb[3][:, kv * 64:(kv + 1) * 64]),
                          reads=[pb[3]], writes=[v4])
            P.barrier()
            actx.close()

            if merged:
                mctx = ExitStack()
                wqp = [mctx.enter_context(nc.sbuf_tensor(f"wqp{s}_{i}", [128, 1024], BF16))
                       for i in range(2)]
                xnb_m = [mctx.enter_context(
                    nc.sbuf_tensor(f"xnb_m{s}_{i}", [128, DCH * QW], BF16))
                    for i in range(2)]
                qc1 = mctx.enter_context(nc.sbuf_tensor(f"qc1{s}", [128, QW], F32))
                qc2 = mctx.enter_context(nc.sbuf_tensor(f"qc2{s}", [128, QW], F32))
                qbf = [mctx.enter_context(nc.sbuf_tensor(f"qbf{s}_{i}", [128, QW], BF16))
                       for i in range(2)]
                pbf = [mctx.enter_context(nc.sbuf_tensor(f"pbf{s}_{i}", [128, QW], BF16))
                       for i in range(4)]
                accf = mctx.enter_context(nc.sbuf_tensor(f"accf{s}", [128, QW], F32))
                accbf = mctx.enter_context(nc.sbuf_tensor(f"accbf{s}", [128, NPAIR * QW], BF16))
                rd_a = mctx.enter_context(nc.sbuf_tensor(f"rd_a{s}", [1, QW], F32))
                rd_b = mctx.enter_context(nc.sbuf_tensor(f"rd_b{s}", [1, QW], F32))
                att_st = [mctx.enter_context(nc.sbuf_tensor(f"att_st{s}_{i}", [128, QW], F32))
                          for i in range(2)]
                colsg_ring = [mctx.enter_context(
                    nc.sbuf_tensor(f"colsg{s}_{i}", [128, TILE], BF16)) for i in range(4)]

                def load_wq(pi):
                    w = wqp[pi % 2]
                    P.dma("sync", w[:, :], wqp_ap[pi * 128:(pi + 1) * 128, :],
                          reads=[wqp_ap], writes=[w], stream="wq")

                def at_tile(t):
                    q0 = t * QW
                    xnb_c = xnb_m[t % 2]

                    def stage_q(pi):
                        qb = qbf[pi % 2]
                        w = wqp[pi % 2]
                        for j in range(DCH):
                            V("tensor", lambda e, j=j, w=w: e.matmul(
                                pb[2][:, :], w[:, j * 128:(j + 1) * 128],
                                xnb_c[:, j * QW:(j + 1) * QW],
                                start=(j == 0), stop=(j == DCH - 1)),
                              reads=[w, xnb_c], writes=[pb[2]], tag="qmm")
                        for h0 in (0, 64):
                            V("vector", lambda e, q0=q0, h0=h0: e.tensor_tensor(
                                qc1[h0:h0 + 64, :], pb[2][h0:h0 + 64, :],
                                cos_d[:, q0:q0 + QW], ALU.mult),
                              reads=[pb[2], cos_d], writes=[qc1])
                        for p0, rp0 in ((0, 32), (32, 0), (64, 96), (96, 64)):
                            V("vector", lambda e, p0=p0, rp0=rp0, q0=q0: e.tensor_tensor(
                                qc2[p0:p0 + 32, :], pb[2][rp0:rp0 + 32, :],
                                sin_s[p0 % 64:p0 % 64 + 32, q0:q0 + QW], ALU.mult),
                              reads=[pb[2], sin_s], writes=[qc2])
                        V("vector", lambda e, qb=qb: e.tensor_tensor(
                            qb[:, :], qc1[:, :], qc2[:, :], ALU.add),
                          reads=[qc1, qc2], writes=[qb])

                    load_wq(0)
                    for pi in range(NPAIR):
                        a, bh = PAIRS[pi]
                        qb = qbf[pi % 2]
                        g = pi // 4
                        kva = a // 4
                        kvb = bh // 4
                        stage_q(pi)
                        if pi + 1 < NPAIR:
                            load_wq(pi + 1)
                        yield
                        slope_b = float(2.0 ** (-(bh + 1) / 2.0))
                        thr = (L - 128 - 160.0 / slope_b) / 128.0
                        kc0 = max(0, min(NKC - 1, int(np.floor(thr)) + 1)) if thr > 0 else 0
                        pending = None
                        for kc in range(kc0, NKC):
                            par = (kc - kc0) % 2
                            pa = pbf[par * 2 + 0]
                            pb_ = pbf[par * 2 + 1]
                            for ti, (r0, c0) in enumerate(((0, 0), (64, 0), (0, 64),
                                                           (64, 64))):
                                dst = (pb[0] if r0 == 0 else pb[1])
                                V("tensor",
                                  lambda e, r0=r0, c0=c0, kc=kc, g=g, qb=qb, dst=dst:
                                  e.matmul(
                                      dst[c0:c0 + 64, :],
                                      ktile[r0:r0 + 64,
                                            g * L + kc * 128 + c0: g * L + kc * 128 + c0 + 64],
                                      qb[r0:r0 + 64, :],
                                      start=True, stop=True, tile_position=(r0, c0)),
                                  reads=[ktile, qb], writes=[dst], tag="smm")
                            if pending is not None:
                                for acc_i, psrc, kvx in ((2, pending[1], kva),
                                                         (3, pending[2], kvb)):
                                    V("tensor",
                                      lambda e, kc=pending[0], psrc=psrc, acc_i=acc_i,
                                      kvx=kvx, kc0=kc0:
                                      e.matmul(
                                        pb[acc_i][0:65, :],
                                        v4[:, kc * VROW + kvx * 72: kc * VROW + kvx * 72 + 65],
                                        psrc[:, :],
                                        start=(kc == kc0), stop=(kc == NKC - 1)),
                                      reads=[v4, psrc], writes=[pb[acc_i]], tag="pv")
                            V("scalar", lambda e, pa=pa, kc=kc, a=a: e.activation(
                                pa[:, :], pb[0][:, :], AF.Exp,
                                bias=bias_hk[:, a * NKC + kc:a * NKC + kc + 1],
                                scale=0.125),
                              reads=[pb[0], bias_hk], writes=[pa], tag="exp")
                            V("scalar", lambda e, pb_=pb_, kc=kc, bh=bh: e.activation(
                                pb_[:, :], pb[1][:, :], AF.Exp,
                                bias=bias_hk[:, bh * NKC + kc:bh * NKC + kc + 1],
                                scale=0.125),
                              reads=[pb[1], bias_hk], writes=[pb_], tag="exp")
                            pending = (kc, pa, pb_)
                            yield
                        for acc_i, psrc, kvx in ((2, pending[1], kva), (3, pending[2], kvb)):
                            V("tensor",
                              lambda e, kc=pending[0], psrc=psrc, acc_i=acc_i, kvx=kvx,
                              kc0=kc0:
                              e.matmul(
                                pb[acc_i][0:65, :],
                                v4[:, kc * VROW + kvx * 72: kc * VROW + kvx * 72 + 65],
                                psrc[:, :],
                                start=(kc == kc0), stop=(kc == NKC - 1)),
                              reads=[v4, psrc], writes=[pb[acc_i]], tag="pv")
                        V("scalar", lambda e: e.activation(rd_a[:, :], pb[2][64:65, :], AF.Ln),
                          reads=[pb[2]], writes=[rd_a])
                        V("scalar", lambda e: e.activation(rd_a[:, :], rd_a[:, :], AF.Exp,
                                                           scale=-1.0),
                          reads=[rd_a], writes=[rd_a])
                        V("scalar", lambda e: e.activation(rd_b[:, :], pb[3][64:65, :], AF.Ln),
                          reads=[pb[3]], writes=[rd_b])
                        V("scalar", lambda e: e.activation(rd_b[:, :], rd_b[:, :], AF.Exp,
                                                           scale=-1.0),
                          reads=[rd_b], writes=[rd_b])
                        first = True
                        for selp, rowt in ((psel_a, rd_a), (psel_b, rd_b)):
                            hi, lo = row_hilo(rowt[:, :], QW, [rowt])
                            V("tensor", lambda e, selp=selp, hi=hi, first=first: e.matmul(
                                pb[0][:, :], selp[:, :], hi, start=first, stop=False),
                              reads=[selp, rhi], writes=[pb[0]])
                            V("tensor", lambda e, selp=selp, lo=lo: e.matmul(
                                pb[0][:, :], selp[:, :], lo, start=False,
                                stop=(selp is psel_b)),
                              reads=[selp, rlo], writes=[pb[0]])
                            first = False
                        V("vector", lambda e: e.tensor_copy(accf[0:64, :], pb[2][0:64, :]),
                          reads=[pb[2]], writes=[accf])
                        V("vector", lambda e: e.tensor_copy(accf[64:128, :], pb[3][0:64, :]),
                          reads=[pb[3]], writes=[accf])
                        V("vector", lambda e, pi=pi: e.tensor_tensor(
                            accbf[:, pi * QW:(pi + 1) * QW], accf[:, :], pb[0][:, :],
                            ALU.mult),
                          reads=[accf, pb[0]], writes=[accbf], tag="accn")
                        yield
                    for ts in range(4):
                        for dhh in range(2):
                            i2 = (ts * 2 + dhh) % 2
                            dst = pb[2 + i2]
                            stg = att_st[i2]
                            for pi2 in range(NPAIR):
                                V("tensor", lambda e, pi2=pi2, ts=ts, dhh=dhh, dst=dst:
                                  e.matmul(
                                    dst[:, :],
                                    accbf[:, pi2 * QW + ts * 128: pi2 * QW + ts * 128 + 128],
                                    wo_sb[:, pi2 * 1024 + dhh * 512:
                                          pi2 * 1024 + dhh * 512 + 512],
                                    start=(pi2 == 0), stop=(pi2 == NPAIR - 1)),
                                  reads=[accbf, wo_sb], writes=[dst], tag="womm")
                            V("vector", lambda e, dst=dst, stg=stg: e.tensor_copy(
                                stg[:, :], dst[:, :]),
                              reads=[dst], writes=[stg])
                            P.dma("sync",
                                  attn_sc[s * L + q0 + ts * 128:
                                          s * L + q0 + ts * 128 + 128,
                                          dhh * 512:dhh * 512 + 512],
                                  stg[:, :], reads=[stg], writes=[attn_sc], stream="atw")
                            yield

                def f1_tile(t):
                    t0 = t * QW
                    xnb_c = xnb_m[t % 2]
                    for jc in range(32):
                        cpb = pb[4 + jc % 2]
                        for j in range(DCH):
                            V("tensor", lambda e, jc=jc, j=j, cpb=cpb: e.matmul(
                                cpb[:, :], wc_sb[j][:, jc * 128:(jc + 1) * 128],
                                xnb_c[:, j * QW:(j + 1) * QW],
                                start=(j == 0), stop=(j == DCH - 1)),
                              reads=[wc_sb[j], xnb_c], writes=[cpb], tag="cmm")
                        cg = colsg_ring[jc % 4]
                        V("vector", lambda e, cpb=cpb, cg=cg: e.tensor_copy(
                            cg[:, :], cpb[:, :]),
                          reads=[cpb], writes=[cg], tag="craw")
                        P.dma("scalar",
                              cols_sc[jc * 128:(jc + 1) * 128,
                                      s * L + t0:s * L + t0 + TILE],
                              cg[:, :], reads=[cg], writes=[cols_sc], stream="clw")
                        yield

                def load_xnb(t):
                    q0 = t * QW
                    xm = xnb_m[t % 2]
                    for j in range(0, DCH, 4):
                        P.dma("sync", xm[:, j * QW:(j + 4) * QW].rearrange(
                                  "p (a t2) -> p a t2", a=4),
                              xnb_sc.rearrange("(a p) t2 -> p a t2", p=128)[
                                  :, j:j + 4, s * L + q0:s * L + q0 + QW],
                              reads=[xnb_sc], writes=[xm], stream="xnr")

                load_xnb(0)
                for t in range(4):
                    q0 = t * QW
                    ga = at_tile(t)
                    gf = f1_tile(t)

                    alive_a = alive_f = True
                    cnt = 0
                    while alive_a or alive_f:
                        if alive_a:
                            try:
                                next(ga)
                            except StopIteration:
                                alive_a = False
                        cnt += 1
                        if cnt == 50 and t + 1 < 4:
                            load_xnb(t + 1)
                        if alive_f and (cnt % 4 == 0 or not alive_a):
                            try:
                                next(gf)
                            except StopIteration:
                                alive_f = False
                P.barrier()
                mctx.close()
            kv_ctx.close()

        for s in range(NSEQ):
            emit_seq(s)
        P.barrier()
        aw_ctx.close()
        ropes_ctx.close()
    else:
        P.barrier()
        ropes_ctx.close()

    if "F2" in phases:
        f2ctx = ExitStack()

        def fsb(n, sh, dt=F32):
            return f2ctx.enter_context(nc.sbuf_tensor(n, sh, dt))

        pwp = [fsb(f"pwp{i}", [128, 4 * 512], BF16) for i in range(3)]
        cols_fb = [fsb(f"cols_fb{g}", [128, 4 * TILE], BF16) for g in range(8)]
        xnb_f = [fsb(f"xnb_f{i}", [128, DCH * TILE], BF16) for i in range(2)]
        y_sb = [fsb("y_sb0", [128, 4 * 1024])] * 2
        xn_t = fsb("xn_t", [128, 4 * 1024])
        x_rg = [fsb(f"x_rg{i}", [128, 1024]) for i in range(2)]
        at_rg = [fsb(f"at_rg{i}", [128, 1024]) for i in range(2)]
        out_rg = [fsb(f"out_rg{i}", [128, 1024]) for i in range(2)]
        lng_bc = fsb("lng_bc", [128, 1024])
        lnb_bc = fsb("lnb_bc", [128, 1024])
        pbrh = fsb("pbrh", [1, 1024], BF16)
        pbrl = fsb("pbrl", [1, 1024], BF16)
        rows4 = [fsb(f"rows4_{i}", [4, 512]) for i in range(3)]
        rows_t = [fsb(f"rows_t{i}", [128, 16]) for i in range(4)]
        invr0 = fsb("invr0", [1, 512])
        st_sc = fsb("st_sc", [128, 8])
        bn6 = fsb("bn6", [128, 12])
        bn2 = fsb("bn2", [128, 2])
        h_bf = fsb("h_bf", [128, 2 * TILE], BF16)
        g2s = [fsb(f"g2s{i}", [C, TILE]) for i in range(2)]
        invnb = [fsb(f"invnb{i}", [C, TILE]) for i in range(2)]
        sim_s = [fsb(f"sim_s{i}", [C, TILE]) for i in range(2)]
        logit_f = fsb("logit_f", [C, TILE])
        logit_t = fsb("logit_t", [128, 4 * C])
        negm = fsb("negm", [128, 1])
        negmh = fsb("negmh", [128, 1])
        maskt = fsb("maskt", [128, C])
        eet = fsb("eet", [128, C])
        dent = fsb("dent", [128, C])
        ssum = fsb("ssum", [128, 1])
        rsum = fsb("rsum", [128, 1])
        w_tm = fsb("w_tm", [128, C])
        w_tmf = fsb("w_tmf", [C, TILE])
        w_hi = fsb("w_hi", [C, TILE], BF16)
        w_lo = fsb("w_lo", [C, TILE], BF16)
        wbig = fsb("wbig", [128, C * TILE], BF16)
        cgr = [fsb(f"cgr{i}", [128, TILE], BF16) for i in range(4)]
        gw1_sb = fsb("gw1_sb", [128, DCH * 512], BF16)
        gw2_sb = fsb("gw2_sb", [128, 4 * 16], BF16)

        crow2 = out_rg[0][0:1, :]
        for dc in range(DCH):
            P.dma("gpsimd", gw1_sb[:, dc * 512:(dc + 1) * 512],
                  gw1_in[dc * 128:(dc + 1) * 128, :], writes=[gw1_sb], stream="wgt")
        for hf in range(4):
            P.dma("gpsimd", gw2_sb[:, hf * 16:(hf + 1) * 16],
                  gw2_in[hf * 128:(hf + 1) * 128, :], writes=[gw2_sb], stream="wgt")
        for src_in, dstbc in ((lng_in, lng_bc), (lnb_in, lnb_bc)):
            P.dma("sync", crow2[:, :], src_in.rearrange("(u d) -> u d", u=1),
                  writes=[out_rg[0]], stream="smallw")
            for dh in range(2):
                hi, lo = row_hilo(crow2[:, dh * 512:(dh + 1) * 512], 512, [out_rg[0]])
                V("tensor", lambda e, hi=hi: e.matmul(
                    pb[6][:, :], ones_row_bf[:, :], hi, start=True, stop=False),
                  reads=[ones_row_bf, rhi], writes=[pb[6]])
                V("tensor", lambda e, lo=lo: e.matmul(
                    pb[6][:, :], ones_row_bf[:, :], lo, start=False, stop=True),
                  reads=[ones_row_bf, rlo], writes=[pb[6]])
                V("vector", lambda e, dstbc=dstbc, dh=dh: e.tensor_copy(
                    dstbc[:, dh * 512:(dh + 1) * 512], pb[6][:, :]),
                  reads=[pb[6]], writes=[dstbc])
        P.dma("sync", crow2[:, :], pb_in.rearrange("(u d) -> u d", u=1),
              writes=[out_rg[0]], stream="smallw")
        for dh in range(2):
            V("vector", lambda e, dh=dh: e.tensor_copy(
                pbrh[:, dh * 512:(dh + 1) * 512], crow2[:, dh * 512:(dh + 1) * 512]),
              reads=[out_rg[0]], writes=[pbrh])
            V("vector", lambda e, dh=dh: e.tensor_copy(
                rtmp[:, :], pbrh[:, dh * 512:(dh + 1) * 512]),
              reads=[pbrh], writes=[rtmp])
            V("vector", lambda e, dh=dh: e.tensor_tensor(
                pbrl[:, dh * 512:(dh + 1) * 512], crow2[:, dh * 512:(dh + 1) * 512],
                rtmp[:, :], ALU.subtract),
              reads=[out_rg[0], rtmp], writes=[pbrl])

        tiles2 = [(s, it) for s in range(NSEQ) for it in range(NTILE)]
        NT2 = len(tiles2)

        def pw_load(dh, g, buf):
            P.dma("sync", buf[:, :].rearrange("p (a q) -> p a q", a=4),
                  pw_sc_ap.rearrange("(a p) q -> p a q", p=128)[
                      :, g * 4:(g + 1) * 4, dh * 512:(dh + 1) * 512],
                  reads=[pw_sc_ap], writes=[buf], stream="pw")

        def cols_load(ti, g):
            s, it = tiles2[ti]
            t0 = it * TILE
            cb = cols_fb[g]
            P.dma("gpsimd", cb[:, :].rearrange("p (a t) -> p a t", a=4),
                  cols_sc.rearrange("(a p) t -> p a t", p=128)[
                      :, g * 4:(g + 1) * 4, s * L + t0:s * L + t0 + TILE],
                  reads=[cols_sc], writes=[cb], stream="clr")

        def at_load(ti, ts):
            s, it = tiles2[ti]
            t0 = it * TILE
            ab = at_rg[ts % 2]
            P.dma("sync", ab[:, :],
                  attn_sc[s * L + t0 + ts * 128: s * L + t0 + (ts + 1) * 128, :],
                  reads=[attn_sc], writes=[ab], stream="ats")

        def f2_rows_prep(ti):
            s, it = tiles2[ti]
            t0 = it * TILE
            r4 = rows4[ti % 3]
            rt = rows_t[ti % 4]
            P.dma("sync", r4[:, :], rows_sc[0:4, s * L + t0:s * L + t0 + TILE],
                  reads=[rows_sc], writes=[r4], stream="rws")

            for ts in range(4):
                V("tensor", lambda e, ts=ts, r4=r4: e.transpose(
                    pb[7][:, 0:4], r4[:, ts * 128:(ts + 1) * 128], ident[0:4, 0:4]),
                  reads=[r4, ident], writes=[pb[7]], tag="rT")
                V("vector", lambda e, ts=ts, rt=rt: e.tensor_copy(
                    rt[:, ts * 4:(ts + 1) * 4], pb[7][:, 0:4]),
                  reads=[pb[7]], writes=[rt])

        def f2_xnb_load(ti):
            s, it = tiles2[ti]
            t0 = it * TILE
            xm = xnb_f[ti % 2]
            for j in range(0, DCH, 4):
                P.dma("sync", xm[:, j * TILE:(j + 4) * TILE].rearrange(
                          "p (a t) -> p a t", a=4),
                      xnb_sc.rearrange("(a p) t -> p a t", p=128)[
                          :, j:j + 4, s * L + t0:s * L + t0 + TILE],
                      reads=[xnb_sc], writes=[xm], stream="xnr")

        def f2_proj_dh(ti, dh):
            yb = y_sb[ti % 2]
            for g in range(8):
                k = dh * 8 + g
                kg = ti * 16 + k
                ng = kg + 2
                if ng < NT2 * 16 and (ng % 16 < 16 if ng // 16 == ti else ng // 16 < NT2):
                    pw_load((ng % 16) // 8, ng % 8, pwp[ng % 3])
                buf = pwp[kg % 3]
                for ts in range(4):
                    if g == 0:
                        V("tensor", lambda e, ts=ts, dh=dh: e.matmul(
                            pb[ts][:, :], ones_row_bf[:, :],
                            pbrh[:, dh * 512:(dh + 1) * 512], start=True, stop=False),
                          reads=[ones_row_bf, pbrh], writes=[pb[ts]], tag="pbj")
                        V("tensor", lambda e, ts=ts, dh=dh: e.matmul(
                            pb[ts][:, :], ones_row_bf[:, :],
                            pbrl[:, dh * 512:(dh + 1) * 512], start=False, stop=False),
                          reads=[ones_row_bf, pbrl], writes=[pb[ts]], tag="pbj")
                    cb = cols_fb[g]
                    for jj in range(4):
                        j = g * 4 + jj
                        V("tensor", lambda e, j=j, jj=jj, ts=ts, buf=buf, cb=cb: e.matmul(
                            pb[ts][:, :],
                            cb[:, jj * TILE + ts * 128: jj * TILE + ts * 128 + 128],
                            buf[:, jj * 512:(jj + 1) * 512],
                            start=False, stop=(j == 31)),
                          reads=[cb, buf], writes=[pb[ts]], tag="pmm")
                if dh == 1 and ti + 1 < NT2:
                    cols_load(ti + 1, g)
            for ts in range(4):
                V("vector", lambda e, ts=ts, dh=dh, yb=yb: e.tensor_tensor(
                    yb[:, ts * 1024 + dh * 512: ts * 1024 + (dh + 1) * 512],
                    pb[ts][:, :],
                    xn_t[:, ts * 1024 + dh * 512: ts * 1024 + (dh + 1) * 512],
                    ALU.add),
                  reads=[pb[ts], xn_t], writes=[yb], tag="yev")

        def f2_gate_p1(ti):
            s, it = tiles2[ti]
            t0 = it * TILE
            P.dma("sync", invr0[:, :], rows_sc[3:4, s * L + t0:s * L + t0 + TILE],
                  reads=[rows_sc], writes=[invr0], stream="rws")
            xnb_c = xnb_f[ti % 2]
            g2c = g2s[ti % 2]
            invc = invnb[ti % 2]
            for hf in range(4):
                for j in range(DCH):
                    V("tensor", lambda e, hf=hf, j=j, xnb_c=xnb_c: e.matmul(
                        pb[4][:, :],
                        gw1_sb[:, j * 512 + hf * 128: j * 512 + (hf + 1) * 128],
                        xnb_c[:, j * TILE:(j + 1) * TILE],
                        start=(j == 0), stop=(j == DCH - 1)),
                      reads=[gw1_sb, xnb_c], writes=[pb[4]], tag="gmm")
                V("scalar", lambda e, hf=hf: e.activation(
                    h_bf[:, (hf % 2) * TILE:(hf % 2 + 1) * TILE], pb[4][:, :], AF.Gelu,
                    bias=b1_fm[:, hf:hf + 1]),
                  reads=[pb[4], b1_fm], writes=[h_bf], tag="gelu1")
                if hf > 0:
                    hp = hf - 1
                    V("tensor", lambda e, hp=hp: e.matmul(
                        pb[5][0:C, :], gw2_sb[:, hp * 16:(hp + 1) * 16],
                        h_bf[:, (hp % 2) * TILE:(hp % 2 + 1) * TILE],
                        start=(hp == 0), stop=False),
                      reads=[gw2_sb, h_bf], writes=[pb[5]], tag="g2mm")
            V("tensor", lambda e: e.matmul(
                pb[5][0:C, :], gw2_sb[:, 3 * 16:4 * 16], h_bf[:, TILE:2 * TILE],
                start=False, stop=True),
              reads=[gw2_sb, h_bf], writes=[pb[5]], tag="g2mm")
            V("scalar", lambda e, g2c=g2c: e.activation(g2c[:, :], pb[5][0:C, :], AF.Tanh,
                                                        scale=0.5, bias=gb2h_t[:, :]),
              reads=[pb[5], gb2h_t], writes=[g2c])
            V("vector", lambda e, g2c=g2c: e.tensor_scalar(g2c[:, :], g2c[:, :], 0.5, 0.5,
                                                           op0=ALU.mult, op1=ALU.add),
              reads=[g2c], writes=[g2c])
            for j in range(DCH):
                V("tensor", lambda e, j=j, xnb_c=xnb_c: e.matmul(
                    pb[4][0:C, :], ce_fm[:, j * C:(j + 1) * C],
                    xnb_c[:, j * TILE:(j + 1) * TILE],
                    start=(j == 0), stop=(j == DCH - 1)),
                  reads=[ce_fm, xnb_c], writes=[pb[4]], tag="simm")
            V("vector", lambda e, ti=ti: e.tensor_copy(sim_s[ti % 2][:, :], pb[4][0:C, :]),
              reads=[pb[4]], writes=[sim_s[ti % 2]], tag="simev")
            hi, lo = row_hilo(invr0[:, :], TILE, [invr0])
            V("tensor", lambda e, hi=hi: e.matmul(
                pb[5][0:C, :], ones_row_bf[:, 0:C], hi, start=True, stop=False),
              reads=[ones_row_bf, rhi], writes=[pb[5]])
            V("tensor", lambda e, lo=lo: e.matmul(
                pb[5][0:C, :], ones_row_bf[:, 0:C], lo, start=False, stop=True),
              reads=[ones_row_bf, rlo], writes=[pb[5]])
            V("vector", lambda e, invc=invc: e.tensor_copy(invc[:, :], pb[5][0:C, :]),
              reads=[pb[5]], writes=[invc])

        def f2_gate_p2(ti):
            g2c = g2s[ti % 2]
            invc = invnb[ti % 2]
            simc = sim_s[ti % 2]
            V("vector", lambda e, simc=simc, invc=invc: e.tensor_tensor(
                logit_f[:, :], simc[:, :], invc[:, :], ALU.mult),
              reads=[simc, invc], writes=[logit_f])
            V("vector", lambda e, g2c=g2c: e.tensor_tensor(logit_f[:, :], logit_f[:, :],
                                                           g2c[:, :], ALU.add),
              reads=[logit_f, g2c], writes=[logit_f])
            for u in range(4):
                V("tensor", lambda e, u=u: e.transpose(
                    pb[6][:, u * C:(u + 1) * C],
                    logit_f[:, u * 128:(u + 1) * 128], ident[0:C, 0:C]),
                  reads=[logit_f, ident], writes=[pb[6]], tag="lgT")
                V("vector", lambda e, u=u: e.tensor_copy(
                    logit_t[:, u * C:(u + 1) * C], pb[6][:, u * C:(u + 1) * C]),
                  reads=[pb[6]], writes=[logit_t])
            for u in range(4):
                lt = logit_t[:, u * C:(u + 1) * C]
                V("vector", lambda e, lt=lt: e.reduce_max(negm[:, :], lt, axis=AX.X,
                                                          negate=True),
                  reads=[logit_t], writes=[negm])
                V("vector", lambda e: e.tensor_scalar_mul(negmh[:, :], negm[:, :], 0.5),
                  reads=[negm], writes=[negmh])
                V("vector", lambda e, lt=lt: e.tensor_scalar(maskt[:, :], lt,
                                                             negm[:, :], None,
                                                             op0=ALU.add),
                  reads=[logit_t, negm], writes=[maskt])
                V("vector", lambda e: e.tensor_single_scalar(maskt[:, :], maskt[:, :],
                                                             0.0, op=ALU.is_ge),
                  reads=[maskt], writes=[maskt])
                V("scalar", lambda e, lt=lt: e.activation(eet[:, :], lt, AF.Tanh,
                                                          scale=0.5, bias=negmh[:, :]),
                  reads=[logit_t, negmh], writes=[eet])
                V("vector", lambda e: e.tensor_scalar(dent[:, :], eet[:, :], -1.0, 1.0,
                                                      op0=ALU.mult, op1=ALU.add),
                  reads=[eet], writes=[dent])
                V("vector", lambda e: e.reciprocal(dent[:, :], dent[:, :]),
                  reads=[dent], writes=[dent])
                V("vector", lambda e: e.scalar_tensor_tensor(
                    eet[:, :], eet[:, :], 1.0, dent[:, :], op0=ALU.add, op1=ALU.mult),
                  reads=[eet, dent], writes=[eet])
                V("vector", lambda e: e.reduce_sum(ssum[:, :], eet[:, :], axis=AX.X),
                  reads=[eet], writes=[ssum])
                V("vector", lambda e: e.reciprocal(rsum[:, :], ssum[:, :]),
                  reads=[ssum], writes=[rsum])
                V("vector", lambda e: e.scalar_tensor_tensor(
                    w_tm[:, :], eet[:, :], rsum[:, :], maskt[:, :],
                    op0=ALU.mult, op1=ALU.mult),
                  reads=[eet, rsum, maskt], writes=[w_tm])
                V("tensor", lambda e, u=u: e.transpose(
                    pb[6][0:C, 128 + (u % 2) * 128:128 + (u % 2 + 1) * 128],
                    w_tm[:, :], ident[:, :]),
                  reads=[w_tm, ident], writes=[pb[6]], tag="wT")
                V("vector", lambda e, u=u: e.tensor_copy(
                    w_tmf[:, u * 128:(u + 1) * 128],
                    pb[6][0:C, 128 + (u % 2) * 128:128 + (u % 2 + 1) * 128]),
                  reads=[pb[6]], writes=[w_tmf])
            V("vector", lambda e: e.tensor_copy(w_hi[:, :], w_tmf[:, :]),
              reads=[w_tmf], writes=[w_hi])
            V("vector", lambda e, invc=invc: e.tensor_copy(invc[:, :], w_hi[:, :]),
              reads=[w_hi], writes=[invc])
            V("vector", lambda e, invc=invc: e.tensor_tensor(invc[:, :], w_tmf[:, :],
                                                             invc[:, :], ALU.subtract),
              reads=[w_tmf, invc], writes=[invc])
            V("vector", lambda e, invc=invc: e.tensor_copy(w_lo[:, :], invc[:, :]),
              reads=[invc], writes=[w_lo])

        def f2_stats(ti):
            yb = y_sb[ti % 2]
            for ts in range(4):
                V("vector", lambda e, ts=ts, yb=yb: e.bn_stats(
                    bn6[:, 0:6], yb[:, ts * 1024:ts * 1024 + 512]),
                  reads=[yb], writes=[bn6])
                V("vector", lambda e, ts=ts, yb=yb: e.bn_stats(
                    bn6[:, 6:12], yb[:, ts * 1024 + 512:(ts + 1) * 1024]),
                  reads=[yb], writes=[bn6])
                V("vector", lambda e: e.bn_aggr(bn2[:, :], bn6[:, :]),
                  reads=[bn6], writes=[bn2])
                V("vector", lambda e, ts=ts: e.tensor_copy(
                    st_sc[:, ts * 2:ts * 2 + 1], bn2[:, 0:1]),
                  reads=[bn2], writes=[st_sc])
                V("vector", lambda e, ts=ts: e.tensor_scalar_add(
                    st_sc[:, ts * 2 + 1:ts * 2 + 2], bn2[:, 1:2], float(LN_EPS)),
                  reads=[bn2], writes=[st_sc])
            for ts in range(4):
                V("scalar", lambda e, ts=ts: e.activation(
                    st_sc[:, ts * 2 + 1:ts * 2 + 2], st_sc[:, ts * 2 + 1:ts * 2 + 2],
                    AF.Ln),
                  reads=[st_sc], writes=[st_sc], tag="r2ln")
            for ts in range(4):
                V("scalar", lambda e, ts=ts: e.activation(
                    st_sc[:, ts * 2 + 1:ts * 2 + 2], st_sc[:, ts * 2 + 1:ts * 2 + 2],
                    AF.Exp, scale=-0.5),
                  reads=[st_sc], writes=[st_sc], tag="r2exp")
            at_load(ti, 0)

        def f2_wb(ti):
            for c in range(C):
                wpb = pb[6 + c % 2]
                V("tensor", lambda e, c=c, wpb=wpb: e.matmul(
                    wpb[:, :], selc[:, c * 128:(c + 1) * 128], w_hi[:, :],
                    start=True, stop=False),
                  reads=[selc, w_hi], writes=[wpb], tag="wbm")
                V("tensor", lambda e, c=c, wpb=wpb: e.matmul(
                    wpb[:, :], selc[:, c * 128:(c + 1) * 128], w_lo[:, :],
                    start=False, stop=True),
                  reads=[selc, w_lo], writes=[wpb], tag="wbm")
                V("vector", lambda e, wpb=wpb, c=c: e.tensor_copy(
                    wbig[:, c * TILE:(c + 1) * TILE], wpb[:, :]),
                  reads=[wpb], writes=[wbig], tag="wbev")

        def f2_gelwmul(ti):
            for g in range(8):
                cb = cols_fb[g]
                for jj in range(4):
                    jc = g * 4 + jj
                    V("scalar", lambda e, jc=jc, jj=jj, cb=cb: e.activation(
                        cgr[jj][:, :], cb[:, jj * TILE:(jj + 1) * TILE], AF.Gelu,
                        bias=bcols_fm[:, jc:jc + 1]),
                      reads=[cb, bcols_fm], writes=[cgr[jj]], tag="gelu2")
                for jj in range(4):
                    jc = g * 4 + jj
                    c = jc // 2
                    eng = "vector" if jj % 2 else "gpsimd"
                    V(eng, lambda e, jj=jj, cb=cb, c=c: e.tensor_tensor(
                        cb[:, jj * TILE:(jj + 1) * TILE], cgr[jj][:, :],
                        wbig[:, c * TILE:(c + 1) * TILE], ALU.mult),
                      reads=[cgr[jj], wbig], writes=[cb], tag="wmul")

        def f2_combine(ti):
            s, it = tiles2[ti]
            t0 = it * TILE
            yb = y_sb[ti % 2]
            rt = rows_t[ti % 4]
            for ts in range(4):
                if ts + 1 < 4:
                    at_load(ti, ts + 1)
                i = ts % 2
                mu2 = st_sc[:, ts * 2:ts * 2 + 1]
                r2 = st_sc[:, ts * 2 + 1:ts * 2 + 2]
                ys = yb[:, ts * 1024:(ts + 1) * 1024]
                xs = xn_t[:, ts * 1024:(ts + 1) * 1024]
                og = out_rg[i]
                V("vector", lambda e, ys=ys, og=og, mu2=mu2, r2=r2: e.tensor_scalar(
                    og[:, :], ys, mu2, r2, op0=ALU.subtract, op1=ALU.mult),
                  reads=[yb, st_sc], writes=[og], tag="fc")
                V("gpsimd", lambda e, og=og: e.tensor_tensor(
                    og[:, :], og[:, :], lng_bc[:, :], ALU.mult),
                  reads=[og, lng_bc], writes=[og], tag="fc")
                V("vector", lambda e, xs=xs, rt=rt, ts=ts: e.tensor_scalar(
                    xs, xs, rt[:, ts * 4 + 2:ts * 4 + 3], rt[:, ts * 4 + 0:ts * 4 + 1],
                    op0=ALU.mult, op1=ALU.add),
                  reads=[xn_t, rt], writes=[xn_t], tag="fc")
                V("gpsimd", lambda e, xs=xs, i=i: e.tensor_tensor(
                    xs, xs, at_rg[i][:, :], ALU.add),
                  reads=[xn_t, at_rg[i]], writes=[xn_t], tag="fc")
                V("vector", lambda e, og=og, xs=xs: e.tensor_tensor(
                    og[:, :], og[:, :], xs, ALU.add),
                  reads=[og, xn_t], writes=[og], tag="fc")
                V("gpsimd", lambda e, og=og: e.tensor_tensor(
                    og[:, :], og[:, :], lnb_bc[:, :], ALU.add),
                  reads=[og, lnb_bc], writes=[og], tag="fc")
                P.dma("scalar",
                      y_out[s * L + t0 + ts * 128: s * L + t0 + (ts + 1) * 128, :],
                      og[:, :], reads=[og], writes=[y_out], stream="out")

        def f2_xnt(ti):
            s, it = tiles2[ti]
            t0 = it * TILE
            rt = rows_t[ti % 4]
            for ts in range(4):
                xb = x_rg[ts % 2]
                P.dma("scalar", xb[:, :],
                      x_in[s * L + t0 + ts * 128: s * L + t0 + (ts + 1) * 128, :],
                      reads=[x_in], writes=[xb], stream="xs")
                V("vector", lambda e, xb=xb, rt=rt, ts=ts: e.tensor_scalar(
                    xn_t[:, ts * 1024:(ts + 1) * 1024], xb[:, :],
                    rt[:, ts * 4 + 0:ts * 4 + 1], rt[:, ts * 4 + 1:ts * 4 + 2],
                    op0=ALU.subtract, op1=ALU.mult),
                  reads=[xb, rt], writes=[xn_t], tag="xnt")

        for k in range(2):
            pw_load(0, k, pwp[k])
        f2_rows_prep(0)
        f2_xnb_load(0)
        if NT2 > 1:
            f2_rows_prep(1)
            f2_xnb_load(1)
        for g in range(8):
            cols_load(0, g)
        f2_gate_p1(0)
        if NT2 > 2:
            f2_xnb_load(2)
        f2_gate_p2(0)
        if NT2 > 1:
            f2_gate_p1(1)
        f2_wb(0)
        f2_gelwmul(0)
        f2_xnt(0)
        for ti in range(NT2):
            f2_proj_dh(ti, 0)
            if ti + 2 < NT2:
                f2_rows_prep(ti + 2)
            if ti + 1 < NT2:
                f2_gate_p2(ti + 1)
            if ti + 2 < NT2:
                f2_gate_p1(ti + 2)
            f2_proj_dh(ti, 1)
            if ti + 1 < NT2:
                f2_wb(ti + 1)
                f2_gelwmul(ti + 1)
            f2_stats(ti)
            f2_combine(ti)
            if ti + 1 < NT2:
                f2_xnt(ti + 1)
            if ti + 3 < NT2:
                f2_xnb_load(ti + 3)
        P.barrier()
        f2ctx.close()


    P.wait_all("sync")
    P.emit()
    ctx.close()
    return nc


_PROGRAM_CACHE = {}


def _get_program():
    key = (tuple(_PHASES), tuple(_DEBUG_TAPS))
    if key not in _PROGRAM_CACHE:
        _PROGRAM_CACHE[key] = build_program()
    return _PROGRAM_CACHE[key]


def kernel(L0_attn_x, L0_attn_Wq, L0_attn_Wk, L0_attn_Wv, L0_attn_Wo,
           L0_ffn_col_emb, L0_ffn_gate_w1, L0_ffn_gate_b1, L0_ffn_gate_w2,
           L0_ffn_gate_b2, L0_ffn_W_cols, L0_ffn_b_cols, L0_ffn_proj_w,
           L0_ffn_proj_b, L0_ffn_ln_g, L0_ffn_ln_b):
    nc = _get_program()
    x = np.ascontiguousarray(np.asarray(L0_attn_x, dtype=np.float32))
    shared = {
        "Wq": np.ascontiguousarray(L0_attn_Wq, dtype=np.float32),
        "Wk": np.ascontiguousarray(L0_attn_Wk, dtype=np.float32),
        "Wv": np.ascontiguousarray(L0_attn_Wv, dtype=np.float32),
        "Wo": np.ascontiguousarray(L0_attn_Wo, dtype=np.float32),
        "col_emb": np.ascontiguousarray(L0_ffn_col_emb, dtype=np.float32),
        "gate_w1": np.ascontiguousarray(L0_ffn_gate_w1, dtype=np.float32),
        "gate_b1": np.ascontiguousarray(L0_ffn_gate_b1, dtype=np.float32),
        "gate_w2": np.ascontiguousarray(L0_ffn_gate_w2, dtype=np.float32),
        "gate_b2": np.ascontiguousarray(L0_ffn_gate_b2, dtype=np.float32),
        "W_cols": np.ascontiguousarray(L0_ffn_W_cols, dtype=np.float32),
        "b_cols": np.ascontiguousarray(L0_ffn_b_cols, dtype=np.float32),
        "proj_w": np.ascontiguousarray(L0_ffn_proj_w, dtype=np.float32),
        "proj_b": np.ascontiguousarray(L0_ffn_proj_b, dtype=np.float32),
        "ln_g": np.ascontiguousarray(L0_ffn_ln_g, dtype=np.float32),
        "ln_b": np.ascontiguousarray(L0_ffn_ln_b, dtype=np.float32),
    }
    in_maps = []
    for i in range(NCORES):
        m = dict(shared)
        m["x"] = x[i * NSEQ:(i + 1) * NSEQ].reshape(NSEQ * L, D)
        in_maps.append(m)
    res = run_bass_kernel_spmd(nc, in_maps, core_ids=list(range(NCORES)))
    out = np.empty((B, L, D), dtype=np.float32)
    for i in range(NCORES):
        yi = res.results[i]["y"]
        out[i * NSEQ:(i + 1) * NSEQ] = yi.reshape(NSEQ, L, D)
    return out


# version 17
# speedup vs baseline: 1.3170x; 1.3170x over previous
import math
import numpy as np
from contextlib import ExitStack

import concourse.bass as bass
import concourse.mybir as mybir
from concourse.bass_utils import run_bass_kernel_spmd

F32 = mybir.dt.float32
BF16 = mybir.dt.bfloat16
I32 = mybir.dt.int32
AF = mybir.ActivationFunctionType
ALU = mybir.AluOpType
AX = mybir.AxisListType

D = 1024
L = 2048
B = 16
NCORES = 8
NSEQ = B // NCORES
H, KV, HD = 16, 4, 64
GROUPS = H // KV
NPAIR = H // 2
C, NN = 16, 256
CF = C * NN
DCH = D // 128
ROPE_BASE = 10000.0
EXP_MARGIN = 30.0
LN_EPS = 1e-5

TILE = 512
NTILE = L // TILE
NSUB = L // 128
QT = 1024
NQT = L // QT
NKC = L // 128

_DEBUG_TAPS = []
_PHASES = ("A", "AT", "F1", "F2")


def _alibi_slopes(n_heads):
    def pow2(n):
        start = 2.0 ** (-2.0 ** (-(math.log2(n) - 3)))
        return [start * start ** i for i in range(n)]
    if math.log2(n_heads).is_integer():
        return pow2(n_heads)
    c = 2 ** math.floor(math.log2(n_heads))
    return pow2(c) + _alibi_slopes(2 * c)[0::2][: n_heads - c]


COMPUTE_ENGINES = ("tensor", "vector", "scalar", "gpsimd")
ALL_ENGINES = ("tensor", "vector", "scalar", "gpsimd", "sync")
SEM_CAP = 50000


def _key(x):
    t = getattr(x, "tensor", None)
    if t is not None:
        return t.name
    return x.name


class Prog:
    def __init__(self, nc, ctx):
        self.nc = nc
        self.ctx = ctx
        self.ops = {e: [] for e in ALL_ENGINES}
        self.sem_objs = {}
        self.sem_count = {}
        self.sem_exact = {}
        for e in COMPUTE_ENGINES:
            sid = "e:" + e
            self.sem_objs[sid] = ctx.enter_context(nc.semaphore("sem_" + e))
            self.sem_count[sid] = 0
            self.sem_exact[sid] = True
        self.dma_gen = {}
        self.track = {}
        self.waited = {e: {} for e in ALL_ENGINES}
        self.handed_out = {}
        self.n_ops = 0

    def _resolve(self, sem_id, val):
        if val is None or not self.sem_exact[sem_id]:
            return self.sem_count[sem_id]
        return val

    def _edges_for(self, reads, writes):
        need = {}
        for x in reads:
            rec = self.track.get(_key(x))
            if rec and rec["w"]:
                s, v = rec["w"]
                need[s] = max(need.get(s, 0), self._resolve(s, v))
        for x in writes:
            rec = self.track.get(_key(x))
            if rec:
                if rec["w"]:
                    s, v = rec["w"]
                    need[s] = max(need.get(s, 0), self._resolve(s, v))
                for s, v in rec["r"].items():
                    need[s] = max(need.get(s, 0), self._resolve(s, v))
        return need

    def _mk_waits(self, engine, need, skip_own_pe):
        waits = []
        for s, v in sorted(need.items()):
            if skip_own_pe and s == "e:tensor":
                continue
            if v <= 0:
                continue
            self.handed_out[s] = max(self.handed_out.get(s, 0), v)
            if self.waited[engine].get(s, 0) >= v:
                continue
            self.waited[engine][s] = v
            waits.append((self.sem_objs[s], v))
        return waits

    def _record(self, reads, writes, sem_id, val):
        for x in reads:
            rec = self.track.setdefault(_key(x), {"w": None, "r": {}})
            old = rec["r"].get(sem_id, 0)
            if val is None or old is None:
                rec["r"][sem_id] = None
            else:
                rec["r"][sem_id] = max(old, val)
        for x in writes:
            self.track[_key(x)] = {"w": (sem_id, val), "r": {}}

    def op(self, engine, fn, reads=(), writes=(), tag=None):
        assert engine in COMPUTE_ENGINES, engine
        need = self._edges_for(reads, writes)
        waits = self._mk_waits(engine, need, skip_own_pe=(engine == "tensor"))
        sid = "e:" + engine
        self.sem_count[sid] += 1
        val = self.sem_count[sid]
        self._record(reads, writes, sid, val)
        self.ops[engine].append((waits, fn, self.sem_objs[sid], 1, tag))
        self.n_ops += 1

    def _dma_sid(self, queue, stream):
        gkey = (queue, stream)
        gen = self.dma_gen.get(gkey, 0)
        sid = f"d:{queue}:{stream}:{gen}"
        if sid not in self.sem_objs:
            self.sem_objs[sid] = self.ctx.enter_context(
                self.nc.semaphore(f"dma_{queue}_{stream}_{gen}"))
            self.sem_count[sid] = 0
            self.sem_exact[sid] = False
        if self.sem_count[sid] >= SEM_CAP:
            self.dma_gen[gkey] = gen + 1
            return self._dma_sid(queue, stream)
        return sid

    def dma(self, queue, out, in_, reads=(), writes=(), stream="w", tag=None, **kw):
        assert queue in ("sync", "gpsimd", "scalar"), queue
        need = self._edges_for(reads, writes)
        waits = self._mk_waits(queue, need, skip_own_pe=False)
        sid = self._dma_sid(queue, stream)
        ho = self.handed_out.get(sid, 0)
        if ho > self.waited[queue].get(sid, 0):
            self.waited[queue][sid] = ho
            waits.append((self.sem_objs[sid], ho))
        self.sem_count[sid] += 16
        self._record(list(reads), list(writes), sid, None)
        fn = lambda eng, out=out, in_=in_, kw=kw: eng.dma_start(out=out, in_=in_, **kw)
        self.ops[queue].append((waits, fn, self.sem_objs[sid], 16, tag))
        self.n_ops += 1

    def wait_all(self, engine):
        need = {s: c for s, c in self.sem_count.items() if c > 0}
        waits = self._mk_waits(engine, need, skip_own_pe=False)
        self.ops[engine].append((waits, None, None, 0, "barrier"))

    def barrier(self):
        for e in ALL_ENGINES:
            self.wait_all(e)

    def emit(self):
        nc = self.nc
        prog = self
        with nc.Block() as block:
            for engine in ALL_ENGINES:
                if not prog.ops[engine]:
                    continue

                def section(eng, engine=engine):
                    for waits, fn, sem, inc, tag in prog.ops[engine]:
                        for h, v in waits:
                            eng.wait_ge(h, v)
                        if fn is not None:
                            inst = fn(eng)
                            if sem is not None:
                                inst.then_inc(sem, inc)

                getattr(block, engine)(section)


def build_program(phases=_PHASES, dump_scratch=False):
    nc = bass.Bass("TRN2", target_bir_lowering=False)
    ctx = ExitStack()
    P = Prog(nc, ctx)

    def sb(name, shape, dt=F32):
        return ctx.enter_context(nc.sbuf_tensor(name, shape, dt))

    def psum(name, shape, dt=F32):
        return ctx.enter_context(nc.psum_tensor(name, shape, dt))

    def V(e, fn, reads=(), writes=(), tag=None):
        P.op(e, fn, reads=reads, writes=writes, tag=tag)

    def dramio(name, shape, dt, kind):
        return nc.dram_tensor(name, shape, dt, kind=kind).ap()

    x_in = dramio("x", [NSEQ * L, D], F32, "ExternalInput")
    wq_in = dramio("Wq", [D, H * HD], F32, "ExternalInput")
    wk_in = dramio("Wk", [D, KV * HD], F32, "ExternalInput")
    wv_in = dramio("Wv", [D, KV * HD], F32, "ExternalInput")
    wo_in = dramio("Wo", [H * HD, D], F32, "ExternalInput")
    ce_in = dramio("col_emb", [C, D], F32, "ExternalInput")
    gw1_in = dramio("gate_w1", [D, D // 2], F32, "ExternalInput")
    gb1_in = dramio("gate_b1", [D // 2], F32, "ExternalInput")
    gw2_in = dramio("gate_w2", [D // 2, C], F32, "ExternalInput")
    gb2_in = dramio("gate_b2", [C], F32, "ExternalInput")
    wcols_in = dramio("W_cols", [C, D, NN], F32, "ExternalInput")
    bcols_in = dramio("b_cols", [C, NN], F32, "ExternalInput")
    pw_in = dramio("proj_w", [CF, D], F32, "ExternalInput")
    pb_in = dramio("proj_b", [D], F32, "ExternalInput")
    lng_in = dramio("ln_g", [D], F32, "ExternalInput")
    lnb_in = dramio("ln_b", [D], F32, "ExternalInput")
    y_out = dramio("y", [D, NSEQ * L], F32, "ExternalOutput")

    def scratch(name, shape, dt, produced_by, consumed_by):
        prod = produced_by in phases
        cons = any(p in phases for p in consumed_by)
        if prod and dump_scratch:
            return dramio(name, shape, dt, "ExternalOutput")
        if not prod and cons:
            return dramio(name, shape, dt, "ExternalInput")
        if prod or cons:
            return nc.dram_tensor(name, shape, dt).ap()
        return None

    xnb_sc = scratch("xnb_sc", [D, NSEQ * L], BF16, "A", ("AT", "F1"))
    xn32_sc = scratch("xn32_sc", [D, NSEQ * L], F32, "A", ("F2",))
    rows_sc = scratch("rows_sc", [4, NSEQ * L], F32, "A", ("F1", "F2"))
    attn_sc = scratch("attn_sc", [D, NSEQ * L], F32, "AT", ("F2",))
    cols_sc = scratch("cols_sc", [CF, NSEQ * L], BF16, "F1", ("F2",))

    taps = {}

    def tap(name, src_ap, shape, dt=F32, reads=()):
        if name not in _DEBUG_TAPS:
            return
        out = dramio("dbg_" + name, shape, dt, "ExternalOutput")
        P.dma("sync", out[:, :], src_ap, reads=reads, stream="dbg")

    pb = [psum(f"pb{i}", [128, 512]) for i in range(8)]

    ident = sb("ident", [128, 128])
    V("gpsimd", lambda e: e.memset(ident[:, :], 0.0), writes=[ident])
    V("gpsimd", lambda e: e.affine_select(out=ident[:, :], in_=ident[:, :],
                                          compare_op=ALU.not_equal, fill=1.0, base=0,
                                          pattern=[[-1, 128]], channel_multiplier=1),
      reads=[ident], writes=[ident])
    ones_row_bf = sb("ones_row_bf", [1, 128], BF16)
    V("gpsimd", lambda e: e.memset(ones_row_bf[:, :], 1.0), writes=[ones_row_bf])
    onescol_bf = sb("onescol_bf", [128, 1], BF16)
    V("gpsimd", lambda e: e.memset(onescol_bf[:, :], 1.0), writes=[onescol_bf])
    selc = sb("selc", [16, C * 128], BF16)
    V("gpsimd", lambda e: e.memset(selc[:, :], 1.0), writes=[selc])
    for c in range(C):
        V("gpsimd", lambda e, c=c: e.affine_select(
            out=selc[:, c * 128:(c + 1) * 128], in_=selc[:, c * 128:(c + 1) * 128],
            compare_op=ALU.is_equal, fill=0.0, base=-c, pattern=[[0, 128]],
            channel_multiplier=1), reads=[selc], writes=[selc])
    psel_a = sb("psel_a", [1, 128], BF16)
    psel_b = sb("psel_b", [1, 128], BF16)
    iotap = sb("iotap", [1, 128])
    V("gpsimd", lambda e: e.iota(iotap[:, :], [[1, 128]], channel_multiplier=0,
                                 allow_small_or_imprecise_dtypes=True), writes=[iotap])
    V("vector", lambda e: e.tensor_single_scalar(psel_a[:, :], iotap[:, :], 64.0, op=ALU.is_lt),
      reads=[iotap], writes=[psel_a])
    V("vector", lambda e: e.tensor_single_scalar(psel_b[:, :], iotap[:, :], 64.0, op=ALU.is_ge),
      reads=[iotap], writes=[psel_b])

    b1_fm = sb("b1_fm", [128, 4])
    pb_fm = sb("pb_fm", [128, 8])
    lng_fm = sb("lng_fm", [128, 8])
    lnb_fm = sb("lnb_fm", [128, 8])
    bcols_fm = sb("bcols_fm", [128, 32])
    gb2_t = sb("gb2_t", [C, 1])
    for dst, src, n in [(b1_fm, gb1_in, 4), (pb_fm, pb_in, 8),
                        (lng_fm, lng_in, 8), (lnb_fm, lnb_in, 8)]:
        P.dma("sync", dst[:, 0:n], src.rearrange("(n p) -> p n", p=128),
              writes=[dst], stream="smallw", allow_slow_non_contiguous=True)
    P.dma("sync", bcols_fm[:, :], bcols_in.rearrange("c (m p) -> p (c m)", p=128),
          writes=[bcols_fm], stream="smallw", allow_slow_non_contiguous=True)
    P.dma("sync", gb2_t[:, :], gb2_in.rearrange("(c u) -> c u", u=1),
          writes=[gb2_t], stream="smallw", allow_slow_non_contiguous=True)
    gb2h_t = sb("gb2h_t", [C, 1])
    V("vector", lambda e: e.tensor_scalar_mul(gb2h_t[:, :], gb2_t[:, :], 0.5),
      reads=[gb2_t], writes=[gb2h_t])

    bias_hk = sb("bias_hk", [128, H * NKC])
    negl2 = sb("negl2", [1, 1])
    V("gpsimd", lambda e: e.memset(negl2[:, :], -float(np.log(2.0) / 2.0)), writes=[negl2])
    eps_t = sb("eps_t", [1, 1])
    V("gpsimd", lambda e: e.memset(eps_t[:, :], float(LN_EPS)), writes=[eps_t])
    bias_tmp_ctx = ExitStack()
    slope_rep = bias_tmp_ctx.enter_context(nc.sbuf_tensor("slope_rep", [1, H * NKC], F32))
    c_row = bias_tmp_ctx.enter_context(nc.sbuf_tensor("c_row", [1, H * NKC], F32))
    kc_row = bias_tmp_ctx.enter_context(nc.sbuf_tensor("kc_row", [1, H * NKC], F32))
    iota_row = bias_tmp_ctx.enter_context(nc.sbuf_tensor("iota_row", [1, 128], F32))
    ones_row_f = bias_tmp_ctx.enter_context(nc.sbuf_tensor("ones_row_f", [1, 128], F32))
    V("gpsimd", lambda e: e.iota(iota_row[:, :], [[1, 128]], channel_multiplier=0,
                                 allow_small_or_imprecise_dtypes=True), writes=[iota_row])
    V("gpsimd", lambda e: e.memset(ones_row_f[:, :], 1.0), writes=[ones_row_f])
    V("gpsimd", lambda e: e.iota(slope_rep[:, :], [[1, H], [0, NKC]], channel_multiplier=0,
                                 allow_small_or_imprecise_dtypes=True), writes=[slope_rep],
      tag="sl-iota")
    V("scalar", lambda e: e.activation(slope_rep[:, :], slope_rep[:, :], AF.Exp,
                                       scale=-float(np.log(2.0) / 2.0), bias=negl2[:, :]),
      reads=[slope_rep, negl2], writes=[slope_rep])
    V("gpsimd", lambda e: e.iota(kc_row[:, :], [[0, H], [128, NKC]], channel_multiplier=0,
                                 allow_small_or_imprecise_dtypes=True), writes=[kc_row])
    V("vector", lambda e: e.tensor_scalar_add(c_row[:, :], kc_row[:, :], -2047.0),
      reads=[kc_row], writes=[c_row])
    V("vector", lambda e: e.tensor_tensor(c_row[:, :], c_row[:, :], slope_rep[:, :], ALU.mult),
      reads=[c_row, slope_rep], writes=[c_row])
    V("vector", lambda e: e.tensor_scalar_add(c_row[:, :], c_row[:, :], -float(EXP_MARGIN)),
      reads=[c_row], writes=[c_row])
    V("tensor", lambda e: e.matmul(pb[0][:, 0:H * NKC], iota_row[:, :], slope_rep[:, :],
                                   start=True, stop=False),
      reads=[iota_row, slope_rep], writes=[pb[0]])
    V("tensor", lambda e: e.matmul(pb[0][:, 0:H * NKC], ones_row_f[:, :], c_row[:, :],
                                   start=False, stop=True),
      reads=[ones_row_f, c_row], writes=[pb[0]])
    V("vector", lambda e: e.tensor_copy(bias_hk[:, :], pb[0][:, 0:H * NKC]),
      reads=[pb[0]], writes=[bias_hk])
    P.barrier()
    bias_tmp_ctx.close()

    ce_fm = sb("ce_fm", [128, DCH * C], BF16)
    stage_rows = sb("stage_rows", [1, 4 * TILE])
    rhi = sb("rhi", [1, 512], BF16)
    rlo = sb("rlo", [1, 512], BF16)
    rtmp = sb("rtmp", [1, 512])
    invf = sb("invf", [128, 1])
    ropes_ctx = ExitStack()
    cos_d = ropes_ctx.enter_context(nc.sbuf_tensor("cos_d", [128, L], F32))
    sin_s = ropes_ctx.enter_context(nc.sbuf_tensor("sin_s", [128, L], F32))
    tmp_ctx = ExitStack()
    angbuf = tmp_ctx.enter_context(nc.sbuf_tensor("angbuf", [128, L], F32))
    rrbuf = tmp_ctx.enter_context(nc.sbuf_tensor("rrbuf", [128, L], F32))
    kfbuf = tmp_ctx.enter_context(nc.sbuf_tensor("kfbuf", [128, L], F32))
    kibuf = tmp_ctx.enter_context(nc.sbuf_tensor("kibuf", [128, L], I32))
    for p0 in (0, 32, 64, 96):
        V("gpsimd", lambda e, p0=p0: e.iota(invf[p0:p0 + 32, :], [[0, 1]],
                                            channel_multiplier=1,
                                            allow_small_or_imprecise_dtypes=True),
          writes=[invf])
    V("scalar", lambda e: e.activation(invf[:, :], invf[:, :], AF.Exp,
                                       scale=-float(np.log(ROPE_BASE) / 32.0)),
      reads=[invf], writes=[invf])
    V("gpsimd", lambda e: e.iota(angbuf[:, :], [[1, L]], channel_multiplier=0,
                                 allow_small_or_imprecise_dtypes=True), writes=[angbuf])
    V("vector", lambda e: e.tensor_scalar_mul(angbuf[:, :], angbuf[:, :], invf[:, :]),
      reads=[angbuf, invf], writes=[angbuf])
    TWO_PI = 2 * np.pi
    HIPI = float(np.float32(6.28125))
    LOPI = float(TWO_PI - 6.28125)

    def range_reduce(dst, shift):
        V("vector", lambda e: e.tensor_scalar(rrbuf[:, :], angbuf[:, :], float(shift),
                                              float(1.0 / TWO_PI), op0=ALU.add, op1=ALU.mult),
          reads=[angbuf], writes=[rrbuf])
        V("vector", lambda e: e.tensor_copy(kibuf[:, :], rrbuf[:, :]), reads=[rrbuf], writes=[kibuf])
        V("vector", lambda e: e.tensor_copy(kfbuf[:, :], kibuf[:, :]), reads=[kibuf], writes=[kfbuf])
        V("vector", lambda e: e.tensor_tensor(rrbuf[:, :], rrbuf[:, :], kfbuf[:, :], ALU.subtract),
          reads=[rrbuf, kfbuf], writes=[rrbuf])
        V("vector", lambda e: e.tensor_single_scalar(rrbuf[:, :], rrbuf[:, :], 0.5, op=ALU.is_ge),
          reads=[rrbuf], writes=[rrbuf])
        V("vector", lambda e: e.tensor_tensor(kfbuf[:, :], kfbuf[:, :], rrbuf[:, :], ALU.add),
          reads=[kfbuf, rrbuf], writes=[kfbuf])
        V("vector", lambda e: e.tensor_scalar(rrbuf[:, :], angbuf[:, :], float(shift), 1.0,
                                              op0=ALU.add, op1=ALU.bypass),
          reads=[angbuf], writes=[rrbuf])
        V("vector", lambda e: e.scalar_tensor_tensor(rrbuf[:, :], kfbuf[:, :], -HIPI, rrbuf[:, :],
                                                     op0=ALU.mult, op1=ALU.add),
          reads=[kfbuf, rrbuf], writes=[rrbuf])
        V("vector", lambda e: e.scalar_tensor_tensor(dst[:, :], kfbuf[:, :], -LOPI, rrbuf[:, :],
                                                     op0=ALU.mult, op1=ALU.add),
          reads=[kfbuf, rrbuf], writes=[dst])
        V("scalar", lambda e: e.activation(dst[:, :], dst[:, :], AF.Sin),
          reads=[dst], writes=[dst])

    range_reduce(sin_s, 0.0)
    range_reduce(cos_d, np.pi / 2)
    for p0 in (0, 64):
        V("vector", lambda e, p0=p0: e.tensor_scalar_mul(sin_s[p0:p0 + 32, :],
                                                         sin_s[p0:p0 + 32, :], -1.0),
          reads=[sin_s], writes=[sin_s])

    ce_tm = tmp_ctx.enter_context(nc.sbuf_tensor("ce_tm", [C, D], F32))
    ce_sq = tmp_ctx.enter_context(nc.sbuf_tensor("ce_sq", [C, D], F32))
    ce_nrm = tmp_ctx.enter_context(nc.sbuf_tensor("ce_nrm", [C, 1], F32))
    ce_nf = tmp_ctx.enter_context(nc.sbuf_tensor("ce_nf", [C, D], F32))
    P.dma("sync", ce_tm[:, :], ce_in[:, :], writes=[ce_tm], stream="smallw")
    V("scalar", lambda e: e.activation(ce_sq[:, :], ce_tm[:, :], AF.Square),
      reads=[ce_tm], writes=[ce_sq])
    V("vector", lambda e: e.reduce_sum(ce_nrm[:, :], ce_sq[:, :], axis=AX.X),
      reads=[ce_sq], writes=[ce_nrm])
    V("scalar", lambda e: e.activation(ce_nrm[:, :], ce_nrm[:, :], AF.Sqrt),
      reads=[ce_nrm], writes=[ce_nrm])
    V("vector", lambda e: e.tensor_single_scalar(ce_nrm[:, :], ce_nrm[:, :], 1e-12, op=ALU.max),
      reads=[ce_nrm], writes=[ce_nrm])
    V("vector", lambda e: e.reciprocal(ce_nrm[:, :], ce_nrm[:, :]),
      reads=[ce_nrm], writes=[ce_nrm])
    V("vector", lambda e: e.tensor_scalar_mul(ce_nf[:, :], ce_tm[:, :], ce_nrm[:, :]),
      reads=[ce_tm, ce_nrm], writes=[ce_nf])
    for j in range(DCH):
        pbt = pb[1 + j // 4]
        V("tensor", lambda e, j=j, pbt=pbt: e.transpose(
            pbt[:, (j % 4) * 128: (j % 4) * 128 + C],
            ce_nf[:, j * 128:(j + 1) * 128], ident[0:C, 0:C]),
          reads=[ce_nf, ident], writes=[pbt])
    for j in range(DCH):
        pbt = pb[1 + j // 4]
        V("vector", lambda e, j=j, pbt=pbt: e.tensor_copy(
            ce_fm[:, j * C:(j + 1) * C], pbt[:, (j % 4) * 128: (j % 4) * 128 + C]),
          reads=[pbt], writes=[ce_fm])

    P.barrier()
    tmp_ctx.close()

    def row_hilo(src_ap, n, src_keys):
        V("vector", lambda e: e.tensor_copy(rhi[:, 0:n], src_ap), reads=src_keys, writes=[rhi])
        V("vector", lambda e: e.tensor_copy(rtmp[:, 0:n], rhi[:, 0:n]), reads=[rhi], writes=[rtmp])
        V("vector", lambda e: e.tensor_tensor(rlo[:, 0:n], src_ap, rtmp[:, 0:n], ALU.subtract),
          reads=list(src_keys) + [rtmp], writes=[rlo])
        return rhi[:, 0:n], rlo[:, 0:n]

    if "A" in phases:
        kv_ctx = ExitStack()
        kdup = [kv_ctx.enter_context(nc.sbuf_tensor(f"kdup{s}", [128, KV * L], BF16))
                for s in range(NSEQ)]
        VROW = KV * 72
        v4 = [kv_ctx.enter_context(nc.sbuf_tensor(f"v4{s}", [128, NKC * VROW], BF16))
              for s in range(NSEQ)]
        actx = ExitStack()
        x_tm = actx.enter_context(nc.sbuf_tensor("x_tm", [128, 4 * D], F32))
        st6 = actx.enter_context(nc.sbuf_tensor("st6", [128, 12], F32))
        st2 = actx.enter_context(nc.sbuf_tensor("st2", [128, 2], F32))
        stq = actx.enter_context(nc.sbuf_tensor("stq", [128, 4], F32))
        sttmp = actx.enter_context(nc.sbuf_tensor("sttmp", [128, 1], F32))
        x_fm = actx.enter_context(nc.sbuf_tensor("x_fm", [128, DCH * TILE], F32))
        xn32_t = actx.enter_context(nc.sbuf_tensor("xn32_t", [128, DCH * TILE], F32))
        xnb_t = actx.enter_context(nc.sbuf_tensor("xnb_t", [128, DCH * TILE], BF16))
        wk_sb = actx.enter_context(nc.sbuf_tensor("wk_sb", [128, DCH * 256], BF16))
        wv_sb = actx.enter_context(nc.sbuf_tensor("wv_sb", [128, DCH * 256], BF16))
        kraw = actx.enter_context(nc.sbuf_tensor("kraw", [128, TILE], F32))
        kc1 = actx.enter_context(nc.sbuf_tensor("kc1", [128, TILE], F32))
        kc2 = actx.enter_context(nc.sbuf_tensor("kc2", [128, TILE], F32))

        for dc in range(DCH):
            P.dma("gpsimd", wk_sb[:, dc * 256:(dc + 1) * 256],
                  wk_in[dc * 128:(dc + 1) * 128, :], writes=[wk_sb], stream="wgt")
            P.dma("gpsimd", wv_sb[:, dc * 256:(dc + 1) * 256],
                  wv_in[dc * 128:(dc + 1) * 128, :], writes=[wv_sb], stream="wgt")

        for s in range(NSEQ):
            V("gpsimd", lambda e, s=s: e.memset(v4[s][:, :], 1.0), writes=[v4[s]])
            for it in range(NTILE):
                t0 = it * TILE
                for u in range(4):
                    P.dma("sync", x_tm[:, u * D:(u + 1) * D],
                          x_in[s * L + t0 + u * 128: s * L + t0 + (u + 1) * 128, :],
                          writes=[x_tm], stream="x")
                for u in range(4):
                    sub = it * 4 + u
                    xv = x_tm[:, u * D:(u + 1) * D]
                    V("vector", lambda e, xv=xv: e.bn_stats(st6[:, 0:6], xv[:, 0:512]),
                      reads=[x_tm], writes=[st6])
                    V("vector", lambda e, xv=xv: e.bn_stats(st6[:, 6:12], xv[:, 512:1024]),
                      reads=[x_tm], writes=[st6])
                    V("vector", lambda e: e.bn_aggr(st2[:, :], st6[:, :]),
                      reads=[st6], writes=[st2])
                    V("vector", lambda e: e.tensor_copy(stq[:, 0:1], st2[:, 0:1]),
                      reads=[st2], writes=[stq])
                    V("vector", lambda e: e.tensor_scalar_add(sttmp[:, :], st2[:, 1:2],
                                                              float(LN_EPS)),
                      reads=[st2], writes=[sttmp])
                    V("scalar", lambda e: e.activation(stq[:, 2:3], sttmp[:, :], AF.Sqrt),
                      reads=[sttmp], writes=[stq])
                    V("vector", lambda e: e.reciprocal(stq[:, 1:2], stq[:, 2:3]),
                      reads=[stq], writes=[stq])
                    V("scalar", lambda e: e.activation(sttmp[:, :], st2[:, 1:2], AF.Sqrt,
                                                       scale=float(D)),
                      reads=[st2], writes=[sttmp])
                    V("vector", lambda e: e.tensor_tensor(sttmp[:, :], sttmp[:, :], stq[:, 1:2],
                                                          ALU.mult),
                      reads=[sttmp, stq], writes=[sttmp])
                    V("vector", lambda e: e.tensor_single_scalar(sttmp[:, :], sttmp[:, :], 1e-12,
                                                                 op=ALU.max),
                      reads=[sttmp], writes=[sttmp])
                    V("vector", lambda e: e.reciprocal(stq[:, 3:4], sttmp[:, :]),
                      reads=[sttmp], writes=[stq])
                    for q in range(4):
                        V("tensor", lambda e, q=q: e.transpose(
                            pb[2][0:1, q * 128:(q + 1) * 128],
                            stq[:, q:q + 1], ident[:, :]),
                          reads=[stq, ident], writes=[pb[2]], tag="stT")
                    for q in range(4):
                        V("vector", lambda e, q=q, u=u: e.tensor_copy(
                            stage_rows[:, q * TILE + u * 128:q * TILE + (u + 1) * 128],
                            pb[2][0:1, q * 128:(q + 1) * 128]),
                          reads=[pb[2]], writes=[stage_rows])
                    for j in range(DCH):
                        pbt = pb[0] if j < 4 else pb[1]
                        V("tensor", lambda e, j=j, xv=xv, pbt=pbt: e.transpose(
                            pbt[:, (j % 4) * 128:(j % 4 + 1) * 128],
                            xv[:, j * 128:(j + 1) * 128], ident[:, :]),
                          reads=[x_tm, ident], writes=[pbt], tag="xT")
                    for j in range(DCH):
                        pbt = pb[0] if j < 4 else pb[1]
                        V("scalar", lambda e, j=j, u=u, pbt=pbt: e.copy(
                            x_fm[:, j * TILE + u * 128:j * TILE + (u + 1) * 128],
                            pbt[:, (j % 4) * 128:(j % 4 + 1) * 128]),
                          reads=[pbt], writes=[x_fm])
                for z in (0, 1):
                    hi, lo = row_hilo(stage_rows[:, z * TILE:(z + 1) * TILE], TILE, [stage_rows])
                    V("tensor", lambda e, z=z, hi=hi: e.matmul(
                        pb[4 + z][:, :], ones_row_bf[:, :], hi,
                        start=True, stop=False), reads=[ones_row_bf, rhi], writes=[pb[4 + z]])
                    V("tensor", lambda e, z=z, lo=lo: e.matmul(
                        pb[4 + z][:, :], ones_row_bf[:, :], lo,
                        start=False, stop=True), reads=[ones_row_bf, rlo], writes=[pb[4 + z]])
                for j in range(DCH):
                    xfj = x_fm[:, j * TILE:(j + 1) * TILE]
                    V("vector", lambda e, xfj=xfj, j=j: e.scalar_tensor_tensor(
                        xn32_t[:, j * TILE:(j + 1) * TILE], xfj, 1.0, pb[4][:, :],
                        op0=ALU.mult, op1=ALU.subtract),
                      reads=[x_fm, pb[4]], writes=[xn32_t])
                    V("vector", lambda e, j=j: e.tensor_tensor(
                        xn32_t[:, j * TILE:(j + 1) * TILE],
                        xn32_t[:, j * TILE:(j + 1) * TILE], pb[5][:, :], ALU.mult),
                      reads=[xn32_t, pb[5]], writes=[xn32_t])
                    V("vector", lambda e, j=j: e.tensor_copy(
                        xnb_t[:, j * TILE:(j + 1) * TILE], xn32_t[:, j * TILE:(j + 1) * TILE]),
                      reads=[xn32_t], writes=[xnb_t])
                for j in range(DCH):
                    P.dma("scalar", xnb_sc[j * 128:(j + 1) * 128, s * L + t0:s * L + t0 + TILE],
                          xnb_t[:, j * TILE:(j + 1) * TILE], reads=[xnb_t],
                          writes=[xnb_sc], stream="xnw")
                    P.dma("scalar", xn32_sc[j * 128:(j + 1) * 128, s * L + t0:s * L + t0 + TILE],
                          xn32_t[:, j * TILE:(j + 1) * TILE], reads=[xn32_t],
                          writes=[xn32_sc], stream="xnw")
                for ri in range(4):
                    P.dma("scalar", rows_sc[ri:ri + 1, s * L + t0:s * L + t0 + TILE],
                          stage_rows[:, ri * TILE:(ri + 1) * TILE], reads=[stage_rows],
                          writes=[rows_sc], stream="xnw")
                for kp in range(2):
                    kpb = pb[6 + kp]
                    for j in range(DCH):
                        V("tensor", lambda e, j=j, kp=kp, kpb=kpb: e.matmul(
                            kpb[:, :],
                            wk_sb[:, j * 256 + kp * 128: j * 256 + (kp + 1) * 128],
                            xnb_t[:, j * TILE:(j + 1) * TILE],
                            start=(j == 0), stop=(j == DCH - 1)),
                          reads=[wk_sb, xnb_t], writes=[kpb], tag="kmm")
                    V("vector", lambda e, kpb=kpb, t0=t0: e.tensor_tensor(
                        kc1[:, :], kpb[:, :], cos_d[:, t0:t0 + TILE], ALU.mult),
                      reads=[kpb, cos_d], writes=[kc1])
                    for p0, rp0 in ((0, 32), (32, 0), (64, 96), (96, 64)):
                        V("vector", lambda e, p0=p0, rp0=rp0, t0=t0, kpb=kpb: e.tensor_tensor(
                            kc2[p0:p0 + 32, :],
                            kpb[rp0:rp0 + 32, :],
                            sin_s[p0:p0 + 32, t0:t0 + TILE], ALU.mult),
                          reads=[kpb, sin_s], writes=[kc2])
                    V("vector", lambda e: e.tensor_tensor(kraw[:, :], kc1[:, :], kc2[:, :],
                                                          ALU.add),
                      reads=[kc1, kc2], writes=[kraw])
                    for half, kvi in ((0, 2 * kp), (64, 2 * kp + 1)):
                        for dst0 in (0, 64):
                            V("vector", lambda e, half=half, kvi=kvi, dst0=dst0, s=s, t0=t0:
                              e.tensor_copy(
                                  kdup[s][dst0:dst0 + 64, kvi * L + t0:kvi * L + t0 + TILE],
                                  kraw[half:half + 64, :]),
                              reads=[kraw], writes=[kdup[s]])
                for u in range(4):
                    sub = it * 4 + u
                    for j in range(DCH):
                        V("tensor", lambda e, j=j, u=u: e.matmul(
                            pb[3][:, 0:256],
                            xnb_t[:, j * TILE + u * 128: j * TILE + (u + 1) * 128],
                            wv_sb[:, j * 256:(j + 1) * 256],
                            start=(j == 0), stop=(j == DCH - 1)),
                          reads=[xnb_t, wv_sb], writes=[pb[3]], tag="vmm")
                    for kv in range(KV):
                        V("vector", lambda e, kv=kv, sub=sub, s=s: e.tensor_copy(
                            v4[s][:, sub * VROW + kv * 72: sub * VROW + kv * 72 + 64],
                            pb[3][:, kv * 64:(kv + 1) * 64]),
                          reads=[pb[3]], writes=[v4[s]])

        P.barrier()
        actx.close()
        a_ctx = kv_ctx
    else:
        a_ctx = None

    if "AT" in phases:
        if a_ctx is None:
            raise RuntimeError("AT phase requires A phase in the same build")
        QW = 512
        NQW = L // QW
        atx = ExitStack()
        wq_sb = atx.enter_context(nc.sbuf_tensor("wq_sb", [128, DCH * 1024], BF16))
        wo_sb = atx.enter_context(nc.sbuf_tensor("wo_sb", [128, DCH * 1024], BF16))
        xnb_at = atx.enter_context(nc.sbuf_tensor("xnb_at", [128, DCH * QW], BF16))
        qc1 = atx.enter_context(nc.sbuf_tensor("qc1", [128, QW], F32))
        qc2 = atx.enter_context(nc.sbuf_tensor("qc2", [128, QW], F32))
        qbf = [atx.enter_context(nc.sbuf_tensor(f"qbf{i}", [128, QW], BF16)) for i in range(2)]
        pbf = [atx.enter_context(nc.sbuf_tensor(f"pbf{i}", [128, QW], BF16)) for i in range(4)]
        accf = atx.enter_context(nc.sbuf_tensor("accf", [128, QW], F32))
        accbf = atx.enter_context(nc.sbuf_tensor("accbf", [128, NPAIR * QW], BF16))
        rd_a = atx.enter_context(nc.sbuf_tensor("rd_a", [1, QW], F32))
        rd_b = atx.enter_context(nc.sbuf_tensor("rd_b", [1, QW], F32))
        att_st = [atx.enter_context(nc.sbuf_tensor(f"att_st{i}", [128, QW], F32))
                  for i in range(3)]

        for dc in range(DCH):
            P.dma("gpsimd", wq_sb[:, dc * 1024:(dc + 1) * 1024],
                  wq_in[dc * 128:(dc + 1) * 128, :], writes=[wq_sb], stream="wgt")
            P.dma("gpsimd", wo_sb[:, dc * 1024:(dc + 1) * 1024],
                  wo_in[dc * 128:(dc + 1) * 128, :], writes=[wo_sb], stream="wgt")

        for s in range(NSEQ):
            for qt in range(NQW):
                q0 = qt * QW
                for j in range(DCH):
                    P.dma("sync", xnb_at[:, j * QW:(j + 1) * QW],
                          xnb_sc[j * 128:(j + 1) * 128, s * L + q0:s * L + q0 + QW],
                          reads=[xnb_sc], writes=[xnb_at], stream="xnr")

                def stage_q(pr, pi, q0=q0):
                    qb = qbf[pi % 2]
                    qps = pb[4 + 2 * (pi % 2)]
                    for j in range(DCH):
                        V("tensor", lambda e, j=j, pr=pr, qps=qps: e.matmul(
                            qps[:, :],
                            wq_sb[:, j * 1024 + pr * 128: j * 1024 + (pr + 1) * 128],
                            xnb_at[:, j * QW:(j + 1) * QW],
                            start=(j == 0), stop=(j == DCH - 1)),
                          reads=[wq_sb, xnb_at], writes=[qps], tag="qmm")
                    V("vector", lambda e, qps=qps, q0=q0: e.tensor_tensor(
                        qc1[:, :], qps[:, :], cos_d[:, q0:q0 + QW], ALU.mult),
                      reads=[qps, cos_d], writes=[qc1])
                    for p0, rp0 in ((0, 32), (32, 0), (64, 96), (96, 64)):
                        V("vector", lambda e, p0=p0, rp0=rp0, qps=qps, q0=q0: e.tensor_tensor(
                            qc2[p0:p0 + 32, :], qps[rp0:rp0 + 32, :],
                            sin_s[p0:p0 + 32, q0:q0 + QW], ALU.mult),
                          reads=[qps, sin_s], writes=[qc2])
                    V("vector", lambda e, qb=qb: e.tensor_tensor(qb[:, :], qc1[:, :],
                                                                 qc2[:, :], ALU.add),
                      reads=[qc1, qc2], writes=[qb])

                def emit_pv(job, s=s):
                    kc, kv, kc0, pa, pb_, acc_a, acc_b = job
                    for acc_i, psrc in ((acc_a, pa), (acc_b, pb_)):
                        V("tensor",
                          lambda e, kc=kc, kv=kv, psrc=psrc, acc_i=acc_i, kc0=kc0, s=s:
                          e.matmul(
                            pb[acc_i][0:65, :],
                            v4[s][:, kc * VROW + kv * 72: kc * VROW + kv * 72 + 65],
                            psrc[:, :],
                            start=(kc == kc0), stop=(kc == NKC - 1)),
                          reads=[v4[s], psrc], writes=[pb[acc_i]], tag="pv")

                pair_order = [3, 0, 4, 1, 5, 2, 6, 7]
                stage_q(pair_order[0], 0)
                pending_finish = None

                def mk_finish(pr, pi):
                    sa = 4 + 2 * (pi % 2)
                    sb = 5 + 2 * (pi % 2)
                    nrm = pb[5 + 2 * ((pi + 1) % 2)]

                    def fin():
                        first = True
                        for selp, rowt in ((psel_a, rd_a), (psel_b, rd_b)):
                            hi, lo = row_hilo(rowt[:, :], QW, [rowt])
                            V("tensor", lambda e, selp=selp, hi=hi, first=first, nrm=nrm:
                              e.matmul(nrm[:, :], selp[:, :], hi, start=first, stop=False),
                              reads=[selp, rhi], writes=[nrm])
                            V("tensor", lambda e, selp=selp, lo=lo, nrm=nrm: e.matmul(
                                nrm[:, :], selp[:, :], lo, start=False,
                                stop=(selp is psel_b)),
                              reads=[selp, rlo], writes=[nrm])
                            first = False
                        V("vector", lambda e, sa=sa: e.tensor_copy(accf[0:64, :],
                                                                   pb[sa][0:64, :]),
                          reads=[pb[sa]], writes=[accf])
                        V("vector", lambda e, sb=sb: e.tensor_copy(accf[64:128, :],
                                                                   pb[sb][0:64, :]),
                          reads=[pb[sb]], writes=[accf])
                        V("vector", lambda e, pr=pr, nrm=nrm: e.tensor_tensor(
                            accbf[:, pr * QW:(pr + 1) * QW], accf[:, :], nrm[:, :],
                            ALU.mult),
                          reads=[accf, nrm], writes=[accbf], tag="accn")
                    return fin

                for pi in range(NPAIR):
                    pr = pair_order[pi]
                    qb = qbf[pi % 2]
                    acc_a = 4 + 2 * (pi % 2)
                    acc_b = 5 + 2 * (pi % 2)
                    kv = (2 * pr) // GROUPS
                    slope_lo = float(2.0 ** (-(2 * pr + 2) / 2.0))
                    thr = (L - 128 - 160.0 / slope_lo) / 128.0
                    kc0 = max(0, min(NKC - 1, int(np.floor(thr)) + 1)) if thr > 0 else 0
                    pending = None
                    nq_emitted = False
                    for kc in range(kc0, NKC):
                        par = (kc - kc0) % 2
                        sA = pb[par * 2 + 0]
                        sB = pb[par * 2 + 1]
                        pa = pbf[par * 2 + 0]
                        pb_ = pbf[par * 2 + 1]
                        for ti, (r0, c0) in enumerate(((0, 0), (64, 0), (0, 64), (64, 64))):
                            dst = (sA if r0 == 0 else sB)
                            V("tensor",
                              lambda e, r0=r0, c0=c0, kc=kc, kv=kv, qb=qb, dst=dst, s=s:
                              e.matmul(
                                  dst[c0:c0 + 64, :],
                                  kdup[s][r0:r0 + 64,
                                          kv * L + kc * 128 + c0: kv * L + kc * 128 + c0 + 64],
                                  qb[r0:r0 + 64, :],
                                  start=True, stop=True, tile_position=(r0, c0)),
                              reads=[kdup[s], qb], writes=[dst], tag="smm")
                        if pending_finish is not None:
                            pending_finish()
                            pending_finish = None
                        if pending is not None:
                            emit_pv(pending)
                        V("scalar", lambda e, pa=pa, kc=kc, pr=pr, sA=sA: e.activation(
                            pa[:, :], sA[:, :], AF.Exp,
                            bias=bias_hk[:, (2 * pr) * NKC + kc:(2 * pr) * NKC + kc + 1],
                            scale=0.125),
                          reads=[sA, bias_hk], writes=[pa], tag="exp")
                        V("scalar", lambda e, pb_=pb_, kc=kc, pr=pr, sB=sB: e.activation(
                            pb_[:, :], sB[:, :], AF.Exp,
                            bias=bias_hk[:, (2 * pr + 1) * NKC + kc:(2 * pr + 1) * NKC + kc + 1],
                            scale=0.125),
                          reads=[sB, bias_hk], writes=[pb_], tag="exp")
                        pending = (kc, kv, kc0, pa, pb_, acc_a, acc_b)
                        if not nq_emitted and kc >= kc0 + 1 and pi + 1 < NPAIR:
                            stage_q(pair_order[pi + 1], pi + 1)
                            nq_emitted = True
                    if not nq_emitted and pi + 1 < NPAIR:
                        stage_q(pair_order[pi + 1], pi + 1)
                    emit_pv(pending)
                    V("scalar", lambda e, acc_a=acc_a: e.activation(
                        rd_a[:, :], pb[acc_a][64:65, :], AF.Ln),
                      reads=[pb[acc_a]], writes=[rd_a])
                    V("scalar", lambda e: e.activation(rd_a[:, :], rd_a[:, :], AF.Exp,
                                                       scale=-1.0),
                      reads=[rd_a], writes=[rd_a])
                    V("scalar", lambda e, acc_b=acc_b: e.activation(
                        rd_b[:, :], pb[acc_b][64:65, :], AF.Ln),
                      reads=[pb[acc_b]], writes=[rd_b])
                    V("scalar", lambda e: e.activation(rd_b[:, :], rd_b[:, :], AF.Exp,
                                                       scale=-1.0),
                      reads=[rd_b], writes=[rd_b])
                    pending_finish = mk_finish(pr, pi)
                if pending_finish is not None:
                    pending_finish()
                    pending_finish = None
                for dc in range(DCH):
                    dst = pb[4 + dc % 2]
                    stg = att_st[dc % 3]
                    for j in range(DCH):
                        V("tensor", lambda e, dc=dc, j=j, dst=dst: e.matmul(
                            dst[:, :],
                            wo_sb[:, j * 1024 + dc * 128: j * 1024 + (dc + 1) * 128],
                            accbf[:, j * QW:(j + 1) * QW],
                            start=(j == 0), stop=(j == DCH - 1)),
                          reads=[wo_sb, accbf], writes=[dst], tag="womm")
                    V("vector", lambda e, dst=dst, stg=stg: e.tensor_copy(stg[:, :], dst[:, :]),
                      reads=[dst], writes=[stg])
                    P.dma("sync", attn_sc[dc * 128:(dc + 1) * 128, s * L + q0:s * L + q0 + QW],
                          stg[:, :], reads=[stg], writes=[attn_sc], stream="atw")
        P.barrier()
        atx.close()
        if a_ctx is not None:
            a_ctx.close()
        ropes_ctx.close()
    elif a_ctx is not None:
        P.barrier()
        a_ctx.close()
        ropes_ctx.close()
    else:
        P.barrier()
        ropes_ctx.close()

    if "F1" in phases:
        fctx = ExitStack()
        wc_sb = [fctx.enter_context(nc.sbuf_tensor(f"wc_sb{dc}", [128, CF], BF16))
                 for dc in range(DCH)]
        gw1_sb = fctx.enter_context(nc.sbuf_tensor("gw1_sb", [128, DCH * 512], BF16))
        gw2_sb = fctx.enter_context(nc.sbuf_tensor("gw2_sb", [128, 4 * 16], BF16))
        xnb_f = [fctx.enter_context(nc.sbuf_tensor(f"xnb_f{i}", [128, DCH * TILE], BF16))
                 for i in range(2)]
        h_bf = fctx.enter_context(nc.sbuf_tensor("h_bf", [128, 4 * TILE], BF16))
        g2s = fctx.enter_context(nc.sbuf_tensor("g2s", [C, TILE], F32))
        invnb = fctx.enter_context(nc.sbuf_tensor("invnb", [C, TILE], F32))
        logit_f = fctx.enter_context(nc.sbuf_tensor("logit_f", [C, TILE], F32))
        logit_t = fctx.enter_context(nc.sbuf_tensor("logit_t", [128, C], F32))
        negm = fctx.enter_context(nc.sbuf_tensor("negm", [128, 1], F32))
        maskt = fctx.enter_context(nc.sbuf_tensor("maskt", [128, C], F32))
        eet = fctx.enter_context(nc.sbuf_tensor("eet", [128, C], F32))
        ssum = fctx.enter_context(nc.sbuf_tensor("ssum", [128, 1], F32))
        rsum = fctx.enter_context(nc.sbuf_tensor("rsum", [128, 1], F32))
        w_tm = fctx.enter_context(nc.sbuf_tensor("w_tm", [128, C], F32))
        w_hi = [fctx.enter_context(nc.sbuf_tensor(f"w_hi{i}", [C, TILE], BF16))
                for i in range(2)]
        w_lo = [fctx.enter_context(nc.sbuf_tensor(f"w_lo{i}", [C, TILE], BF16))
                for i in range(2)]
        w_f32 = fctx.enter_context(nc.sbuf_tensor("w_f32", [C, TILE], F32))
        w_tmf = fctx.enter_context(nc.sbuf_tensor("w_tmf", [C, TILE], F32))
        colsg_ring = [fctx.enter_context(nc.sbuf_tensor(f"colsg{i}", [128, TILE], BF16))
                      for i in range(4)]
        wbs = [fctx.enter_context(nc.sbuf_tensor(f"wbs{i}", [128, TILE], F32))
               for i in range(2)]
        irow = fctx.enter_context(nc.sbuf_tensor("irow", [1, TILE], F32))

        for dc in range(DCH):
            for c in range(C):
                P.dma("gpsimd", wc_sb[dc][:, c * NN:(c + 1) * NN],
                      wcols_in[c, dc * 128:(dc + 1) * 128, :], writes=[wc_sb[dc]], stream="wgt")
            P.dma("gpsimd", gw1_sb[:, dc * 512:(dc + 1) * 512],
                  gw1_in[dc * 128:(dc + 1) * 128, :], writes=[gw1_sb], stream="wgt")
        for hf in range(4):
            P.dma("gpsimd", gw2_sb[:, hf * 16:(hf + 1) * 16],
                  gw2_in[hf * 128:(hf + 1) * 128, :], writes=[gw2_sb], stream="wgt")

        def f1_gate(s, it):
            t0 = it * TILE
            bi = it % 2
            xb = xnb_f[bi]
            P.dma("sync", stage_rows[:, 3 * TILE:4 * TILE],
                  rows_sc[3:4, s * L + t0:s * L + t0 + TILE],
                  reads=[rows_sc], writes=[stage_rows], stream="xnr")
            for j in range(DCH):
                P.dma("sync", xb[:, j * TILE:(j + 1) * TILE],
                      xnb_sc[j * 128:(j + 1) * 128, s * L + t0:s * L + t0 + TILE],
                      reads=[xnb_sc], writes=[xb], stream="xnr")
            for hf in range(4):
                hpb = pb[4 + hf % 2]
                for j in range(DCH):
                    V("tensor", lambda e, hf=hf, j=j, hpb=hpb, xb=xb: e.matmul(
                        hpb[:, :], gw1_sb[:, j * 512 + hf * 128: j * 512 + (hf + 1) * 128],
                        xb[:, j * TILE:(j + 1) * TILE],
                        start=(j == 0), stop=(j == DCH - 1)),
                      reads=[gw1_sb, xb], writes=[hpb], tag="gmm")
                V("scalar", lambda e, hf=hf, hpb=hpb: e.activation(
                    h_bf[:, hf * TILE:(hf + 1) * TILE], hpb[:, :], AF.Gelu,
                    bias=b1_fm[:, hf:hf + 1]),
                  reads=[hpb, b1_fm], writes=[h_bf], tag="gelu1")
            for hf in range(4):
                V("tensor", lambda e, hf=hf: e.matmul(
                    pb[6][0:C, :], gw2_sb[:, hf * 16:(hf + 1) * 16],
                    h_bf[:, hf * TILE:(hf + 1) * TILE],
                    start=(hf == 0), stop=(hf == 3)),
                  reads=[gw2_sb, h_bf], writes=[pb[6]], tag="g2mm")
            V("scalar", lambda e: e.activation(g2s[:, :], pb[6][0:C, :], AF.Tanh,
                                               scale=0.5, bias=gb2h_t[:, :]),
              reads=[pb[6], gb2h_t], writes=[g2s])
            V("vector", lambda e: e.tensor_scalar(g2s[:, :], g2s[:, :], 0.5, 0.5,
                                                  op0=ALU.mult, op1=ALU.add),
              reads=[g2s], writes=[g2s])
            for j in range(DCH):
                V("tensor", lambda e, j=j, xb=xb: e.matmul(
                    pb[7][0:C, :], ce_fm[:, j * C:(j + 1) * C],
                    xb[:, j * TILE:(j + 1) * TILE],
                    start=(j == 0), stop=(j == DCH - 1)),
                  reads=[ce_fm, xb], writes=[pb[7]], tag="simm")
            hi, lo = row_hilo(stage_rows[:, 3 * TILE:4 * TILE], TILE, [stage_rows])
            V("tensor", lambda e, hi=hi: e.matmul(pb[4][0:C, :], ones_row_bf[:, 0:C], hi,
                                                  start=True, stop=False),
              reads=[ones_row_bf, rhi], writes=[pb[4]])
            V("tensor", lambda e, lo=lo: e.matmul(pb[4][0:C, :], ones_row_bf[:, 0:C], lo,
                                                  start=False, stop=True),
              reads=[ones_row_bf, rlo], writes=[pb[4]])
            V("vector", lambda e: e.tensor_copy(invnb[:, :], pb[4][0:C, :]),
              reads=[pb[4]], writes=[invnb])
            V("vector", lambda e: e.tensor_tensor(logit_f[:, :], pb[7][0:C, :],
                                                  invnb[:, :], ALU.mult),
              reads=[pb[7], invnb], writes=[logit_f])
            V("vector", lambda e: e.tensor_tensor(logit_f[:, :], logit_f[:, :], g2s[:, :],
                                                  ALU.add),
              reads=[logit_f, g2s], writes=[logit_f])
            for u in range(4):
                V("tensor", lambda e, u=u: e.transpose(
                    pb[5][:, 0:C], logit_f[:, u * 128:(u + 1) * 128], ident[0:C, 0:C]),
                  reads=[logit_f, ident], writes=[pb[5]], tag="lgT")
                V("vector", lambda e: e.tensor_copy(logit_t[:, :], pb[5][:, 0:C]),
                  reads=[pb[5]], writes=[logit_t])
                V("vector", lambda e: e.reduce_max(negm[:, :], logit_t[:, :], axis=AX.X,
                                                   negate=True),
                  reads=[logit_t], writes=[negm])
                V("vector", lambda e: e.tensor_scalar(maskt[:, :], logit_t[:, :],
                                                      negm[:, :], None, op0=ALU.add),
                  reads=[logit_t, negm], writes=[maskt])
                V("vector", lambda e: e.tensor_single_scalar(maskt[:, :], maskt[:, :], 0.0,
                                                             op=ALU.is_ge),
                  reads=[maskt], writes=[maskt])
                V("scalar", lambda e: e.activation(eet[:, :], logit_t[:, :], AF.Exp,
                                                   bias=negm[:, :]),
                  reads=[logit_t, negm], writes=[eet])
                V("vector", lambda e: e.reduce_sum(ssum[:, :], eet[:, :], axis=AX.X),
                  reads=[eet], writes=[ssum])
                V("vector", lambda e: e.reciprocal(rsum[:, :], ssum[:, :]),
                  reads=[ssum], writes=[rsum])
                V("vector", lambda e: e.scalar_tensor_tensor(w_tm[:, :], eet[:, :],
                                                             rsum[:, :], maskt[:, :],
                                                             op0=ALU.mult, op1=ALU.mult),
                  reads=[eet, rsum, maskt], writes=[w_tm])
                V("tensor", lambda e, u=u: e.transpose(pb[5][0:C, 128:256],
                                                       w_tm[:, :], ident[:, :]),
                  reads=[w_tm, ident], writes=[pb[5]], tag="wT")
                V("vector", lambda e, u=u: e.tensor_copy(w_tmf[:, u * 128:(u + 1) * 128],
                                                         pb[5][0:C, 128:256]),
                  reads=[pb[5]], writes=[w_tmf])
            V("vector", lambda e, bi=bi: e.tensor_copy(w_hi[bi][:, :], w_tmf[:, :]),
              reads=[w_tmf], writes=[w_hi[bi]])
            V("vector", lambda e, bi=bi: e.tensor_copy(w_f32[:, :], w_hi[bi][:, :]),
              reads=[w_hi[bi]], writes=[w_f32])
            V("vector", lambda e: e.tensor_tensor(w_f32[:, :], w_tmf[:, :], w_f32[:, :],
                                                  ALU.subtract),
              reads=[w_tmf, w_f32], writes=[w_f32])
            V("vector", lambda e, bi=bi: e.tensor_copy(w_lo[bi][:, :], w_f32[:, :]),
              reads=[w_f32], writes=[w_lo[bi]])

        def f1_cols(s, it):
            t0 = it * TILE
            bi = it % 2
            xb = xnb_f[bi]
            for c in range(C):
                wpb = pb[2 + c % 2]
                wb = wbs[c % 2]
                V("tensor", lambda e, c=c, wpb=wpb, bi=bi: e.matmul(
                    wpb[:, :], selc[:, c * 128:(c + 1) * 128], w_hi[bi][:, :],
                    start=True, stop=False),
                  reads=[selc, w_hi[bi]], writes=[wpb])
                V("tensor", lambda e, c=c, wpb=wpb, bi=bi: e.matmul(
                    wpb[:, :], selc[:, c * 128:(c + 1) * 128], w_lo[bi][:, :],
                    start=False, stop=True),
                  reads=[selc, w_lo[bi]], writes=[wpb])
                V("vector", lambda e, wpb=wpb, wb=wb: e.tensor_copy(wb[:, :], wpb[:, :]),
                  reads=[wpb], writes=[wb], tag="wbev")
                for jc in (2 * c, 2 * c + 1):
                    cpb = pb[jc % 2]
                    for j in range(DCH):
                        V("tensor", lambda e, jc=jc, j=j, cpb=cpb, xb=xb: e.matmul(
                            cpb[:, :], wc_sb[j][:, jc * 128:(jc + 1) * 128],
                            xb[:, j * TILE:(j + 1) * TILE],
                            start=(j == 0), stop=(j == DCH - 1)),
                          reads=[wc_sb[j], xb], writes=[cpb], tag="cmm")
                    cg = colsg_ring[jc % 4]
                    V("scalar", lambda e, jc=jc, cpb=cpb, cg=cg: e.activation(
                        cg[:, :], cpb[:, :], AF.Gelu, bias=bcols_fm[:, jc:jc + 1]),
                      reads=[cpb, bcols_fm], writes=[cg], tag="gelu2")
                    V("vector", lambda e, cg=cg, wb=wb: e.tensor_tensor(
                        cg[:, :], cg[:, :], wb[:, :], ALU.mult),
                      reads=[cg, wb], writes=[cg], tag="wmul")
                    P.dma("scalar",
                          cols_sc[jc * 128:(jc + 1) * 128, s * L + t0:s * L + t0 + TILE],
                          cg[:, :], reads=[cg], writes=[cols_sc], stream="clw")

        tiles = [(s, it) for s in range(NSEQ) for it in range(NTILE)]
        f1_gate(*tiles[0])
        for ti, (s, it) in enumerate(tiles):
            if ti + 1 < len(tiles):
                f1_gate(*tiles[ti + 1])
            f1_cols(s, it)
        P.barrier()
        fctx.close()

    if "F2" in phases:
        f2ctx = ExitStack()
        pw_sb = f2ctx.enter_context(nc.sbuf_tensor("pw_sb", [128, 32 * 1024], BF16))
        cols_f = f2ctx.enter_context(nc.sbuf_tensor("cols_f", [128, 32 * TILE], BF16))
        xn32_f = f2ctx.enter_context(nc.sbuf_tensor("xn32_f", [128, DCH * TILE], F32))
        att_f = f2ctx.enter_context(nc.sbuf_tensor("att_f", [128, 4 * TILE], F32))
        y_sb = [f2ctx.enter_context(nc.sbuf_tensor(f"y_sb{i}", [128, DCH * TILE], F32))
                for i in range(2)]
        ybf = [f2ctx.enter_context(nc.sbuf_tensor(f"ybf{i}", [128, TILE], BF16))
               for i in range(2)]
        y2bf = [f2ctx.enter_context(nc.sbuf_tensor(f"y2bf{i}", [128, TILE], BF16))
                for i in range(2)]
        zt = [f2ctx.enter_context(nc.sbuf_tensor(f"zt{i}", [128, TILE], F32))
              for i in range(2)]
        e2t = [f2ctx.enter_context(nc.sbuf_tensor(f"e2t{i}", [128, TILE], F32))
               for i in range(2)]
        xpt = zt
        out_st = [f2ctx.enter_context(nc.sbuf_tensor(f"out_st{i}", [128, TILE], F32))
                  for i in range(2)]
        bh = [f2ctx.enter_context(nc.sbuf_tensor(f"bh{i}", [1, TILE], BF16)) for i in range(4)]
        bl = [f2ctx.enter_context(nc.sbuf_tensor(f"bl{i}", [1, TILE], BF16)) for i in range(4)]
        xnc = [f2ctx.enter_context(nc.sbuf_tensor(f"xnc{i}", [128, TILE], F32))
               for i in range(2)]
        mu2r = stage_rows[0:1, 1 * TILE:2 * TILE]
        r2r = stage_rows[0:1, 3 * TILE:4 * TILE]

        for j in range(32):
            P.dma("gpsimd", pw_sb[:, j * 1024:(j + 1) * 1024],
                  pw_in[j * 128:(j + 1) * 128, :], writes=[pw_sb], stream="wgt")

        def f2_loads(s, it):
            t0 = it * TILE
            for j in range(0, 32, 4):
                P.dma("sync", cols_f[:, j * TILE:(j + 4) * TILE].rearrange(
                          "p (a t) -> p a t", a=4),
                      cols_sc.rearrange("(a p) t -> p a t", p=128)[
                          :, j:j + 4, s * L + t0:s * L + t0 + TILE],
                      reads=[cols_sc], writes=[cols_f], stream="clr")
            for j in range(0, DCH, 4):
                P.dma("sync", xn32_f[:, j * TILE:(j + 4) * TILE].rearrange(
                          "p (a t) -> p a t", a=4),
                      xn32_sc.rearrange("(a p) t -> p a t", p=128)[
                          :, j:j + 4, s * L + t0:s * L + t0 + TILE],
                      reads=[xn32_sc], writes=[xn32_f], stream="xnr")

        def f2_projdc(s, it, dc):
            ppb = pb[dc % 2]
            for j in range(32):
                V("tensor", lambda e, dc=dc, j=j, ppb=ppb: e.matmul(
                    ppb[:, :], pw_sb[:, j * 1024 + dc * 128: j * 1024 + (dc + 1) * 128],
                    cols_f[:, j * TILE:(j + 1) * TILE],
                    start=(j == 0), stop=(j == 31)),
                  reads=[pw_sb, cols_f], writes=[ppb], tag="pmm")

        def f2_yev(s, it, dc):
            yb = y_sb[it % 2]
            ppb = pb[dc % 2]
            V("vector", lambda e, dc=dc, ppb=ppb, yb=yb: e.scalar_tensor_tensor(
                yb[:, dc * TILE:(dc + 1) * TILE], ppb[:, :], pb_fm[:, dc:dc + 1],
                xn32_f[:, dc * TILE:(dc + 1) * TILE], op0=ALU.add, op1=ALU.add),
              reads=[ppb, pb_fm, xn32_f], writes=[yb], tag="yev")

        def f2_tail1(s, it):
            t0 = it * TILE
            yb = y_sb[it % 2]
            for ri in (0, 2):
                P.dma("sync", stage_rows[:, ri * TILE:(ri + 1) * TILE],
                      rows_sc[ri:ri + 1, s * L + t0:s * L + t0 + TILE],
                      reads=[rows_sc], writes=[stage_rows], stream="xnr")
            for dc in range(2):
                P.dma("sync", att_f[:, dc * TILE:(dc + 1) * TILE],
                      attn_sc[dc * 128:(dc + 1) * 128, s * L + t0:s * L + t0 + TILE],
                      reads=[attn_sc], writes=[att_f], stream="atr")
            P.dma("sync", xnc[0][:, :], xn32_sc[0:128, s * L + t0:s * L + t0 + TILE],
                  reads=[xn32_sc], writes=[xnc[0]], stream="xnc")
            for dc in range(DCH):
                ybc = ybf[dc % 2]
                y2 = y2bf[dc % 2]
                V("vector", lambda e, dc=dc, ybc=ybc, yb=yb: e.tensor_copy(
                    ybc[:, :], yb[:, dc * TILE:(dc + 1) * TILE]),
                  reads=[yb], writes=[ybc])
                V("vector", lambda e, ybc=ybc, y2=y2: e.tensor_tensor(
                    y2[:, :], ybc[:, :], ybc[:, :], ALU.mult),
                  reads=[ybc], writes=[y2])
                V("tensor", lambda e, dc=dc, ybc=ybc: e.matmul(
                    pb[2][0:1, :], onescol_bf[:, :], ybc[:, :],
                    start=(dc == 0), stop=(dc == DCH - 1)),
                  reads=[onescol_bf, ybc], writes=[pb[2]], tag="sy")
                V("tensor", lambda e, dc=dc, y2=y2: e.matmul(
                    pb[3][0:1, :], onescol_bf[:, :], y2[:, :],
                    start=(dc == 0), stop=(dc == DCH - 1)),
                  reads=[onescol_bf, y2], writes=[pb[3]], tag="sy2")
            V("vector", lambda e: e.tensor_scalar_mul(mu2r[:, :], pb[2][0:1, :],
                                                      float(1.0 / D)),
              reads=[pb[2]], writes=[stage_rows])
            V("vector", lambda e: e.tensor_tensor(r2r[:, :], mu2r[:, :], mu2r[:, :],
                                                  ALU.mult),
              reads=[stage_rows], writes=[stage_rows])
            V("vector", lambda e: e.scalar_tensor_tensor(r2r[:, :], pb[3][0:1, :],
                                                         float(1.0 / D), r2r[:, :],
                                                         op0=ALU.mult, op1=ALU.subtract),
              reads=[pb[3], stage_rows], writes=[stage_rows])
            V("vector", lambda e: e.tensor_single_scalar(r2r[:, :], r2r[:, :],
                                                         float(LN_EPS), op=ALU.add),
              reads=[stage_rows], writes=[stage_rows])
            V("scalar", lambda e: e.activation(r2r[:, :], r2r[:, :], AF.Ln),
              reads=[stage_rows], writes=[stage_rows], tag="r2ln")
            V("scalar", lambda e: e.activation(r2r[:, :], r2r[:, :], AF.Exp, scale=-0.5),
              reads=[stage_rows], writes=[stage_rows], tag="r2exp")
            for z, (bi, rowt) in enumerate(((4, mu2r[:, :]), (5, r2r[:, :]),
                                            (6, stage_rows[:, 2 * TILE:3 * TILE]),
                                            (7, stage_rows[:, 0 * TILE:1 * TILE]))):
                V("vector", lambda e, z=z, rowt=rowt: e.tensor_copy(bh[z][:, :], rowt),
                  reads=[stage_rows], writes=[bh[z]])
                V("vector", lambda e, z=z: e.tensor_copy(rtmp[:, :], bh[z][:, :]),
                  reads=[bh[z]], writes=[rtmp])
                V("vector", lambda e, z=z, rowt=rowt: e.tensor_tensor(
                    bl[z][:, :], rowt, rtmp[:, :], ALU.subtract),
                  reads=[stage_rows, rtmp], writes=[bl[z]])
            for z, bi in enumerate((4, 5, 6, 7)):
                V("tensor", lambda e, bi=bi, z=z: e.matmul(
                    pb[bi][:, :], ones_row_bf[:, :], bh[z][:, :], start=True, stop=False),
                  reads=[ones_row_bf, bh[z]], writes=[pb[bi]])
                V("tensor", lambda e, bi=bi, z=z: e.matmul(
                    pb[bi][:, :], ones_row_bf[:, :], bl[z][:, :], start=False, stop=True),
                  reads=[ones_row_bf, bl[z]], writes=[pb[bi]])

        def f2_combine_dc(s, it, dc):
            t0 = it * TILE
            yb = y_sb[it % 2]
            i = dc % 2
            av = att_f[:, (dc % 2) * TILE:((dc % 2) + 1) * TILE]
            V("vector", lambda e, dc=dc, i=i, yb=yb: e.tensor_tensor(
                zt[i][:, :], yb[:, dc * TILE:(dc + 1) * TILE], pb[4][:, :],
                ALU.subtract),
              reads=[yb, pb[4]], writes=[zt[i]], tag="fc")
            V("vector", lambda e, dc=dc, i=i: e.scalar_tensor_tensor(
                e2t[i][:, :], zt[i][:, :], lng_fm[:, dc:dc + 1], pb[5][:, :],
                op0=ALU.mult, op1=ALU.mult),
              reads=[zt[i], lng_fm, pb[5]], writes=[e2t[i]], tag="fc")
            V("vector", lambda e, dc=dc, i=i: e.tensor_tensor(
                xpt[i][:, :], xnc[i][:, :], pb[6][:, :], ALU.mult),
              reads=[xnc[i], pb[6]], writes=[xpt[i]], tag="fc")
            if dc + 1 < DCH:
                P.dma("sync", xnc[(dc + 1) % 2][:, :],
                      xn32_sc[(dc + 1) * 128:(dc + 2) * 128, s * L + t0:s * L + t0 + TILE],
                      reads=[xn32_sc], writes=[xnc[(dc + 1) % 2]], stream="xnc")
            V("gpsimd", lambda e, av=av, i=i: e.tensor_tensor(
                xpt[i][:, :], xpt[i][:, :], av, ALU.add),
              reads=[xpt[i], att_f], writes=[xpt[i]], tag="fc")
            if dc + 2 < DCH:
                P.dma("sync", att_f[:, (dc % 2) * TILE:((dc % 2) + 1) * TILE],
                      attn_sc[(dc + 2) * 128:(dc + 3) * 128, s * L + t0:s * L + t0 + TILE],
                      reads=[attn_sc], writes=[att_f], stream="atr")
            V("gpsimd", lambda e, i=i: e.tensor_tensor(
                e2t[i][:, :], e2t[i][:, :], xpt[i][:, :], ALU.add),
              reads=[e2t[i], xpt[i]], writes=[e2t[i]], tag="fc")
            V("vector", lambda e, dc=dc, i=i: e.scalar_tensor_tensor(
                out_st[i][:, :], e2t[i][:, :], lnb_fm[:, dc:dc + 1], pb[7][:, :],
                op0=ALU.add, op1=ALU.add),
              reads=[e2t[i], lnb_fm, pb[7]], writes=[out_st[i]], tag="fc")
            P.dma("sync", y_out[dc * 128:(dc + 1) * 128, s * L + t0:s * L + t0 + TILE],
                  out_st[i][:, :], reads=[out_st[i]], writes=[y_out], stream="out")

        tiles2 = [(s, it) for s in range(NSEQ) for it in range(NTILE)]
        f2_loads(*tiles2[0])
        for dc in range(DCH):
            f2_projdc(*tiles2[0], dc)
            f2_yev(*tiles2[0], dc)
        for ti, (s, it) in enumerate(tiles2):
            f2_tail1(s, it)
            nxt = tiles2[ti + 1] if ti + 1 < len(tiles2) else None
            if nxt is not None:
                f2_loads(*nxt)
            for dc in range(DCH):
                if nxt is not None:
                    f2_projdc(*nxt, dc)
                f2_combine_dc(s, it, dc)
                if nxt is not None:
                    f2_yev(*nxt, dc)
        P.barrier()
        f2ctx.close()


    P.wait_all("sync")
    P.emit()
    ctx.close()
    return nc


_PROGRAM_CACHE = {}


def _get_program():
    key = (tuple(_PHASES), tuple(_DEBUG_TAPS))
    if key not in _PROGRAM_CACHE:
        _PROGRAM_CACHE[key] = build_program()
    return _PROGRAM_CACHE[key]


def kernel(L0_attn_x, L0_attn_Wq, L0_attn_Wk, L0_attn_Wv, L0_attn_Wo,
           L0_ffn_col_emb, L0_ffn_gate_w1, L0_ffn_gate_b1, L0_ffn_gate_w2,
           L0_ffn_gate_b2, L0_ffn_W_cols, L0_ffn_b_cols, L0_ffn_proj_w,
           L0_ffn_proj_b, L0_ffn_ln_g, L0_ffn_ln_b):
    nc = _get_program()
    x = np.ascontiguousarray(np.asarray(L0_attn_x, dtype=np.float32))
    shared = {
        "Wq": np.ascontiguousarray(L0_attn_Wq, dtype=np.float32),
        "Wk": np.ascontiguousarray(L0_attn_Wk, dtype=np.float32),
        "Wv": np.ascontiguousarray(L0_attn_Wv, dtype=np.float32),
        "Wo": np.ascontiguousarray(L0_attn_Wo, dtype=np.float32),
        "col_emb": np.ascontiguousarray(L0_ffn_col_emb, dtype=np.float32),
        "gate_w1": np.ascontiguousarray(L0_ffn_gate_w1, dtype=np.float32),
        "gate_b1": np.ascontiguousarray(L0_ffn_gate_b1, dtype=np.float32),
        "gate_w2": np.ascontiguousarray(L0_ffn_gate_w2, dtype=np.float32),
        "gate_b2": np.ascontiguousarray(L0_ffn_gate_b2, dtype=np.float32),
        "W_cols": np.ascontiguousarray(L0_ffn_W_cols, dtype=np.float32),
        "b_cols": np.ascontiguousarray(L0_ffn_b_cols, dtype=np.float32),
        "proj_w": np.ascontiguousarray(L0_ffn_proj_w, dtype=np.float32),
        "proj_b": np.ascontiguousarray(L0_ffn_proj_b, dtype=np.float32),
        "ln_g": np.ascontiguousarray(L0_ffn_ln_g, dtype=np.float32),
        "ln_b": np.ascontiguousarray(L0_ffn_ln_b, dtype=np.float32),
    }
    in_maps = []
    for i in range(NCORES):
        m = dict(shared)
        m["x"] = x[i * NSEQ:(i + 1) * NSEQ].reshape(NSEQ * L, D)
        in_maps.append(m)
    res = run_bass_kernel_spmd(nc, in_maps, core_ids=list(range(NCORES)))
    out = np.empty((B, L, D), dtype=np.float32)
    for i in range(NCORES):
        yi = res.results[i]["y"]
        out[i * NSEQ:(i + 1) * NSEQ] = yi.reshape(D, NSEQ, L).transpose(1, 2, 0)
    return out


# version 18 (final)
# speedup vs baseline: 1.3944x; 1.0587x over previous
import math
import numpy as np
from contextlib import ExitStack

import concourse.bass as bass
import concourse.mybir as mybir
from concourse.bass_utils import run_bass_kernel_spmd

F32 = mybir.dt.float32
BF16 = mybir.dt.bfloat16
I32 = mybir.dt.int32
AF = mybir.ActivationFunctionType
ALU = mybir.AluOpType
AX = mybir.AxisListType

D = 1024
L = 2048
B = 16
NCORES = 8
NSEQ = B // NCORES
H, KV, HD = 16, 4, 64
GROUPS = H // KV
NPAIR = H // 2
C, NN = 16, 256
CF = C * NN
DCH = D // 128
ROPE_BASE = 10000.0
EXP_MARGIN = 30.0
LN_EPS = 1e-5

TILE = 512
NTILE = L // TILE
NSUB = L // 128
QT = 1024
NQT = L // QT
NKC = L // 128

_DEBUG_TAPS = []
_PHASES = ("A", "AT", "F1", "F2")


def _alibi_slopes(n_heads):
    def pow2(n):
        start = 2.0 ** (-2.0 ** (-(math.log2(n) - 3)))
        return [start * start ** i for i in range(n)]
    if math.log2(n_heads).is_integer():
        return pow2(n_heads)
    c = 2 ** math.floor(math.log2(n_heads))
    return pow2(c) + _alibi_slopes(2 * c)[0::2][: n_heads - c]


COMPUTE_ENGINES = ("tensor", "vector", "scalar", "gpsimd")
ALL_ENGINES = ("tensor", "vector", "scalar", "gpsimd", "sync")
SEM_CAP = 50000


def _key(x):
    t = getattr(x, "tensor", None)
    if t is not None:
        return t.name
    return x.name


class Prog:
    def __init__(self, nc, ctx):
        self.nc = nc
        self.ctx = ctx
        self.ops = {e: [] for e in ALL_ENGINES}
        self.sem_objs = {}
        self.sem_count = {}
        self.sem_exact = {}
        for e in COMPUTE_ENGINES:
            sid = "e:" + e
            self.sem_objs[sid] = ctx.enter_context(nc.semaphore("sem_" + e))
            self.sem_count[sid] = 0
            self.sem_exact[sid] = True
        self.dma_gen = {}
        self.track = {}
        self.waited = {e: {} for e in ALL_ENGINES}
        self.handed_out = {}
        self.n_ops = 0

    def _resolve(self, sem_id, val):
        if val is None or not self.sem_exact[sem_id]:
            return self.sem_count[sem_id]
        return val

    def _edges_for(self, reads, writes):
        need = {}
        for x in reads:
            rec = self.track.get(_key(x))
            if rec and rec["w"]:
                s, v = rec["w"]
                need[s] = max(need.get(s, 0), self._resolve(s, v))
        for x in writes:
            rec = self.track.get(_key(x))
            if rec:
                if rec["w"]:
                    s, v = rec["w"]
                    need[s] = max(need.get(s, 0), self._resolve(s, v))
                for s, v in rec["r"].items():
                    need[s] = max(need.get(s, 0), self._resolve(s, v))
        return need

    def _mk_waits(self, engine, need, skip_own_pe):
        waits = []
        for s, v in sorted(need.items()):
            if skip_own_pe and s == "e:tensor":
                continue
            if v <= 0:
                continue
            self.handed_out[s] = max(self.handed_out.get(s, 0), v)
            if self.waited[engine].get(s, 0) >= v:
                continue
            self.waited[engine][s] = v
            waits.append((self.sem_objs[s], v))
        return waits

    def _record(self, reads, writes, sem_id, val):
        for x in reads:
            rec = self.track.setdefault(_key(x), {"w": None, "r": {}})
            old = rec["r"].get(sem_id, 0)
            if val is None or old is None:
                rec["r"][sem_id] = None
            else:
                rec["r"][sem_id] = max(old, val)
        for x in writes:
            self.track[_key(x)] = {"w": (sem_id, val), "r": {}}

    def op(self, engine, fn, reads=(), writes=(), tag=None):
        assert engine in COMPUTE_ENGINES, engine
        need = self._edges_for(reads, writes)
        waits = self._mk_waits(engine, need, skip_own_pe=(engine == "tensor"))
        sid = "e:" + engine
        self.sem_count[sid] += 1
        val = self.sem_count[sid]
        self._record(reads, writes, sid, val)
        self.ops[engine].append((waits, fn, self.sem_objs[sid], 1, tag))
        self.n_ops += 1

    def _dma_sid(self, queue, stream):
        gkey = (queue, stream)
        gen = self.dma_gen.get(gkey, 0)
        sid = f"d:{queue}:{stream}:{gen}"
        if sid not in self.sem_objs:
            self.sem_objs[sid] = self.ctx.enter_context(
                self.nc.semaphore(f"dma_{queue}_{stream}_{gen}"))
            self.sem_count[sid] = 0
            self.sem_exact[sid] = False
        if self.sem_count[sid] >= SEM_CAP:
            self.dma_gen[gkey] = gen + 1
            return self._dma_sid(queue, stream)
        return sid

    def dma(self, queue, out, in_, reads=(), writes=(), stream="w", tag=None, **kw):
        assert queue in ("sync", "gpsimd", "scalar"), queue
        need = self._edges_for(reads, writes)
        waits = self._mk_waits(queue, need, skip_own_pe=False)
        sid = self._dma_sid(queue, stream)
        ho = self.handed_out.get(sid, 0)
        if ho > self.waited[queue].get(sid, 0):
            self.waited[queue][sid] = ho
            waits.append((self.sem_objs[sid], ho))
        self.sem_count[sid] += 16
        self._record(list(reads), list(writes), sid, None)
        fn = lambda eng, out=out, in_=in_, kw=kw: eng.dma_start(out=out, in_=in_, **kw)
        self.ops[queue].append((waits, fn, self.sem_objs[sid], 16, tag))
        self.n_ops += 1

    def wait_all(self, engine):
        need = {s: c for s, c in self.sem_count.items() if c > 0}
        waits = self._mk_waits(engine, need, skip_own_pe=False)
        self.ops[engine].append((waits, None, None, 0, "barrier"))

    def barrier(self):
        for e in ALL_ENGINES:
            self.wait_all(e)

    def emit(self):
        nc = self.nc
        prog = self
        with nc.Block() as block:
            for engine in ALL_ENGINES:
                if not prog.ops[engine]:
                    continue

                def section(eng, engine=engine):
                    for waits, fn, sem, inc, tag in prog.ops[engine]:
                        for h, v in waits:
                            eng.wait_ge(h, v)
                        if fn is not None:
                            inst = fn(eng)
                            if sem is not None:
                                inst.then_inc(sem, inc)

                getattr(block, engine)(section)


def build_program(phases=_PHASES, dump_scratch=False):
    nc = bass.Bass("TRN2", target_bir_lowering=False)
    ctx = ExitStack()
    P = Prog(nc, ctx)

    def sb(name, shape, dt=F32):
        return ctx.enter_context(nc.sbuf_tensor(name, shape, dt))

    def psum(name, shape, dt=F32):
        return ctx.enter_context(nc.psum_tensor(name, shape, dt))

    def V(e, fn, reads=(), writes=(), tag=None):
        P.op(e, fn, reads=reads, writes=writes, tag=tag)

    def dramio(name, shape, dt, kind):
        return nc.dram_tensor(name, shape, dt, kind=kind).ap()

    x_in = dramio("x", [NSEQ * L, D], F32, "ExternalInput")
    wq_in = dramio("Wq", [D, H * HD], F32, "ExternalInput")
    wk_in = dramio("Wk", [D, KV * HD], F32, "ExternalInput")
    wv_in = dramio("Wv", [D, KV * HD], F32, "ExternalInput")
    wo_in = dramio("Wo", [H * HD, D], F32, "ExternalInput")
    ce_in = dramio("col_emb", [C, D], F32, "ExternalInput")
    gw1_in = dramio("gate_w1", [D, D // 2], F32, "ExternalInput")
    gb1_in = dramio("gate_b1", [D // 2], F32, "ExternalInput")
    gw2_in = dramio("gate_w2", [D // 2, C], F32, "ExternalInput")
    gb2_in = dramio("gate_b2", [C], F32, "ExternalInput")
    wcols_in = dramio("W_cols", [C, D, NN], F32, "ExternalInput")
    bcols_in = dramio("b_cols", [C, NN], F32, "ExternalInput")
    pw_in = dramio("proj_w", [CF, D], F32, "ExternalInput")
    pb_in = dramio("proj_b", [D], F32, "ExternalInput")
    lng_in = dramio("ln_g", [D], F32, "ExternalInput")
    lnb_in = dramio("ln_b", [D], F32, "ExternalInput")
    y_out = dramio("y", [D, NSEQ * L], F32, "ExternalOutput")

    def scratch(name, shape, dt, produced_by, consumed_by):
        prod = produced_by in phases
        cons = any(p in phases for p in consumed_by)
        if prod and dump_scratch:
            return dramio(name, shape, dt, "ExternalOutput")
        if not prod and cons:
            return dramio(name, shape, dt, "ExternalInput")
        if prod or cons:
            return nc.dram_tensor(name, shape, dt).ap()
        return None

    xnb_sc = scratch("xnb_sc", [D, NSEQ * L], BF16, "A", ("AT", "F1"))
    xn32_sc = scratch("xn32_sc", [D, NSEQ * L], F32, "A", ("F2",))
    rows_sc = scratch("rows_sc", [4, NSEQ * L], F32, "A", ("F1", "F2"))
    attn_sc = scratch("attn_sc", [D, NSEQ * L], F32, "AT", ("F2",))
    cols_sc = scratch("cols_sc", [CF, NSEQ * L], BF16, "F1", ("F2",))

    taps = {}

    def tap(name, src_ap, shape, dt=F32, reads=()):
        if name not in _DEBUG_TAPS:
            return
        out = dramio("dbg_" + name, shape, dt, "ExternalOutput")
        P.dma("sync", out[:, :], src_ap, reads=reads, stream="dbg")

    pb = [psum(f"pb{i}", [128, 512]) for i in range(8)]

    ident = sb("ident", [128, 128])
    V("gpsimd", lambda e: e.memset(ident[:, :], 0.0), writes=[ident])
    V("gpsimd", lambda e: e.affine_select(out=ident[:, :], in_=ident[:, :],
                                          compare_op=ALU.not_equal, fill=1.0, base=0,
                                          pattern=[[-1, 128]], channel_multiplier=1),
      reads=[ident], writes=[ident])
    ones_row_bf = sb("ones_row_bf", [1, 128], BF16)
    V("gpsimd", lambda e: e.memset(ones_row_bf[:, :], 1.0), writes=[ones_row_bf])
    onescol_bf = sb("onescol_bf", [128, 1], BF16)
    V("gpsimd", lambda e: e.memset(onescol_bf[:, :], 1.0), writes=[onescol_bf])
    selc = sb("selc", [16, C * 128], BF16)
    V("gpsimd", lambda e: e.memset(selc[:, :], 1.0), writes=[selc])
    for c in range(C):
        V("gpsimd", lambda e, c=c: e.affine_select(
            out=selc[:, c * 128:(c + 1) * 128], in_=selc[:, c * 128:(c + 1) * 128],
            compare_op=ALU.is_equal, fill=0.0, base=-c, pattern=[[0, 128]],
            channel_multiplier=1), reads=[selc], writes=[selc])
    psel_a = sb("psel_a", [1, 128], BF16)
    psel_b = sb("psel_b", [1, 128], BF16)
    iotap = sb("iotap", [1, 128])
    V("gpsimd", lambda e: e.iota(iotap[:, :], [[1, 128]], channel_multiplier=0,
                                 allow_small_or_imprecise_dtypes=True), writes=[iotap])
    V("vector", lambda e: e.tensor_single_scalar(psel_a[:, :], iotap[:, :], 64.0, op=ALU.is_lt),
      reads=[iotap], writes=[psel_a])
    V("vector", lambda e: e.tensor_single_scalar(psel_b[:, :], iotap[:, :], 64.0, op=ALU.is_ge),
      reads=[iotap], writes=[psel_b])

    b1_fm = sb("b1_fm", [128, 4])
    pb_fm = sb("pb_fm", [128, 8])
    lng_fm = sb("lng_fm", [128, 8])
    lnb_fm = sb("lnb_fm", [128, 8])
    bcols_fm = sb("bcols_fm", [128, 32])
    gb2_t = sb("gb2_t", [C, 1])
    for dst, src, n in [(b1_fm, gb1_in, 4), (pb_fm, pb_in, 8),
                        (lng_fm, lng_in, 8), (lnb_fm, lnb_in, 8)]:
        P.dma("sync", dst[:, 0:n], src.rearrange("(n p) -> p n", p=128),
              writes=[dst], stream="smallw", allow_slow_non_contiguous=True)
    P.dma("sync", bcols_fm[:, :], bcols_in.rearrange("c (m p) -> p (c m)", p=128),
          writes=[bcols_fm], stream="smallw", allow_slow_non_contiguous=True)
    P.dma("sync", gb2_t[:, :], gb2_in.rearrange("(c u) -> c u", u=1),
          writes=[gb2_t], stream="smallw", allow_slow_non_contiguous=True)
    gb2h_t = sb("gb2h_t", [C, 1])
    V("vector", lambda e: e.tensor_scalar_mul(gb2h_t[:, :], gb2_t[:, :], 0.5),
      reads=[gb2_t], writes=[gb2h_t])

    bias_hk = sb("bias_hk", [128, H * NKC])
    negl2 = sb("negl2", [1, 1])
    V("gpsimd", lambda e: e.memset(negl2[:, :], -float(np.log(2.0) / 2.0)), writes=[negl2])
    eps_t = sb("eps_t", [1, 1])
    V("gpsimd", lambda e: e.memset(eps_t[:, :], float(LN_EPS)), writes=[eps_t])
    bias_tmp_ctx = ExitStack()
    slope_rep = bias_tmp_ctx.enter_context(nc.sbuf_tensor("slope_rep", [1, H * NKC], F32))
    c_row = bias_tmp_ctx.enter_context(nc.sbuf_tensor("c_row", [1, H * NKC], F32))
    kc_row = bias_tmp_ctx.enter_context(nc.sbuf_tensor("kc_row", [1, H * NKC], F32))
    iota_row = bias_tmp_ctx.enter_context(nc.sbuf_tensor("iota_row", [1, 128], F32))
    ones_row_f = bias_tmp_ctx.enter_context(nc.sbuf_tensor("ones_row_f", [1, 128], F32))
    V("gpsimd", lambda e: e.iota(iota_row[:, :], [[1, 128]], channel_multiplier=0,
                                 allow_small_or_imprecise_dtypes=True), writes=[iota_row])
    V("gpsimd", lambda e: e.memset(ones_row_f[:, :], 1.0), writes=[ones_row_f])
    V("gpsimd", lambda e: e.iota(slope_rep[:, :], [[1, H], [0, NKC]], channel_multiplier=0,
                                 allow_small_or_imprecise_dtypes=True), writes=[slope_rep],
      tag="sl-iota")
    V("scalar", lambda e: e.activation(slope_rep[:, :], slope_rep[:, :], AF.Exp,
                                       scale=-float(np.log(2.0) / 2.0), bias=negl2[:, :]),
      reads=[slope_rep, negl2], writes=[slope_rep])
    V("gpsimd", lambda e: e.iota(kc_row[:, :], [[0, H], [128, NKC]], channel_multiplier=0,
                                 allow_small_or_imprecise_dtypes=True), writes=[kc_row])
    V("vector", lambda e: e.tensor_scalar_add(c_row[:, :], kc_row[:, :], -2047.0),
      reads=[kc_row], writes=[c_row])
    V("vector", lambda e: e.tensor_tensor(c_row[:, :], c_row[:, :], slope_rep[:, :], ALU.mult),
      reads=[c_row, slope_rep], writes=[c_row])
    V("vector", lambda e: e.tensor_scalar_add(c_row[:, :], c_row[:, :], -float(EXP_MARGIN)),
      reads=[c_row], writes=[c_row])
    V("tensor", lambda e: e.matmul(pb[0][:, 0:H * NKC], iota_row[:, :], slope_rep[:, :],
                                   start=True, stop=False),
      reads=[iota_row, slope_rep], writes=[pb[0]])
    V("tensor", lambda e: e.matmul(pb[0][:, 0:H * NKC], ones_row_f[:, :], c_row[:, :],
                                   start=False, stop=True),
      reads=[ones_row_f, c_row], writes=[pb[0]])
    V("vector", lambda e: e.tensor_copy(bias_hk[:, :], pb[0][:, 0:H * NKC]),
      reads=[pb[0]], writes=[bias_hk])
    P.barrier()
    bias_tmp_ctx.close()

    ce_fm = sb("ce_fm", [128, DCH * C], BF16)
    stage_rows = sb("stage_rows", [1, 4 * TILE])
    rhi = sb("rhi", [1, 512], BF16)
    rlo = sb("rlo", [1, 512], BF16)
    rtmp = sb("rtmp", [1, 512])
    invf = sb("invf", [128, 1])
    ropes_ctx = ExitStack()
    cos_d = ropes_ctx.enter_context(nc.sbuf_tensor("cos_d", [128, L], F32))
    sin_s = ropes_ctx.enter_context(nc.sbuf_tensor("sin_s", [128, L], F32))
    tmp_ctx = ExitStack()
    angbuf = tmp_ctx.enter_context(nc.sbuf_tensor("angbuf", [128, L], F32))
    rrbuf = tmp_ctx.enter_context(nc.sbuf_tensor("rrbuf", [128, L], F32))
    kfbuf = tmp_ctx.enter_context(nc.sbuf_tensor("kfbuf", [128, L], F32))
    kibuf = tmp_ctx.enter_context(nc.sbuf_tensor("kibuf", [128, L], I32))
    for p0 in (0, 32, 64, 96):
        V("gpsimd", lambda e, p0=p0: e.iota(invf[p0:p0 + 32, :], [[0, 1]],
                                            channel_multiplier=1,
                                            allow_small_or_imprecise_dtypes=True),
          writes=[invf])
    V("scalar", lambda e: e.activation(invf[:, :], invf[:, :], AF.Exp,
                                       scale=-float(np.log(ROPE_BASE) / 32.0)),
      reads=[invf], writes=[invf])
    V("gpsimd", lambda e: e.iota(angbuf[:, :], [[1, L]], channel_multiplier=0,
                                 allow_small_or_imprecise_dtypes=True), writes=[angbuf])
    V("vector", lambda e: e.tensor_scalar_mul(angbuf[:, :], angbuf[:, :], invf[:, :]),
      reads=[angbuf, invf], writes=[angbuf])
    TWO_PI = 2 * np.pi
    HIPI = float(np.float32(6.28125))
    LOPI = float(TWO_PI - 6.28125)

    def range_reduce(dst, shift):
        V("vector", lambda e: e.tensor_scalar(rrbuf[:, :], angbuf[:, :], float(shift),
                                              float(1.0 / TWO_PI), op0=ALU.add, op1=ALU.mult),
          reads=[angbuf], writes=[rrbuf])
        V("vector", lambda e: e.tensor_copy(kibuf[:, :], rrbuf[:, :]), reads=[rrbuf], writes=[kibuf])
        V("vector", lambda e: e.tensor_copy(kfbuf[:, :], kibuf[:, :]), reads=[kibuf], writes=[kfbuf])
        V("vector", lambda e: e.tensor_tensor(rrbuf[:, :], rrbuf[:, :], kfbuf[:, :], ALU.subtract),
          reads=[rrbuf, kfbuf], writes=[rrbuf])
        V("vector", lambda e: e.tensor_single_scalar(rrbuf[:, :], rrbuf[:, :], 0.5, op=ALU.is_ge),
          reads=[rrbuf], writes=[rrbuf])
        V("vector", lambda e: e.tensor_tensor(kfbuf[:, :], kfbuf[:, :], rrbuf[:, :], ALU.add),
          reads=[kfbuf, rrbuf], writes=[kfbuf])
        V("vector", lambda e: e.tensor_scalar(rrbuf[:, :], angbuf[:, :], float(shift), 1.0,
                                              op0=ALU.add, op1=ALU.bypass),
          reads=[angbuf], writes=[rrbuf])
        V("vector", lambda e: e.scalar_tensor_tensor(rrbuf[:, :], kfbuf[:, :], -HIPI, rrbuf[:, :],
                                                     op0=ALU.mult, op1=ALU.add),
          reads=[kfbuf, rrbuf], writes=[rrbuf])
        V("vector", lambda e: e.scalar_tensor_tensor(dst[:, :], kfbuf[:, :], -LOPI, rrbuf[:, :],
                                                     op0=ALU.mult, op1=ALU.add),
          reads=[kfbuf, rrbuf], writes=[dst])
        V("scalar", lambda e: e.activation(dst[:, :], dst[:, :], AF.Sin),
          reads=[dst], writes=[dst])

    range_reduce(sin_s, 0.0)
    range_reduce(cos_d, np.pi / 2)
    for p0 in (0, 64):
        V("vector", lambda e, p0=p0: e.tensor_scalar_mul(sin_s[p0:p0 + 32, :],
                                                         sin_s[p0:p0 + 32, :], -1.0),
          reads=[sin_s], writes=[sin_s])

    ce_tm = tmp_ctx.enter_context(nc.sbuf_tensor("ce_tm", [C, D], F32))
    ce_sq = tmp_ctx.enter_context(nc.sbuf_tensor("ce_sq", [C, D], F32))
    ce_nrm = tmp_ctx.enter_context(nc.sbuf_tensor("ce_nrm", [C, 1], F32))
    ce_nf = tmp_ctx.enter_context(nc.sbuf_tensor("ce_nf", [C, D], F32))
    P.dma("sync", ce_tm[:, :], ce_in[:, :], writes=[ce_tm], stream="smallw")
    V("scalar", lambda e: e.activation(ce_sq[:, :], ce_tm[:, :], AF.Square),
      reads=[ce_tm], writes=[ce_sq])
    V("vector", lambda e: e.reduce_sum(ce_nrm[:, :], ce_sq[:, :], axis=AX.X),
      reads=[ce_sq], writes=[ce_nrm])
    V("scalar", lambda e: e.activation(ce_nrm[:, :], ce_nrm[:, :], AF.Sqrt),
      reads=[ce_nrm], writes=[ce_nrm])
    V("vector", lambda e: e.tensor_single_scalar(ce_nrm[:, :], ce_nrm[:, :], 1e-12, op=ALU.max),
      reads=[ce_nrm], writes=[ce_nrm])
    V("vector", lambda e: e.reciprocal(ce_nrm[:, :], ce_nrm[:, :]),
      reads=[ce_nrm], writes=[ce_nrm])
    V("vector", lambda e: e.tensor_scalar_mul(ce_nf[:, :], ce_tm[:, :], ce_nrm[:, :]),
      reads=[ce_tm, ce_nrm], writes=[ce_nf])
    for j in range(DCH):
        pbt = pb[1 + j // 4]
        V("tensor", lambda e, j=j, pbt=pbt: e.transpose(
            pbt[:, (j % 4) * 128: (j % 4) * 128 + C],
            ce_nf[:, j * 128:(j + 1) * 128], ident[0:C, 0:C]),
          reads=[ce_nf, ident], writes=[pbt])
    for j in range(DCH):
        pbt = pb[1 + j // 4]
        V("vector", lambda e, j=j, pbt=pbt: e.tensor_copy(
            ce_fm[:, j * C:(j + 1) * C], pbt[:, (j % 4) * 128: (j % 4) * 128 + C]),
          reads=[pbt], writes=[ce_fm])

    P.barrier()
    tmp_ctx.close()

    def row_hilo(src_ap, n, src_keys):
        V("vector", lambda e: e.tensor_copy(rhi[:, 0:n], src_ap), reads=src_keys, writes=[rhi])
        V("vector", lambda e: e.tensor_copy(rtmp[:, 0:n], rhi[:, 0:n]), reads=[rhi], writes=[rtmp])
        V("vector", lambda e: e.tensor_tensor(rlo[:, 0:n], src_ap, rtmp[:, 0:n], ALU.subtract),
          reads=list(src_keys) + [rtmp], writes=[rlo])
        return rhi[:, 0:n], rlo[:, 0:n]

    if "A" in phases:
        kv_ctx = ExitStack()
        kdup = [kv_ctx.enter_context(nc.sbuf_tensor(f"kdup{s}", [128, KV * L], BF16))
                for s in range(NSEQ)]
        VROW = KV * 72
        v4 = [kv_ctx.enter_context(nc.sbuf_tensor(f"v4{s}", [128, NKC * VROW], BF16))
              for s in range(NSEQ)]
        actx = ExitStack()
        x_tm = actx.enter_context(nc.sbuf_tensor("x_tm", [128, 4 * D], F32))
        st6 = actx.enter_context(nc.sbuf_tensor("st6", [128, 12], F32))
        st2 = actx.enter_context(nc.sbuf_tensor("st2", [128, 2], F32))
        stq = actx.enter_context(nc.sbuf_tensor("stq", [128, 4], F32))
        sttmp = actx.enter_context(nc.sbuf_tensor("sttmp", [128, 1], F32))
        x_fm = actx.enter_context(nc.sbuf_tensor("x_fm", [128, DCH * TILE], F32))
        xn32_t = actx.enter_context(nc.sbuf_tensor("xn32_t", [128, DCH * TILE], F32))
        xnb_t = actx.enter_context(nc.sbuf_tensor("xnb_t", [128, DCH * TILE], BF16))
        wk_sb = actx.enter_context(nc.sbuf_tensor("wk_sb", [128, DCH * 256], BF16))
        wv_sb = actx.enter_context(nc.sbuf_tensor("wv_sb", [128, DCH * 256], BF16))
        kraw = actx.enter_context(nc.sbuf_tensor("kraw", [128, TILE], F32))
        kc1 = actx.enter_context(nc.sbuf_tensor("kc1", [128, TILE], F32))
        kc2 = actx.enter_context(nc.sbuf_tensor("kc2", [128, TILE], F32))

        for dc in range(DCH):
            P.dma("gpsimd", wk_sb[:, dc * 256:(dc + 1) * 256],
                  wk_in[dc * 128:(dc + 1) * 128, :], writes=[wk_sb], stream="wgt")
            P.dma("gpsimd", wv_sb[:, dc * 256:(dc + 1) * 256],
                  wv_in[dc * 128:(dc + 1) * 128, :], writes=[wv_sb], stream="wgt")

        for s in range(NSEQ):
            V("gpsimd", lambda e, s=s: e.memset(v4[s][:, :], 1.0), writes=[v4[s]])
            for it in range(NTILE):
                t0 = it * TILE
                for u in range(4):
                    P.dma("sync", x_tm[:, u * D:(u + 1) * D],
                          x_in[s * L + t0 + u * 128: s * L + t0 + (u + 1) * 128, :],
                          writes=[x_tm], stream="x")
                for u in range(4):
                    sub = it * 4 + u
                    xv = x_tm[:, u * D:(u + 1) * D]
                    V("vector", lambda e, xv=xv: e.bn_stats(st6[:, 0:6], xv[:, 0:512]),
                      reads=[x_tm], writes=[st6])
                    V("vector", lambda e, xv=xv: e.bn_stats(st6[:, 6:12], xv[:, 512:1024]),
                      reads=[x_tm], writes=[st6])
                    V("vector", lambda e: e.bn_aggr(st2[:, :], st6[:, :]),
                      reads=[st6], writes=[st2])
                    V("vector", lambda e: e.tensor_copy(stq[:, 0:1], st2[:, 0:1]),
                      reads=[st2], writes=[stq])
                    V("vector", lambda e: e.tensor_scalar_add(sttmp[:, :], st2[:, 1:2],
                                                              float(LN_EPS)),
                      reads=[st2], writes=[sttmp])
                    V("scalar", lambda e: e.activation(stq[:, 2:3], sttmp[:, :], AF.Sqrt),
                      reads=[sttmp], writes=[stq])
                    V("vector", lambda e: e.reciprocal(stq[:, 1:2], stq[:, 2:3]),
                      reads=[stq], writes=[stq])
                    V("scalar", lambda e: e.activation(sttmp[:, :], st2[:, 1:2], AF.Sqrt,
                                                       scale=float(D)),
                      reads=[st2], writes=[sttmp])
                    V("vector", lambda e: e.tensor_tensor(sttmp[:, :], sttmp[:, :], stq[:, 1:2],
                                                          ALU.mult),
                      reads=[sttmp, stq], writes=[sttmp])
                    V("vector", lambda e: e.tensor_single_scalar(sttmp[:, :], sttmp[:, :], 1e-12,
                                                                 op=ALU.max),
                      reads=[sttmp], writes=[sttmp])
                    V("vector", lambda e: e.reciprocal(stq[:, 3:4], sttmp[:, :]),
                      reads=[sttmp], writes=[stq])
                    for q in range(4):
                        V("tensor", lambda e, q=q: e.transpose(
                            pb[2][0:1, q * 128:(q + 1) * 128],
                            stq[:, q:q + 1], ident[:, :]),
                          reads=[stq, ident], writes=[pb[2]], tag="stT")
                    for q in range(4):
                        V("vector", lambda e, q=q, u=u: e.tensor_copy(
                            stage_rows[:, q * TILE + u * 128:q * TILE + (u + 1) * 128],
                            pb[2][0:1, q * 128:(q + 1) * 128]),
                          reads=[pb[2]], writes=[stage_rows])
                    for j in range(DCH):
                        pbt = pb[0] if j < 4 else pb[1]
                        V("tensor", lambda e, j=j, xv=xv, pbt=pbt: e.transpose(
                            pbt[:, (j % 4) * 128:(j % 4 + 1) * 128],
                            xv[:, j * 128:(j + 1) * 128], ident[:, :]),
                          reads=[x_tm, ident], writes=[pbt], tag="xT")
                    for j in range(DCH):
                        pbt = pb[0] if j < 4 else pb[1]
                        V("scalar", lambda e, j=j, u=u, pbt=pbt: e.copy(
                            x_fm[:, j * TILE + u * 128:j * TILE + (u + 1) * 128],
                            pbt[:, (j % 4) * 128:(j % 4 + 1) * 128]),
                          reads=[pbt], writes=[x_fm])
                for z in (0, 1):
                    hi, lo = row_hilo(stage_rows[:, z * TILE:(z + 1) * TILE], TILE, [stage_rows])
                    V("tensor", lambda e, z=z, hi=hi: e.matmul(
                        pb[4 + z][:, :], ones_row_bf[:, :], hi,
                        start=True, stop=False), reads=[ones_row_bf, rhi], writes=[pb[4 + z]])
                    V("tensor", lambda e, z=z, lo=lo: e.matmul(
                        pb[4 + z][:, :], ones_row_bf[:, :], lo,
                        start=False, stop=True), reads=[ones_row_bf, rlo], writes=[pb[4 + z]])
                for j in range(DCH):
                    xfj = x_fm[:, j * TILE:(j + 1) * TILE]
                    V("vector", lambda e, xfj=xfj, j=j: e.scalar_tensor_tensor(
                        xn32_t[:, j * TILE:(j + 1) * TILE], xfj, 1.0, pb[4][:, :],
                        op0=ALU.mult, op1=ALU.subtract),
                      reads=[x_fm, pb[4]], writes=[xn32_t])
                    V("vector", lambda e, j=j: e.tensor_tensor(
                        xn32_t[:, j * TILE:(j + 1) * TILE],
                        xn32_t[:, j * TILE:(j + 1) * TILE], pb[5][:, :], ALU.mult),
                      reads=[xn32_t, pb[5]], writes=[xn32_t])
                    V("vector", lambda e, j=j: e.tensor_copy(
                        xnb_t[:, j * TILE:(j + 1) * TILE], xn32_t[:, j * TILE:(j + 1) * TILE]),
                      reads=[xn32_t], writes=[xnb_t])
                for j in range(0, DCH, 4):
                    P.dma("scalar", xnb_sc.rearrange("(a p) t -> p a t", p=128)[
                              :, j:j + 4, s * L + t0:s * L + t0 + TILE],
                          xnb_t[:, j * TILE:(j + 4) * TILE].rearrange(
                              "p (a t) -> p a t", a=4),
                          reads=[xnb_t], writes=[xnb_sc], stream="xnw")
                    P.dma("scalar", xn32_sc.rearrange("(a p) t -> p a t", p=128)[
                              :, j:j + 4, s * L + t0:s * L + t0 + TILE],
                          xn32_t[:, j * TILE:(j + 4) * TILE].rearrange(
                              "p (a t) -> p a t", a=4),
                          reads=[xn32_t], writes=[xn32_sc], stream="xnw")
                for ri in range(4):
                    P.dma("scalar", rows_sc[ri:ri + 1, s * L + t0:s * L + t0 + TILE],
                          stage_rows[:, ri * TILE:(ri + 1) * TILE], reads=[stage_rows],
                          writes=[rows_sc], stream="xnw")
                for kp in range(2):
                    kpb = pb[6 + kp]
                    for j in range(DCH):
                        V("tensor", lambda e, j=j, kp=kp, kpb=kpb: e.matmul(
                            kpb[:, :],
                            wk_sb[:, j * 256 + kp * 128: j * 256 + (kp + 1) * 128],
                            xnb_t[:, j * TILE:(j + 1) * TILE],
                            start=(j == 0), stop=(j == DCH - 1)),
                          reads=[wk_sb, xnb_t], writes=[kpb], tag="kmm")
                    V("vector", lambda e, kpb=kpb, t0=t0: e.tensor_tensor(
                        kc1[:, :], kpb[:, :], cos_d[:, t0:t0 + TILE], ALU.mult),
                      reads=[kpb, cos_d], writes=[kc1])
                    for p0, rp0 in ((0, 32), (32, 0), (64, 96), (96, 64)):
                        V("vector", lambda e, p0=p0, rp0=rp0, t0=t0, kpb=kpb: e.tensor_tensor(
                            kc2[p0:p0 + 32, :],
                            kpb[rp0:rp0 + 32, :],
                            sin_s[p0:p0 + 32, t0:t0 + TILE], ALU.mult),
                          reads=[kpb, sin_s], writes=[kc2])
                    V("vector", lambda e: e.tensor_tensor(kraw[:, :], kc1[:, :], kc2[:, :],
                                                          ALU.add),
                      reads=[kc1, kc2], writes=[kraw])
                    for half, kvi in ((0, 2 * kp), (64, 2 * kp + 1)):
                        for dst0 in (0, 64):
                            V("vector", lambda e, half=half, kvi=kvi, dst0=dst0, s=s, t0=t0:
                              e.tensor_copy(
                                  kdup[s][dst0:dst0 + 64, kvi * L + t0:kvi * L + t0 + TILE],
                                  kraw[half:half + 64, :]),
                              reads=[kraw], writes=[kdup[s]])
                for u in range(4):
                    sub = it * 4 + u
                    for j in range(DCH):
                        V("tensor", lambda e, j=j, u=u: e.matmul(
                            pb[3][:, 0:256],
                            xnb_t[:, j * TILE + u * 128: j * TILE + (u + 1) * 128],
                            wv_sb[:, j * 256:(j + 1) * 256],
                            start=(j == 0), stop=(j == DCH - 1)),
                          reads=[xnb_t, wv_sb], writes=[pb[3]], tag="vmm")
                    for kv in range(KV):
                        V("vector", lambda e, kv=kv, sub=sub, s=s: e.tensor_copy(
                            v4[s][:, sub * VROW + kv * 72: sub * VROW + kv * 72 + 64],
                            pb[3][:, kv * 64:(kv + 1) * 64]),
                          reads=[pb[3]], writes=[v4[s]])

        P.barrier()
        actx.close()
        a_ctx = kv_ctx
    else:
        a_ctx = None

    if "AT" in phases:
        if a_ctx is None:
            raise RuntimeError("AT phase requires A phase in the same build")
        QW = 512
        NQW = L // QW
        atx = ExitStack()
        wq_sb = atx.enter_context(nc.sbuf_tensor("wq_sb", [128, DCH * 1024], BF16))
        wo_sb = atx.enter_context(nc.sbuf_tensor("wo_sb", [128, DCH * 1024], BF16))
        xnb_at = atx.enter_context(nc.sbuf_tensor("xnb_at", [128, DCH * QW], BF16))
        qc1 = atx.enter_context(nc.sbuf_tensor("qc1", [128, QW], F32))
        qc2 = atx.enter_context(nc.sbuf_tensor("qc2", [128, QW], F32))
        qbf = [atx.enter_context(nc.sbuf_tensor(f"qbf{i}", [128, QW], BF16)) for i in range(2)]
        pbf = [atx.enter_context(nc.sbuf_tensor(f"pbf{i}", [128, QW], BF16)) for i in range(4)]
        accf = atx.enter_context(nc.sbuf_tensor("accf", [128, QW], F32))
        accbf = atx.enter_context(nc.sbuf_tensor("accbf", [128, NPAIR * QW], BF16))
        rd_a = atx.enter_context(nc.sbuf_tensor("rd_a", [1, QW], F32))
        rd_b = atx.enter_context(nc.sbuf_tensor("rd_b", [1, QW], F32))
        att_st = [atx.enter_context(nc.sbuf_tensor(f"att_st{i}", [128, QW], F32))
                  for i in range(3)]

        for dc in range(DCH):
            P.dma("gpsimd", wq_sb[:, dc * 1024:(dc + 1) * 1024],
                  wq_in[dc * 128:(dc + 1) * 128, :], writes=[wq_sb], stream="wgt")
            P.dma("gpsimd", wo_sb[:, dc * 1024:(dc + 1) * 1024],
                  wo_in[dc * 128:(dc + 1) * 128, :], writes=[wo_sb], stream="wgt")

        for s in range(NSEQ):
            for qt in range(NQW):
                q0 = qt * QW
                for j in range(0, DCH, 4):
                    P.dma("sync", xnb_at[:, j * QW:(j + 4) * QW].rearrange(
                              "p (a t) -> p a t", a=4),
                          xnb_sc.rearrange("(a p) t -> p a t", p=128)[
                              :, j:j + 4, s * L + q0:s * L + q0 + QW],
                          reads=[xnb_sc], writes=[xnb_at], stream="xnr")

                def stage_q(pr, pi, q0=q0):
                    qb = qbf[pi % 2]
                    qps = pb[4 + 2 * (pi % 2)]
                    for j in range(DCH):
                        V("tensor", lambda e, j=j, pr=pr, qps=qps: e.matmul(
                            qps[:, :],
                            wq_sb[:, j * 1024 + pr * 128: j * 1024 + (pr + 1) * 128],
                            xnb_at[:, j * QW:(j + 1) * QW],
                            start=(j == 0), stop=(j == DCH - 1)),
                          reads=[wq_sb, xnb_at], writes=[qps], tag="qmm")
                    V("vector", lambda e, qps=qps, q0=q0: e.tensor_tensor(
                        qc1[:, :], qps[:, :], cos_d[:, q0:q0 + QW], ALU.mult),
                      reads=[qps, cos_d], writes=[qc1])
                    for p0, rp0 in ((0, 32), (32, 0), (64, 96), (96, 64)):
                        V("vector", lambda e, p0=p0, rp0=rp0, qps=qps, q0=q0: e.tensor_tensor(
                            qc2[p0:p0 + 32, :], qps[rp0:rp0 + 32, :],
                            sin_s[p0:p0 + 32, q0:q0 + QW], ALU.mult),
                          reads=[qps, sin_s], writes=[qc2])
                    V("vector", lambda e, qb=qb: e.tensor_tensor(qb[:, :], qc1[:, :],
                                                                 qc2[:, :], ALU.add),
                      reads=[qc1, qc2], writes=[qb])

                def emit_pv(job, s=s):
                    kc, kv, kc0, pa, pb_, acc_a, acc_b = job
                    for acc_i, psrc in ((acc_a, pa), (acc_b, pb_)):
                        V("tensor",
                          lambda e, kc=kc, kv=kv, psrc=psrc, acc_i=acc_i, kc0=kc0, s=s:
                          e.matmul(
                            pb[acc_i][0:65, :],
                            v4[s][:, kc * VROW + kv * 72: kc * VROW + kv * 72 + 65],
                            psrc[:, :],
                            start=(kc == kc0), stop=(kc == NKC - 1)),
                          reads=[v4[s], psrc], writes=[pb[acc_i]], tag="pv")

                pair_order = [3, 0, 4, 1, 5, 2, 6, 7]
                stage_q(pair_order[0], 0)
                pending_finish = None

                def mk_finish(pr, pi):
                    sa = 4 + 2 * (pi % 2)
                    sb = 5 + 2 * (pi % 2)
                    nrm = pb[5 + 2 * ((pi + 1) % 2)]

                    def fin():
                        first = True
                        for selp, rowt in ((psel_a, rd_a), (psel_b, rd_b)):
                            hi, lo = row_hilo(rowt[:, :], QW, [rowt])
                            V("tensor", lambda e, selp=selp, hi=hi, first=first, nrm=nrm:
                              e.matmul(nrm[:, :], selp[:, :], hi, start=first, stop=False),
                              reads=[selp, rhi], writes=[nrm])
                            V("tensor", lambda e, selp=selp, lo=lo, nrm=nrm: e.matmul(
                                nrm[:, :], selp[:, :], lo, start=False,
                                stop=(selp is psel_b)),
                              reads=[selp, rlo], writes=[nrm])
                            first = False
                        V("vector", lambda e, sa=sa: e.tensor_copy(accf[0:64, :],
                                                                   pb[sa][0:64, :]),
                          reads=[pb[sa]], writes=[accf])
                        V("vector", lambda e, sb=sb: e.tensor_copy(accf[64:128, :],
                                                                   pb[sb][0:64, :]),
                          reads=[pb[sb]], writes=[accf])
                        V("vector", lambda e, pr=pr, nrm=nrm: e.tensor_tensor(
                            accbf[:, pr * QW:(pr + 1) * QW], accf[:, :], nrm[:, :],
                            ALU.mult),
                          reads=[accf, nrm], writes=[accbf], tag="accn")
                    return fin

                for pi in range(NPAIR):
                    pr = pair_order[pi]
                    qb = qbf[pi % 2]
                    acc_a = 4 + 2 * (pi % 2)
                    acc_b = 5 + 2 * (pi % 2)
                    kv = (2 * pr) // GROUPS
                    slope_lo = float(2.0 ** (-(2 * pr + 2) / 2.0))
                    thr = (L - 128 - 160.0 / slope_lo) / 128.0
                    kc0 = max(0, min(NKC - 1, int(np.floor(thr)) + 1)) if thr > 0 else 0
                    pending = None
                    nq_emitted = False
                    for kc in range(kc0, NKC):
                        par = (kc - kc0) % 2
                        sA = pb[par * 2 + 0]
                        sB = pb[par * 2 + 1]
                        pa = pbf[par * 2 + 0]
                        pb_ = pbf[par * 2 + 1]
                        for ti, (r0, c0) in enumerate(((0, 0), (64, 0), (0, 64), (64, 64))):
                            dst = (sA if r0 == 0 else sB)
                            V("tensor",
                              lambda e, r0=r0, c0=c0, kc=kc, kv=kv, qb=qb, dst=dst, s=s:
                              e.matmul(
                                  dst[c0:c0 + 64, :],
                                  kdup[s][r0:r0 + 64,
                                          kv * L + kc * 128 + c0: kv * L + kc * 128 + c0 + 64],
                                  qb[r0:r0 + 64, :],
                                  start=True, stop=True, tile_position=(r0, c0)),
                              reads=[kdup[s], qb], writes=[dst], tag="smm")
                        if pending_finish is not None:
                            pending_finish()
                            pending_finish = None
                        if pending is not None:
                            emit_pv(pending)
                        V("scalar", lambda e, pa=pa, kc=kc, pr=pr, sA=sA: e.activation(
                            pa[:, :], sA[:, :], AF.Exp,
                            bias=bias_hk[:, (2 * pr) * NKC + kc:(2 * pr) * NKC + kc + 1],
                            scale=0.125),
                          reads=[sA, bias_hk], writes=[pa], tag="exp")
                        V("scalar", lambda e, pb_=pb_, kc=kc, pr=pr, sB=sB: e.activation(
                            pb_[:, :], sB[:, :], AF.Exp,
                            bias=bias_hk[:, (2 * pr + 1) * NKC + kc:(2 * pr + 1) * NKC + kc + 1],
                            scale=0.125),
                          reads=[sB, bias_hk], writes=[pb_], tag="exp")
                        pending = (kc, kv, kc0, pa, pb_, acc_a, acc_b)
                        if not nq_emitted and kc >= kc0 + 1 and pi + 1 < NPAIR:
                            stage_q(pair_order[pi + 1], pi + 1)
                            nq_emitted = True
                    if not nq_emitted and pi + 1 < NPAIR:
                        stage_q(pair_order[pi + 1], pi + 1)
                    emit_pv(pending)
                    V("scalar", lambda e, acc_a=acc_a: e.activation(
                        rd_a[:, :], pb[acc_a][64:65, :], AF.Ln),
                      reads=[pb[acc_a]], writes=[rd_a])
                    V("scalar", lambda e: e.activation(rd_a[:, :], rd_a[:, :], AF.Exp,
                                                       scale=-1.0),
                      reads=[rd_a], writes=[rd_a])
                    V("scalar", lambda e, acc_b=acc_b: e.activation(
                        rd_b[:, :], pb[acc_b][64:65, :], AF.Ln),
                      reads=[pb[acc_b]], writes=[rd_b])
                    V("scalar", lambda e: e.activation(rd_b[:, :], rd_b[:, :], AF.Exp,
                                                       scale=-1.0),
                      reads=[rd_b], writes=[rd_b])
                    pending_finish = mk_finish(pr, pi)
                if pending_finish is not None:
                    pending_finish()
                    pending_finish = None
                for dc in range(DCH):
                    dst = pb[4 + dc % 2]
                    stg = att_st[dc % 3]
                    for j in range(DCH):
                        V("tensor", lambda e, dc=dc, j=j, dst=dst: e.matmul(
                            dst[:, :],
                            wo_sb[:, j * 1024 + dc * 128: j * 1024 + (dc + 1) * 128],
                            accbf[:, j * QW:(j + 1) * QW],
                            start=(j == 0), stop=(j == DCH - 1)),
                          reads=[wo_sb, accbf], writes=[dst], tag="womm")
                    V("vector", lambda e, dst=dst, stg=stg: e.tensor_copy(stg[:, :], dst[:, :]),
                      reads=[dst], writes=[stg])
                    P.dma("sync", attn_sc[dc * 128:(dc + 1) * 128, s * L + q0:s * L + q0 + QW],
                          stg[:, :], reads=[stg], writes=[attn_sc], stream="atw")
        P.barrier()
        atx.close()
        if a_ctx is not None:
            a_ctx.close()
        ropes_ctx.close()
    elif a_ctx is not None:
        P.barrier()
        a_ctx.close()
        ropes_ctx.close()
    else:
        P.barrier()
        ropes_ctx.close()

    if "F1" in phases:
        fctx = ExitStack()
        wc_sb = [fctx.enter_context(nc.sbuf_tensor(f"wc_sb{dc}", [128, CF], BF16))
                 for dc in range(DCH)]
        gw1_sb = fctx.enter_context(nc.sbuf_tensor("gw1_sb", [128, DCH * 512], BF16))
        gw2_sb = fctx.enter_context(nc.sbuf_tensor("gw2_sb", [128, 4 * 16], BF16))
        xnb_f = [fctx.enter_context(nc.sbuf_tensor(f"xnb_f{i}", [128, DCH * TILE], BF16))
                 for i in range(2)]
        h_bf = fctx.enter_context(nc.sbuf_tensor("h_bf", [128, 4 * TILE], BF16))
        g2s = fctx.enter_context(nc.sbuf_tensor("g2s", [C, TILE], F32))
        invnb = fctx.enter_context(nc.sbuf_tensor("invnb", [C, TILE], F32))
        logit_f = fctx.enter_context(nc.sbuf_tensor("logit_f", [C, TILE], F32))
        logit_t = fctx.enter_context(nc.sbuf_tensor("logit_t", [128, C], F32))
        negm = fctx.enter_context(nc.sbuf_tensor("negm", [128, 1], F32))
        maskt = fctx.enter_context(nc.sbuf_tensor("maskt", [128, C], F32))
        eet = fctx.enter_context(nc.sbuf_tensor("eet", [128, C], F32))
        ssum = fctx.enter_context(nc.sbuf_tensor("ssum", [128, 1], F32))
        rsum = fctx.enter_context(nc.sbuf_tensor("rsum", [128, 1], F32))
        w_tm = fctx.enter_context(nc.sbuf_tensor("w_tm", [128, C], F32))
        w_hi = [fctx.enter_context(nc.sbuf_tensor(f"w_hi{i}", [C, TILE], BF16))
                for i in range(2)]
        w_lo = [fctx.enter_context(nc.sbuf_tensor(f"w_lo{i}", [C, TILE], BF16))
                for i in range(2)]
        w_f32 = fctx.enter_context(nc.sbuf_tensor("w_f32", [C, TILE], F32))
        w_tmf = fctx.enter_context(nc.sbuf_tensor("w_tmf", [C, TILE], F32))
        colsg_ring = [fctx.enter_context(nc.sbuf_tensor(f"colsg{i}", [128, TILE], BF16))
                      for i in range(4)]
        wbs = [fctx.enter_context(nc.sbuf_tensor(f"wbs{i}", [128, TILE], F32))
               for i in range(2)]
        irow = fctx.enter_context(nc.sbuf_tensor("irow", [1, TILE], F32))

        for dc in range(DCH):
            for c in range(C):
                P.dma("gpsimd", wc_sb[dc][:, c * NN:(c + 1) * NN],
                      wcols_in[c, dc * 128:(dc + 1) * 128, :], writes=[wc_sb[dc]], stream="wgt")
            P.dma("gpsimd", gw1_sb[:, dc * 512:(dc + 1) * 512],
                  gw1_in[dc * 128:(dc + 1) * 128, :], writes=[gw1_sb], stream="wgt")
        for hf in range(4):
            P.dma("gpsimd", gw2_sb[:, hf * 16:(hf + 1) * 16],
                  gw2_in[hf * 128:(hf + 1) * 128, :], writes=[gw2_sb], stream="wgt")

        def f1_gate(s, it):
            t0 = it * TILE
            bi = it % 2
            xb = xnb_f[bi]
            P.dma("sync", stage_rows[:, 3 * TILE:4 * TILE],
                  rows_sc[3:4, s * L + t0:s * L + t0 + TILE],
                  reads=[rows_sc], writes=[stage_rows], stream="xnr")
            for j in range(0, DCH, 4):
                P.dma("sync", xb[:, j * TILE:(j + 4) * TILE].rearrange(
                          "p (a t) -> p a t", a=4),
                      xnb_sc.rearrange("(a p) t -> p a t", p=128)[
                          :, j:j + 4, s * L + t0:s * L + t0 + TILE],
                      reads=[xnb_sc], writes=[xb], stream="xnr")
            for hf in range(4):
                hpb = pb[4 + hf % 2]
                for j in range(DCH):
                    V("tensor", lambda e, hf=hf, j=j, hpb=hpb, xb=xb: e.matmul(
                        hpb[:, :], gw1_sb[:, j * 512 + hf * 128: j * 512 + (hf + 1) * 128],
                        xb[:, j * TILE:(j + 1) * TILE],
                        start=(j == 0), stop=(j == DCH - 1)),
                      reads=[gw1_sb, xb], writes=[hpb], tag="gmm")
                V("scalar", lambda e, hf=hf, hpb=hpb: e.activation(
                    h_bf[:, hf * TILE:(hf + 1) * TILE], hpb[:, :], AF.Gelu,
                    bias=b1_fm[:, hf:hf + 1]),
                  reads=[hpb, b1_fm], writes=[h_bf], tag="gelu1")
            for hf in range(4):
                V("tensor", lambda e, hf=hf: e.matmul(
                    pb[6][0:C, :], gw2_sb[:, hf * 16:(hf + 1) * 16],
                    h_bf[:, hf * TILE:(hf + 1) * TILE],
                    start=(hf == 0), stop=(hf == 3)),
                  reads=[gw2_sb, h_bf], writes=[pb[6]], tag="g2mm")
            V("scalar", lambda e: e.activation(g2s[:, :], pb[6][0:C, :], AF.Tanh,
                                               scale=0.5, bias=gb2h_t[:, :]),
              reads=[pb[6], gb2h_t], writes=[g2s])
            V("vector", lambda e: e.tensor_scalar(g2s[:, :], g2s[:, :], 0.5, 0.5,
                                                  op0=ALU.mult, op1=ALU.add),
              reads=[g2s], writes=[g2s])
            for j in range(DCH):
                V("tensor", lambda e, j=j, xb=xb: e.matmul(
                    pb[7][0:C, :], ce_fm[:, j * C:(j + 1) * C],
                    xb[:, j * TILE:(j + 1) * TILE],
                    start=(j == 0), stop=(j == DCH - 1)),
                  reads=[ce_fm, xb], writes=[pb[7]], tag="simm")
            hi, lo = row_hilo(stage_rows[:, 3 * TILE:4 * TILE], TILE, [stage_rows])
            V("tensor", lambda e, hi=hi: e.matmul(pb[4][0:C, :], ones_row_bf[:, 0:C], hi,
                                                  start=True, stop=False),
              reads=[ones_row_bf, rhi], writes=[pb[4]])
            V("tensor", lambda e, lo=lo: e.matmul(pb[4][0:C, :], ones_row_bf[:, 0:C], lo,
                                                  start=False, stop=True),
              reads=[ones_row_bf, rlo], writes=[pb[4]])
            V("vector", lambda e: e.tensor_copy(invnb[:, :], pb[4][0:C, :]),
              reads=[pb[4]], writes=[invnb])
            V("vector", lambda e: e.tensor_tensor(logit_f[:, :], pb[7][0:C, :],
                                                  invnb[:, :], ALU.mult),
              reads=[pb[7], invnb], writes=[logit_f])
            V("vector", lambda e: e.tensor_tensor(logit_f[:, :], logit_f[:, :], g2s[:, :],
                                                  ALU.add),
              reads=[logit_f, g2s], writes=[logit_f])
            for u in range(4):
                V("tensor", lambda e, u=u: e.transpose(
                    pb[5][:, 0:C], logit_f[:, u * 128:(u + 1) * 128], ident[0:C, 0:C]),
                  reads=[logit_f, ident], writes=[pb[5]], tag="lgT")
                V("vector", lambda e: e.tensor_copy(logit_t[:, :], pb[5][:, 0:C]),
                  reads=[pb[5]], writes=[logit_t])
                V("vector", lambda e: e.reduce_max(negm[:, :], logit_t[:, :], axis=AX.X,
                                                   negate=True),
                  reads=[logit_t], writes=[negm])
                V("vector", lambda e: e.tensor_scalar(maskt[:, :], logit_t[:, :],
                                                      negm[:, :], None, op0=ALU.add),
                  reads=[logit_t, negm], writes=[maskt])
                V("vector", lambda e: e.tensor_single_scalar(maskt[:, :], maskt[:, :], 0.0,
                                                             op=ALU.is_ge),
                  reads=[maskt], writes=[maskt])
                V("scalar", lambda e: e.activation(eet[:, :], logit_t[:, :], AF.Exp,
                                                   bias=negm[:, :]),
                  reads=[logit_t, negm], writes=[eet])
                V("vector", lambda e: e.reduce_sum(ssum[:, :], eet[:, :], axis=AX.X),
                  reads=[eet], writes=[ssum])
                V("vector", lambda e: e.reciprocal(rsum[:, :], ssum[:, :]),
                  reads=[ssum], writes=[rsum])
                V("vector", lambda e: e.scalar_tensor_tensor(w_tm[:, :], eet[:, :],
                                                             rsum[:, :], maskt[:, :],
                                                             op0=ALU.mult, op1=ALU.mult),
                  reads=[eet, rsum, maskt], writes=[w_tm])
                V("tensor", lambda e, u=u: e.transpose(pb[5][0:C, 128:256],
                                                       w_tm[:, :], ident[:, :]),
                  reads=[w_tm, ident], writes=[pb[5]], tag="wT")
                V("vector", lambda e, u=u: e.tensor_copy(w_tmf[:, u * 128:(u + 1) * 128],
                                                         pb[5][0:C, 128:256]),
                  reads=[pb[5]], writes=[w_tmf])
            V("vector", lambda e, bi=bi: e.tensor_copy(w_hi[bi][:, :], w_tmf[:, :]),
              reads=[w_tmf], writes=[w_hi[bi]])
            V("vector", lambda e, bi=bi: e.tensor_copy(w_f32[:, :], w_hi[bi][:, :]),
              reads=[w_hi[bi]], writes=[w_f32])
            V("vector", lambda e: e.tensor_tensor(w_f32[:, :], w_tmf[:, :], w_f32[:, :],
                                                  ALU.subtract),
              reads=[w_tmf, w_f32], writes=[w_f32])
            V("vector", lambda e, bi=bi: e.tensor_copy(w_lo[bi][:, :], w_f32[:, :]),
              reads=[w_f32], writes=[w_lo[bi]])

        def f1_cols(s, it):
            t0 = it * TILE
            bi = it % 2
            xb = xnb_f[bi]
            for c in range(C):
                wpb = pb[2 + c % 2]
                wb = wbs[c % 2]
                V("tensor", lambda e, c=c, wpb=wpb, bi=bi: e.matmul(
                    wpb[:, :], selc[:, c * 128:(c + 1) * 128], w_hi[bi][:, :],
                    start=True, stop=False),
                  reads=[selc, w_hi[bi]], writes=[wpb])
                V("tensor", lambda e, c=c, wpb=wpb, bi=bi: e.matmul(
                    wpb[:, :], selc[:, c * 128:(c + 1) * 128], w_lo[bi][:, :],
                    start=False, stop=True),
                  reads=[selc, w_lo[bi]], writes=[wpb])
                V("vector", lambda e, wpb=wpb, wb=wb: e.tensor_copy(wb[:, :], wpb[:, :]),
                  reads=[wpb], writes=[wb], tag="wbev")
                for jc in (2 * c, 2 * c + 1):
                    cpb = pb[jc % 2]
                    for j in range(DCH):
                        V("tensor", lambda e, jc=jc, j=j, cpb=cpb, xb=xb: e.matmul(
                            cpb[:, :], wc_sb[j][:, jc * 128:(jc + 1) * 128],
                            xb[:, j * TILE:(j + 1) * TILE],
                            start=(j == 0), stop=(j == DCH - 1)),
                          reads=[wc_sb[j], xb], writes=[cpb], tag="cmm")
                    cg = colsg_ring[jc % 4]
                    V("scalar", lambda e, jc=jc, cpb=cpb, cg=cg: e.activation(
                        cg[:, :], cpb[:, :], AF.Gelu, bias=bcols_fm[:, jc:jc + 1]),
                      reads=[cpb, bcols_fm], writes=[cg], tag="gelu2")
                    V("vector", lambda e, cg=cg, wb=wb: e.tensor_tensor(
                        cg[:, :], cg[:, :], wb[:, :], ALU.mult),
                      reads=[cg, wb], writes=[cg], tag="wmul")
                    P.dma("scalar",
                          cols_sc[jc * 128:(jc + 1) * 128, s * L + t0:s * L + t0 + TILE],
                          cg[:, :], reads=[cg], writes=[cols_sc], stream="clw")

        tiles = [(s, it) for s in range(NSEQ) for it in range(NTILE)]
        f1_gate(*tiles[0])
        for ti, (s, it) in enumerate(tiles):
            if ti + 1 < len(tiles):
                f1_gate(*tiles[ti + 1])
            f1_cols(s, it)
        P.barrier()
        fctx.close()

    if "F2" in phases:
        f2ctx = ExitStack()
        pw_sb = f2ctx.enter_context(nc.sbuf_tensor("pw_sb", [128, 32 * 1024], BF16))
        cols_f = f2ctx.enter_context(nc.sbuf_tensor("cols_f", [128, 32 * TILE], BF16))
        xn32_f = f2ctx.enter_context(nc.sbuf_tensor("xn32_f", [128, DCH * TILE], F32))
        att_f = f2ctx.enter_context(nc.sbuf_tensor("att_f", [128, 4 * TILE], F32))
        y_sb = [f2ctx.enter_context(nc.sbuf_tensor(f"y_sb{i}", [128, DCH * TILE], F32))
                for i in range(2)]
        ybf = [f2ctx.enter_context(nc.sbuf_tensor(f"ybf{i}", [128, TILE], BF16))
               for i in range(2)]
        y2bf = [f2ctx.enter_context(nc.sbuf_tensor(f"y2bf{i}", [128, TILE], BF16))
                for i in range(2)]
        zt = [f2ctx.enter_context(nc.sbuf_tensor(f"zt{i}", [128, TILE], F32))
              for i in range(2)]
        e2t = [f2ctx.enter_context(nc.sbuf_tensor(f"e2t{i}", [128, TILE], F32))
               for i in range(2)]
        xpt = zt
        out_st = [f2ctx.enter_context(nc.sbuf_tensor(f"out_st{i}", [128, TILE], F32))
                  for i in range(2)]
        bh = [f2ctx.enter_context(nc.sbuf_tensor(f"bh{i}", [1, TILE], BF16)) for i in range(4)]
        bl = [f2ctx.enter_context(nc.sbuf_tensor(f"bl{i}", [1, TILE], BF16)) for i in range(4)]
        xnc = [f2ctx.enter_context(nc.sbuf_tensor(f"xnc{i}", [128, TILE], F32))
               for i in range(2)]
        mu2r = stage_rows[0:1, 1 * TILE:2 * TILE]
        r2r = stage_rows[0:1, 3 * TILE:4 * TILE]

        for j in range(32):
            P.dma("gpsimd", pw_sb[:, j * 1024:(j + 1) * 1024],
                  pw_in[j * 128:(j + 1) * 128, :], writes=[pw_sb], stream="wgt")

        def f2_loads(s, it):
            t0 = it * TILE
            for j in range(0, 32, 4):
                P.dma("sync", cols_f[:, j * TILE:(j + 4) * TILE].rearrange(
                          "p (a t) -> p a t", a=4),
                      cols_sc.rearrange("(a p) t -> p a t", p=128)[
                          :, j:j + 4, s * L + t0:s * L + t0 + TILE],
                      reads=[cols_sc], writes=[cols_f], stream="clr")
            for j in range(0, DCH, 4):
                P.dma("sync", xn32_f[:, j * TILE:(j + 4) * TILE].rearrange(
                          "p (a t) -> p a t", a=4),
                      xn32_sc.rearrange("(a p) t -> p a t", p=128)[
                          :, j:j + 4, s * L + t0:s * L + t0 + TILE],
                      reads=[xn32_sc], writes=[xn32_f], stream="xnr")

        def f2_projdc(s, it, dc):
            ppb = pb[dc % 2]
            for j in range(32):
                V("tensor", lambda e, dc=dc, j=j, ppb=ppb: e.matmul(
                    ppb[:, :], pw_sb[:, j * 1024 + dc * 128: j * 1024 + (dc + 1) * 128],
                    cols_f[:, j * TILE:(j + 1) * TILE],
                    start=(j == 0), stop=(j == 31)),
                  reads=[pw_sb, cols_f], writes=[ppb], tag="pmm")

        def f2_yev(s, it, dc):
            yb = y_sb[it % 2]
            ppb = pb[dc % 2]
            V("vector", lambda e, dc=dc, ppb=ppb, yb=yb: e.scalar_tensor_tensor(
                yb[:, dc * TILE:(dc + 1) * TILE], ppb[:, :], pb_fm[:, dc:dc + 1],
                xn32_f[:, dc * TILE:(dc + 1) * TILE], op0=ALU.add, op1=ALU.add),
              reads=[ppb, pb_fm, xn32_f], writes=[yb], tag="yev")

        def f2_tail1(s, it):
            t0 = it * TILE
            yb = y_sb[it % 2]
            for ri in (0, 2):
                P.dma("sync", stage_rows[:, ri * TILE:(ri + 1) * TILE],
                      rows_sc[ri:ri + 1, s * L + t0:s * L + t0 + TILE],
                      reads=[rows_sc], writes=[stage_rows], stream="xnr")
            for dc in range(2):
                P.dma("sync", att_f[:, dc * TILE:(dc + 1) * TILE],
                      attn_sc[dc * 128:(dc + 1) * 128, s * L + t0:s * L + t0 + TILE],
                      reads=[attn_sc], writes=[att_f], stream="atr")
            P.dma("sync", xnc[0][:, :], xn32_sc[0:128, s * L + t0:s * L + t0 + TILE],
                  reads=[xn32_sc], writes=[xnc[0]], stream="xnc")
            for dc in range(DCH):
                ybc = ybf[dc % 2]
                y2 = y2bf[dc % 2]
                V("vector", lambda e, dc=dc, ybc=ybc, yb=yb: e.tensor_copy(
                    ybc[:, :], yb[:, dc * TILE:(dc + 1) * TILE]),
                  reads=[yb], writes=[ybc])
                V("vector", lambda e, ybc=ybc, y2=y2: e.tensor_tensor(
                    y2[:, :], ybc[:, :], ybc[:, :], ALU.mult),
                  reads=[ybc], writes=[y2])
                V("tensor", lambda e, dc=dc, ybc=ybc: e.matmul(
                    pb[2][0:1, :], onescol_bf[:, :], ybc[:, :],
                    start=(dc == 0), stop=(dc == DCH - 1)),
                  reads=[onescol_bf, ybc], writes=[pb[2]], tag="sy")
                V("tensor", lambda e, dc=dc, y2=y2: e.matmul(
                    pb[3][0:1, :], onescol_bf[:, :], y2[:, :],
                    start=(dc == 0), stop=(dc == DCH - 1)),
                  reads=[onescol_bf, y2], writes=[pb[3]], tag="sy2")
            V("vector", lambda e: e.tensor_scalar_mul(mu2r[:, :], pb[2][0:1, :],
                                                      float(1.0 / D)),
              reads=[pb[2]], writes=[stage_rows])
            V("vector", lambda e: e.tensor_tensor(r2r[:, :], mu2r[:, :], mu2r[:, :],
                                                  ALU.mult),
              reads=[stage_rows], writes=[stage_rows])
            V("vector", lambda e: e.scalar_tensor_tensor(r2r[:, :], pb[3][0:1, :],
                                                         float(1.0 / D), r2r[:, :],
                                                         op0=ALU.mult, op1=ALU.subtract),
              reads=[pb[3], stage_rows], writes=[stage_rows])
            V("vector", lambda e: e.tensor_single_scalar(r2r[:, :], r2r[:, :],
                                                         float(LN_EPS), op=ALU.add),
              reads=[stage_rows], writes=[stage_rows])
            V("scalar", lambda e: e.activation(r2r[:, :], r2r[:, :], AF.Ln),
              reads=[stage_rows], writes=[stage_rows], tag="r2ln")
            V("scalar", lambda e: e.activation(r2r[:, :], r2r[:, :], AF.Exp, scale=-0.5),
              reads=[stage_rows], writes=[stage_rows], tag="r2exp")
            for z, (bi, rowt) in enumerate(((4, mu2r[:, :]), (5, r2r[:, :]),
                                            (6, stage_rows[:, 2 * TILE:3 * TILE]),
                                            (7, stage_rows[:, 0 * TILE:1 * TILE]))):
                V("vector", lambda e, z=z, rowt=rowt: e.tensor_copy(bh[z][:, :], rowt),
                  reads=[stage_rows], writes=[bh[z]])
                V("vector", lambda e, z=z: e.tensor_copy(rtmp[:, :], bh[z][:, :]),
                  reads=[bh[z]], writes=[rtmp])
                V("vector", lambda e, z=z, rowt=rowt: e.tensor_tensor(
                    bl[z][:, :], rowt, rtmp[:, :], ALU.subtract),
                  reads=[stage_rows, rtmp], writes=[bl[z]])
            for z, bi in enumerate((4, 5, 6, 7)):
                V("tensor", lambda e, bi=bi, z=z: e.matmul(
                    pb[bi][:, :], ones_row_bf[:, :], bh[z][:, :], start=True, stop=False),
                  reads=[ones_row_bf, bh[z]], writes=[pb[bi]])
                V("tensor", lambda e, bi=bi, z=z: e.matmul(
                    pb[bi][:, :], ones_row_bf[:, :], bl[z][:, :], start=False, stop=True),
                  reads=[ones_row_bf, bl[z]], writes=[pb[bi]])

        def f2_combine_dc(s, it, dc):
            t0 = it * TILE
            yb = y_sb[it % 2]
            i = dc % 2
            av = att_f[:, (dc % 2) * TILE:((dc % 2) + 1) * TILE]
            V("vector", lambda e, dc=dc, i=i, yb=yb: e.tensor_tensor(
                zt[i][:, :], yb[:, dc * TILE:(dc + 1) * TILE], pb[4][:, :],
                ALU.subtract),
              reads=[yb, pb[4]], writes=[zt[i]], tag="fc")
            V("vector", lambda e, dc=dc, i=i: e.scalar_tensor_tensor(
                e2t[i][:, :], zt[i][:, :], lng_fm[:, dc:dc + 1], pb[5][:, :],
                op0=ALU.mult, op1=ALU.mult),
              reads=[zt[i], lng_fm, pb[5]], writes=[e2t[i]], tag="fc")
            V("vector", lambda e, dc=dc, i=i: e.tensor_tensor(
                xpt[i][:, :], xnc[i][:, :], pb[6][:, :], ALU.mult),
              reads=[xnc[i], pb[6]], writes=[xpt[i]], tag="fc")
            if dc + 1 < DCH:
                P.dma("sync", xnc[(dc + 1) % 2][:, :],
                      xn32_sc[(dc + 1) * 128:(dc + 2) * 128, s * L + t0:s * L + t0 + TILE],
                      reads=[xn32_sc], writes=[xnc[(dc + 1) % 2]], stream="xnc")
            V("gpsimd", lambda e, av=av, i=i: e.tensor_tensor(
                xpt[i][:, :], xpt[i][:, :], av, ALU.add),
              reads=[xpt[i], att_f], writes=[xpt[i]], tag="fc")
            if dc + 2 < DCH:
                P.dma("sync", att_f[:, (dc % 2) * TILE:((dc % 2) + 1) * TILE],
                      attn_sc[(dc + 2) * 128:(dc + 3) * 128, s * L + t0:s * L + t0 + TILE],
                      reads=[attn_sc], writes=[att_f], stream="atr")
            V("gpsimd", lambda e, i=i: e.tensor_tensor(
                e2t[i][:, :], e2t[i][:, :], xpt[i][:, :], ALU.add),
              reads=[e2t[i], xpt[i]], writes=[e2t[i]], tag="fc")
            V("vector", lambda e, dc=dc, i=i: e.scalar_tensor_tensor(
                out_st[i][:, :], e2t[i][:, :], lnb_fm[:, dc:dc + 1], pb[7][:, :],
                op0=ALU.add, op1=ALU.add),
              reads=[e2t[i], lnb_fm, pb[7]], writes=[out_st[i]], tag="fc")
            P.dma("sync", y_out[dc * 128:(dc + 1) * 128, s * L + t0:s * L + t0 + TILE],
                  out_st[i][:, :], reads=[out_st[i]], writes=[y_out], stream="out")

        tiles2 = [(s, it) for s in range(NSEQ) for it in range(NTILE)]
        f2_loads(*tiles2[0])
        for dc in range(DCH):
            f2_projdc(*tiles2[0], dc)
            f2_yev(*tiles2[0], dc)
        for ti, (s, it) in enumerate(tiles2):
            f2_tail1(s, it)
            nxt = tiles2[ti + 1] if ti + 1 < len(tiles2) else None
            if nxt is not None:
                f2_loads(*nxt)
            for dc in range(DCH):
                if nxt is not None:
                    f2_projdc(*nxt, dc)
                f2_combine_dc(s, it, dc)
                if nxt is not None:
                    f2_yev(*nxt, dc)
        P.barrier()
        f2ctx.close()


    P.wait_all("sync")
    P.emit()
    ctx.close()
    return nc


_PROGRAM_CACHE = {}


def _get_program():
    key = (tuple(_PHASES), tuple(_DEBUG_TAPS))
    if key not in _PROGRAM_CACHE:
        _PROGRAM_CACHE[key] = build_program()
    return _PROGRAM_CACHE[key]


def kernel(L0_attn_x, L0_attn_Wq, L0_attn_Wk, L0_attn_Wv, L0_attn_Wo,
           L0_ffn_col_emb, L0_ffn_gate_w1, L0_ffn_gate_b1, L0_ffn_gate_w2,
           L0_ffn_gate_b2, L0_ffn_W_cols, L0_ffn_b_cols, L0_ffn_proj_w,
           L0_ffn_proj_b, L0_ffn_ln_g, L0_ffn_ln_b):
    nc = _get_program()
    x = np.ascontiguousarray(np.asarray(L0_attn_x, dtype=np.float32))
    shared = {
        "Wq": np.ascontiguousarray(L0_attn_Wq, dtype=np.float32),
        "Wk": np.ascontiguousarray(L0_attn_Wk, dtype=np.float32),
        "Wv": np.ascontiguousarray(L0_attn_Wv, dtype=np.float32),
        "Wo": np.ascontiguousarray(L0_attn_Wo, dtype=np.float32),
        "col_emb": np.ascontiguousarray(L0_ffn_col_emb, dtype=np.float32),
        "gate_w1": np.ascontiguousarray(L0_ffn_gate_w1, dtype=np.float32),
        "gate_b1": np.ascontiguousarray(L0_ffn_gate_b1, dtype=np.float32),
        "gate_w2": np.ascontiguousarray(L0_ffn_gate_w2, dtype=np.float32),
        "gate_b2": np.ascontiguousarray(L0_ffn_gate_b2, dtype=np.float32),
        "W_cols": np.ascontiguousarray(L0_ffn_W_cols, dtype=np.float32),
        "b_cols": np.ascontiguousarray(L0_ffn_b_cols, dtype=np.float32),
        "proj_w": np.ascontiguousarray(L0_ffn_proj_w, dtype=np.float32),
        "proj_b": np.ascontiguousarray(L0_ffn_proj_b, dtype=np.float32),
        "ln_g": np.ascontiguousarray(L0_ffn_ln_g, dtype=np.float32),
        "ln_b": np.ascontiguousarray(L0_ffn_ln_b, dtype=np.float32),
    }
    in_maps = []
    for i in range(NCORES):
        m = dict(shared)
        m["x"] = x[i * NSEQ:(i + 1) * NSEQ].reshape(NSEQ * L, D)
        in_maps.append(m)
    res = run_bass_kernel_spmd(nc, in_maps, core_ids=list(range(NCORES)))
    out = np.empty((B, L, D), dtype=np.float32)
    for i in range(NCORES):
        yi = res.results[i]["y"]
        out[i * NSEQ:(i + 1) * NSEQ] = yi.reshape(D, NSEQ, L).transpose(1, 2, 0)
    return out
